# version 02
# baseline (speedup 1.0000x reference)
import numpy as np
import concourse.bass as bass
import concourse.mybir as mybir
from concourse.bass_utils import run_bass_kernel_spmd

F32 = mybir.dt.float32
BF16 = mybir.dt.bfloat16
I32 = mybir.dt.int32
U8 = mybir.dt.uint8
AF = mybir.ActivationFunctionType
ALU = mybir.AluOpType
AX = mybir.AxisListType

C = 1024
H = 16
N = 64
NCORES = 8
B_CORE = 2
L = 128
INV_C = 1.0 / C


class P:
    def __init__(self, nc, semnames):
        self.nc = nc
        self.c = {}
        self.ctxs = []
        for nm in semnames:
            g = nc.semaphore(nm)
            self.c[nm] = [g.__enter__(), 0]
            self.ctxs.append(g)

    def inc(self, inst, nm, n=1):
        self.c[nm][1] += n
        inst.then_inc(self.c[nm][0], n)
        return self.c[nm][1]

    def val(self, nm):
        return self.c[nm][1]

    def wait(self, eng, nm, v=None):
        eng.wait_ge(self.c[nm][0], self.c[nm][1] if v is None else v)


def build_l1(T, debug=()):
    NT = T // L
    TILES = B_CORE * NT
    nc = bass.Bass("TRN2", target_bir_lowering=False, detect_race_conditions=False)

    x_d = nc.dram_tensor("x", [TILES * L, C], F32, kind="ExternalInput").ap()
    Wd = {w: nc.dram_tensor(w, [C, C], F32, kind="ExternalInput").ap()
          for w in ("Wr", "Wk", "Wv", "Wo")}
    w1_d = nc.dram_tensor("w1", [C, 64], F32, kind="ExternalInput").ap()
    a1_d = nc.dram_tensor("a1", [C, 64], F32, kind="ExternalInput").ap()
    g1_d = nc.dram_tensor("g1", [C, 128], F32, kind="ExternalInput").ap()
    w2_d = nc.dram_tensor("w2", [64, C], F32, kind="ExternalInput").ap()
    a2_d = nc.dram_tensor("a2", [64, C], F32, kind="ExternalInput").ap()
    g2_d = nc.dram_tensor("g2", [128, C], F32, kind="ExternalInput").ap()
    vecs_d = nc.dram_tensor("vecs", [13, C], F32, kind="ExternalInput").ap()
    identf_d = nc.dram_tensor("identf", [128, 128], F32, kind="ExternalInput").ap()
    masks_d = nc.dram_tensor("masks", [128, 384], U8, kind="ExternalInput").ap()

    x1_d = nc.dram_tensor("x1o", [TILES * L, C], F32, kind="ExternalOutput").ap()
    xm_d = nc.dram_tensor("xmo", [TILES * L, C], BF16, kind="ExternalOutput").ap()
    imp_d = nc.dram_tensor("impo", [128, TILES], F32, kind="ExternalOutput").ap()
    n1_d = nc.dram_tensor("n1o", [128, TILES], F32, kind="ExternalOutput").ap()
    dbg_d = {nm: nc.dram_tensor("dbg_" + nm, [TILES * L, C], F32, kind="ExternalOutput").ap()
             for nm in debug}

    ctx = []

    def sb(name, shape, dtype=F32):
        g = nc.sbuf_tensor(name, shape, dtype)
        t = g.__enter__()
        ctx.append(g)
        return t

    Ws = {w: sb(w + "_s", [128, 8, C], BF16) for w in ("Wr", "Wk", "Wv", "Wo")}
    lora1 = sb("lora1", [128, 16, 256], BF16)
    w2s = sb("w2s", [64, C], BF16)
    a2s = sb("a2s", [64, C], BF16)
    g2s = sb("g2s", [128, C], BF16)
    lam = sb("lam", [128, 8, 8], F32)
    identf = sb("identf_s", [128, 128], F32)
    masks = sb("masks_s", [128, 384], U8)
    zer_f = sb("zer_f", [128, 128], F32)
    onesf = sb("onesf", [128, 128], F32)
    cpool = sb("cpool", [128, 8], F32)
    kkb = sb("kkb", [128, C], BF16)
    kab = sb("kab", [128, C], BF16)
    omkab = sb("omkab", [128, C], BF16)
    rkb = sb("rkb", [128, C], BF16)
    gab = sb("gab", [128, C], BF16)
    beb = sb("beb", [128, C], BF16)
    w0b = sb("w0b", [128, C], BF16)
    a0b = sb("a0b", [128, C], BF16)
    ictx = []

    def sbi(name, shape, dtype=F32):
        g = nc.sbuf_tensor(name, shape, dtype)
        t = g.__enter__()
        ictx.append(g)
        return t
    l1f = sbi("l1f", [128, 8, 256], F32)
    vrow = {nm: sbi("vr_" + nm, [1, C], F32)
            for nm in ("k_k", "k_a", "rk", "gam", "bet", "w0", "a0")}

    pb = []
    for i in range(8):
        g = nc.psum_tensor(f"pb{i}", [128, 512], F32)
        pb.append(g.__enter__())
        ctx.append(g)
    MSL = masks[:, 0:128]
    MSU = masks[:, 128:256]
    MIU = masks[:, 256:384]

    blk = nc.Block()
    block = blk.__enter__()
    p = P(nc, ["dx0", "dx1", "dw", "dg", "do0", "do1", "dm", "dbgs", "sv", "ss", "tt"])
    dxn = ["dx0", "dx1"]
    don = ["do0", "do1"]
    mile = {}

    def init_sync(s):
        s.dma_start(out=identf[:], in_=identf_d[:]).then_inc(p.c["dm"][0], 16)
        s.dma_start(out=masks[:], in_=masks_d[:]).then_inc(p.c["dm"][0], 16)
        for j in range(6):
            s.dma_start(out=lam[:, :, j:j + 1],
                        in_=vecs_d[j:j + 1, :].rearrange("o (k q) -> q k o", q=128),
                        allow_slow_non_contiguous=True).then_inc(p.c["dm"][0], 16)
        p.c["dm"][1] += 5 * 16
        for j, nm in enumerate(("w0", "a0", "k_k", "k_a", "rk", "gam", "bet")):
            s.dma_start(out=vrow[nm][:], in_=vecs_d[6 + j:7 + j, :]).then_inc(p.c["dm"][0], 16)
        p.c["dm"][1] += 10 * 16
    block.sync(init_sync)

    def init_gp(g):
        for w in ("Wr", "Wk", "Wv", "Wo"):
            g.dma_start(out=Ws[w][:], in_=Wd[w].rearrange("(k q) n -> q k n", q=128)).then_inc(p.c["dw"][0], 16)
        g.dma_start(out=w2s[:], in_=w2_d[:]).then_inc(p.c["dw"][0], 16)
        g.dma_start(out=a2s[:], in_=a2_d[:]).then_inc(p.c["dw"][0], 16)
        g.dma_start(out=g2s[:], in_=g2_d[:]).then_inc(p.c["dw"][0], 16)
        g.dma_start(out=l1f[:, :, 0:64], in_=w1_d.rearrange("(k q) n -> q k n", q=128)).then_inc(p.c["dw"][0], 16)
        g.dma_start(out=l1f[:, :, 64:128], in_=a1_d.rearrange("(k q) n -> q k n", q=128)).then_inc(p.c["dw"][0], 16)
        g.dma_start(out=l1f[:, :, 128:256], in_=g1_d.rearrange("(k q) n -> q k n", q=128)).then_inc(p.c["dw"][0], 16)
        p.c["dw"][1] += 10 * 16
    block.gpsimd(init_gp)

    def init_vec(v):
        for col, val in ((0, 1e-5), (1, -0.5), (2, 1.0), (3, 6.4e-4)):
            v.memset(cpool[:, col:col + 1], val)
        v.memset(onesf[:], 1.0)
        v.memset(zer_f[:], 0.0)
        p.wait(v, "dw")
        p.wait(v, "dm")
        for k in range(8):
            v.tensor_copy(lora1[:, k, :], l1f[:, k, :])
            v.tensor_scalar_mul(lora1[:, 8 + k, 0:64], l1f[:, k, 0:64], lam[:, k, 3:4])
            v.tensor_scalar_mul(lora1[:, 8 + k, 64:128], l1f[:, k, 64:128], lam[:, k, 4:5])
            v.tensor_scalar_mul(lora1[:, 8 + k, 128:256], l1f[:, k, 128:256], lam[:, k, 5:6])
        p.inc(v.drain(), "sv")
    block.vector(init_vec)
    SV_ONES = p.val("sv")

    bnames = (("k_k", kkb), ("k_a", kab), ("rk", rkb), ("gam", gab),
              ("bet", beb), ("w0", w0b), ("a0", a0b))
    for bi, (nm, dst) in enumerate(bnames):
        def tsec(t, nm=nm, bi=bi):
            if bi == 0:
                p.wait(t, "dm")
                p.wait(t, "sv", SV_ONES)
            else:
                p.wait(t, "ss")
            m = t.matmul(pb[0][:], onesf[0:1, :], vrow[nm][0:1, 0:512], start=True, stop=True)
            p.inc(m, "tt")
            m = t.matmul(pb[1][:], onesf[0:1, :], vrow[nm][0:1, 512:1024], start=True, stop=True)
            p.inc(m, "tt")
        block.tensor(tsec)

        def ssec(sc, dst=dst):
            p.wait(sc, "tt")
            sc.copy(dst[:, 0:512], pb[0][:])
            sc.copy(dst[:, 512:1024], pb[1][:])
            p.inc(sc.drain(), "ss")
        block.scalar(ssec)
    SS_BC = p.val("ss")

    def init_vec2(v):
        p.wait(v, "ss", SS_BC)
        v.tensor_scalar(omkab[:], kab[:], -1.0, 1.0, ALU.mult, ALU.add)
        p.inc(v.drain(), "sv")
    block.vector(init_vec2)

    for g in reversed(ictx):
        g.__exit__(None, None, None)
    xt = sb("xt", [128, 2, C])
    st = sb("st", [128, 16])
    ln0 = sb("ln0", [128, C])
    xreg = sb("xreg", [128, C])
    sqs = xreg
    lwp = xreg
    lnat = sb("lnat", [128, 16, 128], BF16)
    lnxx = lnat
    prevc = sb("prevc", [128, 8])
    mixr = sb("mixr", [128, 8, 128], BF16)
    mixk = sb("mixk", [128, 8, 128], BF16)
    mixv = sb("mixv", [128, 8, 128], BF16)
    r_s = sb("r_s", [128, C])
    k_s = sb("k_s", [128, C])
    v_s = sb("v_s", [128, C])
    t1_s = sb("t1_s", [128, C])
    t2_s = sb("t2_s", [128, C])
    asig = sb("asig", [128, C])
    g_s = sb("g_s", [128, C])
    win_b = sb("win_b", [128, 64])
    ain_b = sb("ain_b", [128, 64])
    gin_b = sb("gin_b", [128, 128])
    winT = sb("winT", [64, 128], BF16)
    ainT = sb("ainT", [64, 128], BF16)
    ginT = sb("ginT", [128, 128], BF16)
    kkn = sb("kkn", [128, C])
    kt_s = sb("kt_s", [128, C])
    bsc = sb("bsc", [128, C])
    ss16 = sb("ss16", [128, 16])
    rn16 = sb("rn16", [128, 16])
    cum = sb("cum", [128, 8, 128])
    lncm = cum
    dd = sb("dd", [128, 8, 128], BF16)
    dLf = sb("dLf", [128, 8])
    ddi = sb("ddi", [128, 8, 128], BF16)
    dldc = cum
    bh_c = mixr
    kh_c = mixk
    rt_c = mixv
    btl = sb("btl", [128, C], BF16)
    ktl = sb("ktl", [128, C], BF16)
    vyc = sb("vyc", [128, C], BF16)
    v_b = vyc
    o_s = sb("o_s", [128, C])
    y_s = o_s
    ao_s = ln0
    x1_s = sb("x1_s", [128, 2, C])
    xm_s = t1_s
    impacc = sb("impacc", [128, TILES])
    n1acc = sb("n1acc", [128, TILES])
    stf2 = sb("stf2", [128, H * 64])
    stb2 = sb("stb2", [128, H * 64], BF16)
    wsn128 = ("A", "AT", "F1T", "CT", "GbT", "GkT", "F2", "F3", "F4", "F5", "F6",
              "F7", "Q0", "Q1", "TT")
    wsn64 = ("CV", "M0", "U")
    ws = [dict([(n, sb(f"ws{i}{n}", [128, 128], BF16)[:]) for n in wsn128]
               + [(n, sb(f"ws{i}{n}", [128, 64], BF16)[:]) for n in wsn64])
          for i in range(2)]

    for i in range(TILES):
        b = i // NT
        chunk = i % NT
        par = i % 2

        def s_load(s, i=i, par=par):
            if i == 0:
                p.wait(s, "ss", SS_BC)
                p.wait(s, "sv", SV_ONES)
            if i >= 2:
                p.wait(s, "sv", mile[("xfree", i - 2)])
            s.dma_start(out=xt[:, par, :], in_=x_d[i * L:(i + 1) * L, :]).then_inc(p.c[dxn[par]][0], 16)
            p.c[dxn[par]][1] += 16
        block.sync(s_load)
        xv = p.val(dxn[par])

        def s_sumsq(sc, par=par, xv=xv, i=i):
            p.wait(sc, dxn[par], xv)
            if i > 0:
                p.wait(sc, "sv", mile[("mix", i - 1)])
            sc.activation(sqs[:], xt[:, par, :], AF.Square, accum_out=st[:, 4:5])
            sc.activation(st[:, 5:6], st[:, 4:5], AF.Copy, scale=INV_C)
            p.inc(sc.drain(), "ss")
        block.scalar(s_sumsq)

        def s_var(v, par=par, xv=xv, i=i):
            p.wait(v, dxn[par], xv)
            if i > 0:
                p.wait(v, "tt", mile[("trdone", i - 1)])
            v.reduce_sum(st[:, 0:1], xt[:, par, :], axis=AX.X)
            p.wait(v, "ss")
            v.tensor_scalar_mul(st[:, 2:3], st[:, 0:1], INV_C)
            v.drain()
            v.tensor_mul(st[:, 1:2], st[:, 2:3], st[:, 2:3])
            v.drain()
            v.tensor_sub(st[:, 3:4], st[:, 5:6], st[:, 1:2])
            p.inc(v.drain(), "sv")
        block.vector(s_var)

        def s_rstd(sc):
            p.wait(sc, "sv")
            sc.activation(st[:, 7:8], st[:, 3:4], AF.Ln, bias=cpool[:, 0:1])
            sc.drain()
            sc.activation(st[:, 6:7], st[:, 7:8], AF.Exp, scale=-0.5)
            p.inc(sc.drain(), "ss")
        block.scalar(s_rstd)

        def s_ln0(v, par=par):
            p.wait(v, "ss")
            i1 = v.tensor_scalar(ln0[:], xt[:, par, :], st[:, 2:3], st[:, 6:7],
                                 ALU.subtract, ALU.mult)
            p.inc(i1, "sv")
        block.vector(s_ln0)
        SV_LN0 = p.val("sv")

        tt0 = p.val("tt")
        sv0 = p.val("sv")
        ss0 = p.val("ss")
        trev = [sv0 + k + 1 for k in range(8)]
        trevs = [ss0 + k + 1 for k in range(8)]

        def s_tr(t, tt0=tt0, trev=trev, trevs=trevs, i=i):
            p.wait(t, "sv", SV_LN0)
            for k in range(8):
                if k >= 2:
                    p.wait(t, "sv", trev[k - 2])
                    p.wait(t, "ss", trevs[k - 2])
                i1 = t.transpose(pb[2 + (k % 2)][:, 0:128], ln0[:, k * 128:(k + 1) * 128], identf[:])
                p.inc(i1, "tt")
            mile[("trdone", i)] = p.val("tt")
        block.tensor(s_tr)

        for k in range(8):
            def s_trev_v(v, k=k, tt0=tt0):
                p.wait(v, "tt", tt0 + k + 1)
                i1 = v.tensor_copy(lncm[:, k, :], pb[2 + (k % 2)][:, 0:128])
                assert p.inc(i1, "sv") == trev[k]
            block.vector(s_trev_v)

            def s_trev_s(sc, k=k, tt0=tt0):
                p.wait(sc, "tt", tt0 + k + 1)
                sc.copy(lnxx[:, k, :], pb[2 + (k % 2)][:, 0:128])
                sc.drain()
                assert p.inc(sc.nop(), "ss") == trevs[k]
            block.scalar(s_trev_s)

        def s_mix(v, chunk=chunk, i=i):
            p.wait(v, "sv", trev[7])
            if i > 0:
                p.wait(v, "tt", mile[("projdone", i - 1)])
            for k in range(8):
                v.tensor_sub(xreg[:, k * 128 + 1:k * 128 + 128], lncm[:, k, 0:127], lncm[:, k, 1:128])
                if chunk == 0:
                    v.tensor_scalar_mul(xreg[:, k * 128:k * 128 + 1], lncm[:, k, 0:1], -1.0)
                else:
                    v.tensor_sub(xreg[:, k * 128:k * 128 + 1], prevc[:, k:k + 1], lncm[:, k, 0:1])
            v.drain()
            v.tensor_copy(prevc[:], lncm[:, :, 127])
            for k in range(8):
                v.scalar_tensor_tensor(mixr[:, k, :], xreg[:, k * 128:(k + 1) * 128], lam[:, k, 0:1],
                                       lncm[:, k, :], ALU.mult, ALU.add)
                v.scalar_tensor_tensor(mixk[:, k, :], xreg[:, k * 128:(k + 1) * 128], lam[:, k, 1:2],
                                       lncm[:, k, :], ALU.mult, ALU.add)
                v.scalar_tensor_tensor(mixv[:, k, :], xreg[:, k * 128:(k + 1) * 128], lam[:, k, 2:3],
                                       lncm[:, k, :], ALU.mult, ALU.add)
                v.tensor_copy(lnxx[:, 8 + k, :], xreg[:, k * 128:(k + 1) * 128])
            p.inc(v.drain(), "sv")
        block.vector(s_mix)
        MIX_SV = p.val("sv")
        mile[("mix", i)] = MIX_SV

        def s_proj(t, i=i):
            p.wait(t, "sv", MIX_SV)
            p.wait(t, "ss", trevs[7])
            if i == 0:
                p.wait(t, "dw")
                p.wait(t, "ss", SS_BC)
            else:
                p.wait(t, "ss", mile[("projfree", i - 1)])
                p.wait(t, "sv", mile[("projfreev", i - 1)])
                p.wait(t, "ss", mile[("lorafree", i - 1)])
            for pi, (mix, wname) in enumerate(((mixr, "Wr"), (mixk, "Wk"), (mixv, "Wv"))):
                b0, b1 = pb[2 * pi], pb[2 * pi + 1]
                m0 = m1 = None
                for k in range(8):
                    m0 = t.matmul(b0[:], mix[:, k, :], Ws[wname][:, k, 0:512],
                                  start=(k == 0), stop=(k == 7))
                    m1 = t.matmul(b1[:], mix[:, k, :], Ws[wname][:, k, 512:1024],
                                  start=(k == 0), stop=(k == 7))
                p.inc(m0, "tt")
                p.inc(m1, "tt")
            mlast = None
            for k in range(16):
                mlast = t.matmul(pb[6][:, 0:256], lnxx[:, k, :], lora1[:, k, :],
                                 start=(k == 0), stop=(k == 15))
            p.inc(mlast, "tt")
            mile[("projdone", i)] = p.val("tt")
        block.tensor(s_proj)
        TT_PROJ = p.val("tt")

        def s_pev(sc, i=i):
            p.wait(sc, "tt", TT_PROJ - 5)
            sc.copy(r_s[:, 0:512], pb[0][:])
            sc.copy(r_s[:, 512:1024], pb[1][:])
            p.wait(sc, "tt", TT_PROJ - 3)
            sc.copy(k_s[:, 0:512], pb[2][:])
            sc.copy(k_s[:, 512:1024], pb[3][:])
            mile[("projfree", i)] = p.inc(sc.drain(), "ss")
            p.wait(sc, "tt", TT_PROJ)
            sc.activation(win_b[:], pb[6][:, 0:64], AF.Tanh)
            sc.copy(ain_b[:], pb[6][:, 64:128])
            sc.activation(gin_b[:], pb[6][:, 128:256], AF.Sigmoid)
            mile[("lorafree", i)] = p.inc(sc.drain(), "ss")
        block.scalar(s_pev)
        SS_LORIN = p.val("ss")

        def s_pev_v(v, i=i):
            p.wait(v, "tt", TT_PROJ - 1)
            v.tensor_copy(v_s[:, 0:512], pb[4][:])
            v.tensor_copy(v_s[:, 512:1024], pb[5][:])
            mile[("projfreev", i)] = p.inc(v.drain(), "sv")
        block.vector(s_pev_v)

        def s_l2(t):
            p.wait(t, "ss", SS_LORIN)
            i1 = t.transpose(pb[6][0:64, 0:128], win_b[:], identf[:])
            p.inc(i1, "tt")
            i2 = t.transpose(pb[7][0:64, 0:128], ain_b[:], identf[:])
            p.inc(i2, "tt")
            i3 = t.transpose(pb[6][:, 128:256], gin_b[:], identf[:])
            p.inc(i3, "tt")
        block.tensor(s_l2)
        TT_LT = p.val("tt")

        def s_l2ev(v):
            p.wait(v, "tt", TT_LT)
            v.tensor_copy(winT[:], pb[6][0:64, 0:128])
            v.tensor_copy(ainT[:], pb[7][0:64, 0:128])
            v.tensor_copy(ginT[:], pb[6][:, 128:256])
            p.inc(v.drain(), "sv")
        block.vector(s_l2ev)
        SV_LT = p.val("sv")

        def s_l3(t, i=i):
            p.wait(t, "sv", SV_LT)
            for (lhsT, rhs, bk) in ((winT, w2s, 0), (ainT, a2s, 2), (ginT, g2s, 4)):
                m = t.matmul(pb[bk][:], lhsT[:], rhs[:, 0:512], start=True, stop=True)
                p.inc(m, "tt")
                m = t.matmul(pb[bk + 1][:], lhsT[:], rhs[:, 512:1024], start=True, stop=True)
                p.inc(m, "tt")
        block.tensor(s_l3)
        TT_L3 = p.val("tt")

        def s_wag_v(v, i=i):
            p.wait(v, "tt", TT_L3)
            if i > 0:
                v.wait_ge(p.c["dg"][0], mile[("xmdma", i - 1)])
            v.tensor_add(t1_s[:, 0:512], pb[0][:], w0b[:, 0:512])
            v.tensor_add(t1_s[:, 512:1024], pb[1][:], w0b[:, 512:1024])
            v.tensor_add(t2_s[:, 0:512], pb[2][:], a0b[:, 0:512])
            v.tensor_add(t2_s[:, 512:1024], pb[3][:], a0b[:, 512:1024])
            v.tensor_copy(g_s[:, 0:512], pb[4][:])
            v.tensor_copy(g_s[:, 512:1024], pb[5][:])
            p.inc(v.drain(), "sv")
        block.vector(s_wag_v)
        SV_WAG = p.val("sv")

        def s_wag_s(sc):
            p.wait(sc, "sv", SV_WAG)
            sc.activation(lwp[:], t1_s[:], AF.Exp, scale=-1.0)
            sc.drain()
            sc.activation(t1_s[:], lwp[:], AF.Ln, bias=cpool[:, 2:3])
            sc.drain()
            sc.activation(lwp[:], t1_s[:], AF.Exp, scale=-1.0, bias=cpool[:, 1:2])
            sc.activation(asig[:], t2_s[:], AF.Sigmoid)
            p.inc(sc.drain(), "ss")
        block.scalar(s_wag_s)
        SS_WAG = p.val("ss")

        def s_sp1(v):
            p.wait(v, "ss", SS_WAG)
            v.tensor_mul(t1_s[:], k_s[:], kkb[:])
            v.tensor_mul(t2_s[:], t1_s[:], t1_s[:])
            v.reduce_sum(ss16[:], t2_s[:].rearrange("q (h n) -> q h n", h=H), axis=AX.X)
            p.inc(v.drain(), "sv")
        block.vector(s_sp1)

        def s_sp1s(sc):
            p.wait(sc, "sv")
            sc.activation(rn16[:], ss16[:], AF.Sqrt)
            p.inc(sc.drain(), "ss")
        block.scalar(s_sp1s)

        def s_sp2(v):
            p.wait(v, "ss")
            v.tensor_scalar_max(rn16[:], rn16[:], 1e-12)
            v.drain()
            v.reciprocal(ss16[:], rn16[:])
            v.drain()
            bc16 = ss16[:].rearrange("q (h o) -> q h o", o=1).broadcast_to((128, H, N))
            v.tensor_tensor(kkn[:].rearrange("q (h n) -> q h n", h=H), t1_s[:].rearrange("q (h n) -> q h n", h=H), bc16, ALU.mult)
            v.tensor_mul(bsc[:], kkn[:], asig[:])
            v.tensor_mul(t2_s[:], asig[:], kab[:])
            v.tensor_add(t2_s[:], t2_s[:], omkab[:])
            v.tensor_mul(kt_s[:], k_s[:], t2_s[:])
            v.tensor_copy(v_b[:], v_s[:])
            p.inc(v.drain(), "sv")
        block.vector(s_sp2)
        SV_PREP = p.val("sv")

        tt1 = p.val("tt")
        sv1 = p.val("sv")
        lwev = [sv1 + k + 1 for k in range(8)]

        def s_ct1(t, tt1=tt1, lwev=lwev):
            p.wait(t, "sv", SV_PREP)
            for k in range(8):
                if k >= 2:
                    p.wait(t, "sv", lwev[k - 2])
                i1 = t.transpose(pb[2 + (k % 2)][:, 0:128], lwp[:, k * 128:(k + 1) * 128], identf[:])
                p.inc(i1, "tt")
        block.tensor(s_ct1)

        for k in range(8):
            def s_lwev(v, k=k, tt1=tt1):
                p.wait(v, "tt", tt1 + k + 1)
                v.tensor_tensor_scan(cum[:, k, :], pb[2 + (k % 2)][:, 0:128],
                                     zer_f[:], 0.0,
                                     ALU.add, ALU.bypass)
                v.drain()
                assert p.inc(v.nop(), "sv") == lwev[k]
            block.vector(s_lwev)

        def s_dexp(sc):
            p.wait(sc, "sv", lwev[7])
            sc.activation(dd[:].rearrange("q a b -> q (a b)"),
                          cum[:].rearrange("q a b -> q (a b)"), AF.Exp, scale=-1.0)
            sc.activation(ddi[:].rearrange("q a b -> q (a b)"),
                          cum[:].rearrange("q a b -> q (a b)"), AF.Exp)
            sc.activation(dLf[:], cum[:, :, 127], AF.Exp, scale=-1.0)
            sc.drain()
            for k in range(8):
                sc.mul(asig[:, k * 128 + 1:k * 128 + 128], dd[:, k, 0:127], -1.0)
            p.inc(sc.drain(), "ss")
        block.scalar(s_dexp)

        def s_dsh0(v):
            p.wait(v, "ss")
            v.memset(asig[:].rearrange("q (a b) -> q a b", a=8)[:, :, 0:1], -1.0)
            v.tensor_copy(dLf[:], cumf_dL[:]) if False else None
            for k in range(8):
                v.tensor_scalar_mul(dldc[:, k, :], ddi[:, k, :], dLf[:, k:k + 1])
            p.inc(v.drain(), "sv")
        block.vector(s_dsh0)
        SV_DLD = p.val("sv")

        tt2 = p.val("tt")
        sv2 = p.val("sv")
        ctev = [sv2 + j + 1 for j in range(40)]

        def s_ct2(t, tt2=tt2, ctev=ctev):
            p.wait(t, "sv", SV_DLD)
            srcs = [kkn, bsc, kt_s, r_s, None]
            for si in range(5):
                for k in range(8):
                    idx = si * 8 + k
                    if idx >= 2:
                        p.wait(t, "sv", ctev[idx - 2])
                    if si == 4:
                        i1 = t.transpose(pb[2 + (idx % 2)][:, 0:128], dldc[:, k, :], identf[:])
                    else:
                        i1 = t.transpose(pb[2 + (idx % 2)][:, 0:128],
                                         srcs[si][:, k * 128:(k + 1) * 128], identf[:])
                    p.inc(i1, "tt")
        block.tensor(s_ct2)

        specs = ((None, None), (bh_c, ddi), (kh_c, ddi), (rt_c, dd), (None, None))
        for si in range(5):
            for k in range(8):
                idx = si * 8 + k
                def s_ctev(v, idx=idx, k=k, si=si):
                    dst, mulsrc = specs[si]
                    p.wait(v, "tt", tt2 + idx + 1)
                    bank = pb[2 + (idx % 2)]
                    if si == 4:
                        v.tensor_tensor(btl[:, k * 128:(k + 1) * 128], bank[:, 0:128],
                                        bsc[:, k * 128:(k + 1) * 128], ALU.mult)
                        v.tensor_tensor(ktl[:, k * 128:(k + 1) * 128], bank[:, 0:128],
                                        kt_s[:, k * 128:(k + 1) * 128], ALU.mult)
                    elif si == 0:
                        v.tensor_tensor(lnat[:, k, :], bank[:, 0:128],
                                        asig[:, k * 128:(k + 1) * 128], ALU.mult)
                    else:
                        v.tensor_tensor(dst[:, k, :], bank[:, 0:128], mulsrc[:, k, :], ALU.mult)
                    v.drain()
                    assert p.inc(v.nop(), "sv") == ctev[idx]
                block.vector(s_ctev)
        SV_CT = p.val("sv")

        for h in range(H):
            w = ws[h % 2]
            slot = b * H + h
            srow = 64 * (slot % 2)
            scol = (slot // 2) * 64
            stbS = stb2[srow:srow + 64, scol:scol + 64]
            stfS = stf2[srow:srow + 64, scol:scol + 64]
            hp = 64 * (h % 2)
            hch = h // 2
            BQ = [pb[4 * (h % 2) + j] for j in range(4)]
            atS = lnat[hp:hp + 64, hch, :]
            bhS = bh_c[hp:hp + 64, hch, :]
            khS = kh_c[hp:hp + 64, hch, :]
            rtS = rt_c[hp:hp + 64, hch, :]
            vS = v_b[:, h * 64:(h + 1) * 64]
            btlS = btl[:, h * 64:(h + 1) * 64]
            ktlS = ktl[:, h * 64:(h + 1) * 64]
            dLS = dLf[hp:hp + 64, hch:hch + 1]

            svh = p.val("sv")
            bev = svh + 1
            sqev = [svh + 2 + l for l in range(6)]
            qev = [svh + 8 + j for j in range(6)]
            cvev = svh + 14
            if chunk > 0:
                m0ev = svh + 15
                uev = svh + 16
                scanev = svh + 17
            else:
                m0ev = None
                uev = svh + 15
                scanev = svh + 16
            mile[("scanev", i, h)] = scanev

            def s_scan_t(t, h=h, w=w, stbS=stbS, atS=atS, bhS=bhS, khS=khS,
                         rtS=rtS, vS=vS, btlS=btlS, ktlS=ktlS, chunk=chunk, BQ=BQ,
                         bev=bev, sqev=sqev, qev=qev, cvev=cvev, m0ev=m0ev,
                         uev=uev, i=i):
                if h < 2:
                    p.wait(t, "sv", SV_CT)
                else:
                    p.wait(t, "sv", mile[("scanev", i, h - 2)])
                m = t.matmul(BQ[0][:, 0:128], bhS, atS, start=True, stop=True)
                p.inc(m, "tt")
                m = t.matmul(BQ[0][:, 128:256], bhS, rtS, start=True, stop=True)
                p.inc(m, "tt")
                m = t.matmul(BQ[1][:, 0:128], khS, atS, start=True, stop=True)
                p.inc(m, "tt")
                m = t.matmul(BQ[1][:, 128:256], khS, rtS, start=True, stop=True)
                p.inc(m, "tt")
                m = t.matmul(BQ[2][:, 0:128], atS, bhS, start=True, stop=True)
                p.inc(m, "tt")
                p.wait(t, "sv", bev)
                cur, curT = "A", "AT"
                for lvl in range(6):
                    bq = lvl % 2
                    m = t.matmul(BQ[bq][:, 256:384], w[curT], w[cur], start=True, stop=True)
                    p.inc(m, "tt")
                    if lvl < 5:
                        m = t.matmul(BQ[bq][:, 384:512], w[cur], w[curT], start=True, stop=True)
                        p.inc(m, "tt")
                    p.wait(t, "sv", sqev[lvl])
                m = t.matmul(BQ[2][:, 128:256], w["F2"], w["F1T"], start=True, stop=True)
                p.inc(m, "tt")
                p.wait(t, "sv", qev[0])
                for j, fn in enumerate(("F3", "F4", "F5", "F6", "F7")):
                    m = t.matmul(BQ[2][:, 128:256], w[fn], w["Q%d" % (j % 2)],
                                 start=True, stop=True)
                    p.inc(m, "tt")
                    p.wait(t, "sv", qev[j + 1])
                m = t.matmul(BQ[2][:, 0:64], w["CT"], vS, start=True, stop=True)
                p.inc(m, "tt")
                p.wait(t, "sv", cvev)
                if chunk > 0:
                    m = t.matmul(BQ[3][:, 0:64], atS, stbS,
                                 start=True, stop=True)
                    p.inc(m, "tt")
                    p.wait(t, "sv", m0ev)
                    m = t.matmul(BQ[3][:, 64:128], w["TT"], w["CV"], start=True, stop=False)
                    m = t.matmul(BQ[3][:, 64:128], w["TT"], w["M0"], start=False, stop=True)
                    p.inc(m, "tt")
                else:
                    m = t.matmul(BQ[3][:, 64:128], w["TT"], w["CV"], start=True, stop=True)
                    p.inc(m, "tt")
                p.wait(t, "sv", uev)
                if chunk > 0:
                    m = t.matmul(BQ[3][:, 128:192], rtS, stbS,
                                 start=True, stop=False)
                    m = t.matmul(BQ[3][:, 128:192], w["GbT"], w["U"], start=False, stop=False)
                else:
                    m = t.matmul(BQ[3][:, 128:192], w["GbT"], w["U"], start=True, stop=False)
                m = t.matmul(BQ[3][:, 128:192], w["GkT"], vS, start=False, stop=True)
                p.inc(m, "tt")
                m = t.matmul(BQ[3][0:64, 192:256], btlS, w["U"], start=True, stop=False)
                m = t.matmul(BQ[3][0:64, 192:256], ktlS, vS, start=False, stop=True)
                p.inc(m, "tt")
            block.tensor(s_scan_t)
            TT_SC = p.val("tt")
            n_incs = 5 + 11 + 6 + 1 + (1 if chunk > 0 else 0) + 1 + 1 + 1

            def s_scan_v(v, h=h, w=w, i=i, stfS=stfS, stbS=stbS, dLS=dLS, chunk=chunk,
                         TT_SC=TT_SC, BQ=BQ, n_incs=n_incs, bev=bev, sqev=sqev,
                         qev=qev, cvev=cvev, m0ev=m0ev, uev=uev, scanev=scanev):
                t0 = TT_SC - n_incs
                p.wait(v, "tt", t0 + 5)
                v.select(w["AT"], MSU, BQ[0][:, 0:128], zer_f[:])
                v.select(w["F1T"], MSU, BQ[0][:, 0:128], identf[:])
                v.select(w["GbT"], MIU, BQ[0][:, 128:256], zer_f[:])
                v.select(w["CT"], MSU, BQ[1][:, 0:128], zer_f[:])
                v.select(w["GkT"], MIU, BQ[1][:, 128:256], zer_f[:])
                v.select(w["A"], MSL, BQ[2][:, 0:128], zer_f[:])
                assert p.inc(v.drain(), "sv") == bev
                nmm = t0 + 5
                for lvl in range(6):
                    nmm += 2 if lvl < 5 else 1
                    p.wait(v, "tt", nmm)
                    bq = lvl % 2
                    fdst = ("F2", "F3", "F4", "F5", "F6", "F7")[lvl]
                    v.select(w[fdst], MSL, BQ[bq][:, 256:384], identf[:])
                    if lvl < 5:
                        v.tensor_copy(w["A"], BQ[bq][:, 256:384])
                        v.tensor_copy(w["AT"], BQ[bq][:, 384:512])
                    assert p.inc(v.drain(), "sv") == sqev[lvl]
                for j in range(6):
                    nmm += 1
                    p.wait(v, "tt", nmm)
                    dst = "TT" if j == 5 else ("Q%d" % (j % 2))
                    v.tensor_copy(w[dst], BQ[2][:, 128:256])
                    v.drain()
                    assert p.inc(v.nop(), "sv") == qev[j]
                nmm += 1
                p.wait(v, "tt", nmm)
                v.tensor_copy(w["CV"], BQ[2][:, 0:64])
                v.drain()
                assert p.inc(v.nop(), "sv") == cvev
                if chunk > 0:
                    nmm += 1
                    p.wait(v, "tt", nmm)
                    v.tensor_copy(w["M0"], BQ[3][:, 0:64])
                    v.drain()
                    assert p.inc(v.nop(), "sv") == m0ev
                nmm += 1
                p.wait(v, "tt", nmm)
                v.tensor_copy(w["U"], BQ[3][:, 64:128])
                v.drain()
                assert p.inc(v.nop(), "sv") == uev
                nmm += 2
                p.wait(v, "tt", nmm)
                v.tensor_copy(o_s[:, h * 64:(h + 1) * 64], BQ[3][:, 128:192])
                if chunk > 0:
                    v.scalar_tensor_tensor(stfS, stfS, dLS,
                                           BQ[3][0:64, 192:256], ALU.mult, ALU.add)
                else:
                    v.tensor_copy(stfS, BQ[3][0:64, 192:256])
                v.drain()
                v.tensor_copy(stbS, stfS)
                v.drain()
                assert p.inc(v.nop(), "sv") == scanev
            block.vector(s_scan_v)
        SV_SCANEND = p.val("sv")

        if debug and "o" in debug:
            def s_dbgo(sy, i=i):
                p.wait(sy, "sv", SV_SCANEND)
                sy.dma_start(out=dbg_d["o"][i * L:(i + 1) * L, :], in_=o_s[:]).then_inc(p.c["dbgs"][0], 16)
                p.c["dbgs"][1] += 16
            block.sync(s_dbgo)
            DBGO = p.val("dbgs")

            def s_dbgow(v, DBGO=DBGO):
                v.wait_ge(p.c["dbgs"][0], DBGO)
            block.vector(s_dbgow)

        def s_epi_v(v):
            p.wait(v, "sv", SV_SCANEND)
            v.tensor_mul(t1_s[:], o_s[:], o_s[:])
            v.reduce_sum(ss16[:], t1_s[:].rearrange("q (h n) -> q h n", h=H), axis=AX.X)
            v.reduce_sum(rn16[:], o_s[:].rearrange("q (h n) -> q h n", h=H), axis=AX.X)
            v.drain()
            v.tensor_scalar_mul(rn16[:], rn16[:], 1.0 / N)
            v.tensor_scalar_mul(ss16[:], ss16[:], 1.0 / N)
            v.drain()
            v.tensor_mul(t2_s[:, 0:16], rn16[:], rn16[:])
            v.drain()
            v.tensor_sub(ss16[:], ss16[:], t2_s[:, 0:16])
            p.inc(v.drain(), "sv")
        block.vector(s_epi_v)

        def s_epi_s(sc):
            p.wait(sc, "sv")
            sc.activation(t2_s[:, 16:32], ss16[:], AF.Ln, bias=cpool[:, 3:4])
            sc.drain()
            sc.activation(ss16[:], t2_s[:, 16:32], AF.Exp, scale=-0.5)
            p.inc(sc.drain(), "ss")
        block.scalar(s_epi_s)

        def s_epi_v2(v):
            p.wait(v, "ss")
            bc = lambda a: a.rearrange("q (h o) -> q h o", o=1).broadcast_to((128, H, N))
            h3 = lambda a: a.rearrange("q (h n) -> q h n", h=H)
            v.tensor_tensor(h3(y_s[:]), h3(o_s[:]), bc(rn16[:]), ALU.subtract)
            v.tensor_tensor(h3(y_s[:]), h3(y_s[:]), bc(ss16[:]), ALU.mult)
            v.tensor_mul(y_s[:], y_s[:], gab[:])
            v.tensor_add(y_s[:], y_s[:], beb[:])
            v.tensor_mul(t1_s[:], r_s[:], kt_s[:])
            v.tensor_mul(t1_s[:], t1_s[:], rkb[:])
            v.reduce_sum(rn16[:], t1_s[:].rearrange("q (h n) -> q h n", h=H), axis=AX.X)
            v.drain()
            v.tensor_tensor(h3(t2_s[:]), h3(v_s[:]), bc(rn16[:]), ALU.mult)
            v.tensor_add(y_s[:], y_s[:], t2_s[:])
            v.tensor_mul(y_s[:], y_s[:], g_s[:])
            p.inc(v.drain(), "sv")
        block.vector(s_epi_v2)
        SV_Y = p.val("sv")

        tt3 = p.val("tt")
        ss3 = p.val("ss")
        yev = [ss3 + k + 1 for k in range(8)]

        def s_wo_t(t, tt3=tt3, yev=yev):
            p.wait(t, "sv", SV_Y)
            for k in range(8):
                if k >= 2:
                    p.wait(t, "ss", yev[k - 2])
                i1 = t.transpose(pb[2 + (k % 2)][:, 128:256], y_s[:, k * 128:(k + 1) * 128], identf[:])
                p.inc(i1, "tt")
        block.tensor(s_wo_t)

        for k in range(8):
            def s_yev(sc, k=k, tt3=tt3):
                p.wait(sc, "tt", tt3 + k + 1)
                sc.copy(vyc[:, k * 128:(k + 1) * 128], pb[2 + (k % 2)][:, 128:256])
                sc.drain()
                assert p.inc(sc.nop(), "ss") == yev[k]
            block.scalar(s_yev)

        def s_wo_mm(t):
            p.wait(t, "ss", yev[7])
            m0 = m1 = None
            for k in range(8):
                m0 = t.matmul(pb[0][:], vyc[:, k * 128:(k + 1) * 128], Ws["Wo"][:, k, 0:512],
                              start=(k == 0), stop=(k == 7))
                m1 = t.matmul(pb[1][:], vyc[:, k * 128:(k + 1) * 128], Ws["Wo"][:, k, 512:1024],
                              start=(k == 0), stop=(k == 7))
            p.inc(m0, "tt")
            p.inc(m1, "tt")
        block.tensor(s_wo_mm)
        TT_WO = p.val("tt")

        def s_fin_s(sc):
            p.wait(sc, "tt", TT_WO)
            sc.copy(ao_s[:, 0:512], pb[0][:])
            sc.copy(ao_s[:, 512:1024], pb[1][:])
            sc.drain()
            sc.activation(sqs[:], ao_s[:], AF.Square, accum_out=st[:, 4:5])
            sc.activation(st[:, 5:6], st[:, 4:5], AF.Copy, scale=INV_C)
            p.inc(sc.drain(), "ss")
        block.scalar(s_fin_s)

        def s_fin_v(v):
            p.wait(v, "ss")
            v.reduce_sum(st[:, 0:1], ao_s[:], axis=AX.X)
            v.drain()
            v.tensor_scalar_mul(st[:, 2:3], st[:, 0:1], INV_C)
            v.drain()
            v.tensor_mul(st[:, 1:2], st[:, 2:3], st[:, 2:3])
            v.drain()
            v.tensor_sub(st[:, 3:4], st[:, 5:6], st[:, 1:2])
            p.inc(v.drain(), "sv")
        block.vector(s_fin_v)

        def s_fin_s2(sc):
            p.wait(sc, "sv")
            sc.activation(st[:, 7:8], st[:, 3:4], AF.Ln, bias=cpool[:, 0:1])
            sc.drain()
            sc.activation(st[:, 6:7], st[:, 7:8], AF.Exp, scale=-0.5)
            p.inc(sc.drain(), "ss")
        block.scalar(s_fin_s2)

        def s_fin_v2(v, par=par, i=i):
            p.wait(v, "ss")
            if i >= 2:
                v.wait_ge(p.c[don[par]][0], mile[("outdma", i - 2)])
            v.tensor_scalar(t1_s[:], ao_s[:], st[:, 2:3], st[:, 6:7],
                            ALU.subtract, ALU.mult)
            v.tensor_add(x1_s[:, par, :], xt[:, par, :], t1_s[:])
            mile[("xfree", i)] = p.inc(v.nop(), "sv")
        block.vector(s_fin_v2)

        def s_fin_s3(sc, par=par):
            p.wait(sc, "sv")
            sc.activation(sqs[:], x1_s[:, par, :], AF.Square, accum_out=st[:, 4:5])
            sc.activation(st[:, 5:6], st[:, 4:5], AF.Copy, scale=INV_C)
            p.inc(sc.drain(), "ss")
        block.scalar(s_fin_s3)

        def s_fin_v3(v, par=par):
            p.wait(v, "ss")
            v.reduce_sum(st[:, 0:1], x1_s[:, par, :], axis=AX.X)
            v.drain()
            v.tensor_scalar_mul(st[:, 2:3], st[:, 0:1], INV_C)
            v.drain()
            v.tensor_mul(st[:, 1:2], st[:, 2:3], st[:, 2:3])
            v.drain()
            v.tensor_sub(st[:, 3:4], st[:, 5:6], st[:, 1:2])
            p.inc(v.drain(), "sv")
        block.vector(s_fin_v3)

        def s_fin_s4(sc):
            p.wait(sc, "sv")
            sc.activation(st[:, 7:8], st[:, 3:4], AF.Ln, bias=cpool[:, 0:1])
            sc.drain()
            sc.activation(st[:, 6:7], st[:, 7:8], AF.Exp, scale=-0.5)
            p.inc(sc.drain(), "ss")
        block.scalar(s_fin_s4)

        def s_fin_v4(v, par=par, i=i):
            p.wait(v, "ss")
            v.tensor_scalar(xm_s[:], x1_s[:, par, :], st[:, 2:3], st[:, 6:7],
                            ALU.subtract, ALU.mult)
            v.reduce_sum(n1acc[:, i:i + 1], xm_s[:], axis=AX.X, apply_absolute_value=True)
            p.inc(v.drain(), "sv")
        block.vector(s_fin_v4)

        def s_fin_s5(sc, i=i):
            p.wait(sc, "sv")
            sc.activation(sqs[:], xm_s[:], AF.Square, accum_out=st[:, 8:9])
            sc.drain()
            sc.activation(impacc[:, i:i + 1], st[:, 8:9], AF.Sqrt)
            mile[("outready", i)] = p.inc(sc.drain(), "ss")
        block.scalar(s_fin_s5)

        def s_out(s, i=i, par=par):
            p.wait(s, "ss", mile[("outready", i)])
            s.dma_start(out=x1_d[i * L:(i + 1) * L, :], in_=x1_s[:, par, :]).then_inc(p.c[don[par]][0], 16)
            p.c[don[par]][1] += 16
            mile[("outdma", i)] = p.c[don[par]][1]
        block.sync(s_out)

        def s_outg(g, i=i):
            p.wait(g, "ss", mile[("outready", i)])
            g.dma_start(out=xm_d[i * L:(i + 1) * L, :], in_=xm_s[:]).then_inc(p.c["dg"][0], 16)
            p.c["dg"][1] += 16
            mile[("xmdma", i)] = p.c["dg"][1]
        block.gpsimd(s_outg)

        if debug:
            dbg_src = {"r": r_s, "kt": kt_s, "v": v_s, "lwpos": lwp, "asig": asig,
                       "g": g_s, "y": y_s, "kkn": kkn, "bscan": bsc,
                       "ln0": ln0, "ao": ao_s, "xm": xm_s}

            def s_dbg(s, i=i):
                p.wait(s, "ss", mile[("outready", i)])
                for nm in [d for d in debug if d != "o"]:
                    s.dma_start(out=dbg_d[nm][i * L:(i + 1) * L, :], in_=dbg_src[nm][:]).then_inc(p.c["dbgs"][0], 16)
                    p.c["dbgs"][1] += 16
            block.sync(s_dbg)
            DBGV = p.val("dbgs")

            def s_dbgwait(v, DBGV=DBGV):
                v.wait_ge(p.c["dbgs"][0], DBGV)
            block.vector(s_dbgwait)

    def s_tail(s):
        p.wait(s, "ss", mile[("outready", TILES - 1)])
        s.dma_start(out=imp_d[:], in_=impacc[:]).then_inc(p.c["dm"][0], 16)
        s.dma_start(out=n1_d[:], in_=n1acc[:]).then_inc(p.c["dm"][0], 16)
        p.c["dm"][1] += 32
        s.wait_ge(p.c["do0"][0], p.c["do0"][1])
        s.wait_ge(p.c["dg"][0], p.c["dg"][1])
        s.wait_ge(p.c["do1"][0], p.c["do1"][1])
        s.wait_ge(p.c["dm"][0], p.c["dm"][1])
        if debug:
            s.wait_ge(p.c["dbgs"][0], p.c["dbgs"][1])
    block.sync(s_tail)

    blk.__exit__(None, None, None)
    for g in reversed(ctx):
        g.__exit__(None, None, None)
    for g in p.ctxs:
        g.__exit__(None, None, None)
    return nc


def l1_inputs(x_core, inp):
    A = lambda k: np.ascontiguousarray(np.asarray(inp["L0_attn_" + k], dtype=np.float32))
    vecs = np.stack([A(k).reshape(-1) for k in
                     ("x_r", "x_k", "x_v", "x_w", "x_a", "x_g", "w0", "a0",
                      "k_k", "k_a", "r_k", "ln_gamma", "ln_beta")], axis=0)
    t = np.arange(128)[:, None]
    s = np.arange(128)[None, :]
    msl = (s < t).astype(np.uint8)
    masks = np.concatenate([msl, msl.T, (s <= t).astype(np.uint8).T], axis=1)
    return {
        "x": np.ascontiguousarray(x_core, dtype=np.float32),
        "Wr": A("Wr"), "Wk": A("Wk"), "Wv": A("Wv"), "Wo": A("Wo"),
        "w1": A("w1"), "w2": A("w2"), "a1": A("a1"), "a2": A("a2"),
        "g1": A("g1"), "g2": A("g2"),
        "vecs": np.ascontiguousarray(vecs),
        "identf": np.eye(128, dtype=np.float32),
        "masks": np.ascontiguousarray(masks),
    }







SLOT_FAM = [(0, 0), (1, 0), (2, 0), (0, 1), (1, 1), (2, 1), (0, 2), (1, 2)]
FAMW = {0: ((C, 4096), (4096, C)), 1: ((C, 2048), (2048, 2048), (2048, C)),
        2: ((C, 3072), (3072, C))}


def build_l2(T, n_all, kcap=1408, nbisect=36):
    TOK = B_CORE * T
    TILES = TOK // 128
    NTA = n_all // 128
    GT = kcap // 128 + 1
    NG = GT - 1
    assert (GT * 128) % 512 == 0
    TCH = GT * 128 // 512
    pos = 0.9 * (n_all - 1)
    k_lo = int(pos) + 1
    frac = float(np.float32(pos - int(pos)))
    BIG = float(2 ** 26)
    BIGI = 2 ** 26
    HB = TILES // 2

    nc = bass.Bass("TRN2", target_bir_lowering=False, detect_race_conditions=False)
    x1_d = nc.dram_tensor("x1", [TOK, C], F32, kind="ExternalInput").ap()
    xm_d = nc.dram_tensor("xm16", [TOK, C], BF16, kind="ExternalInput").ap()
    impo_d = nc.dram_tensor("imp_own", [128, TILES], F32, kind="ExternalInput").ap()
    n1o_d = nc.dram_tensor("n1_own", [128, TILES], F32, kind="ExternalInput").ap()
    impa_d = nc.dram_tensor("imp_all", [128, NTA], F32, kind="ExternalInput").ap()
    noise_d = nc.dram_tensor("noise", [128, TILES * 8], F32, kind="ExternalInput").ap()
    eu_d = nc.dram_tensor("eu", [1, 8], F32, kind="ExternalInput").ap()
    gw1_d = nc.dram_tensor("gw1", [C, C], F32, kind="ExternalInput").ap()
    gw2_d = nc.dram_tensor("gw2", [C, 512], F32, kind="ExternalInput").ap()
    gw3_d = nc.dram_tensor("gw3", [512, 8], F32, kind="ExternalInput").ap()
    aw_d = nc.dram_tensor("aw", [C, 8], F32, kind="ExternalInput").ap()
    gvec_d = nc.dram_tensor("gvec", [4, C], F32, kind="ExternalInput").ap()
    gsml_d = nc.dram_tensor("gsml", [2, 512], F32, kind="ExternalInput").ap()
    suo_d = nc.dram_tensor("suo", [128, 128], F32, kind="ExternalInput").ap()
    identf_d = nc.dram_tensor("identf", [128, 128], F32, kind="ExternalInput").ap()
    ew_d = {}
    eb_d = {}
    for f in (0, 1, 2):
        nexp = 3 if f < 2 else 2
        for li, sh in enumerate(FAMW[f]):
            ew_d[(f, li)] = nc.dram_tensor(f"e{f}w{li + 1}", [nexp] + list(sh), F32,
                                           kind="ExternalInput").ap()
            eb_d[(f, li)] = nc.dram_tensor(f"e{f}b{li + 1}", [nexp, sh[1]], F32,
                                           kind="ExternalInput").ap()

    y_d = nc.dram_tensor("y", [TOK, C], F32, kind="ExternalOutput").ap()
    xs_dram = nc.dram_tensor("xsd", [TOK, C + 8], BF16, kind="Internal").ap()
    comb_dram = nc.dram_tensor("combd", [TOK, C], F32, kind="Internal").ap()

    ctx = []

    def sb(name, shape, dtype=F32):
        g = nc.sbuf_tensor(name, shape, dtype)
        t = g.__enter__()
        ctx.append(g)
        return t

    identf = sb("identf_s", [128, 128], F32)
    onesf = sb("onesf", [128, 128], F32)
    cpool = sb("cpool", [128, 8], F32)
    msku = sb("msku", [128, TILES], U8)
    probs = sb("probs", [128, TILES, 8], F32)
    prg = sb("prg", [128, GT, 8], F32)
    idxg = sb("idxg", [128, NG], I32)
    XgT = sb("XgT", [128, 8, GT * 128], BF16)
    comb = sb("comb", [128, GT, C], F32)
    Cbf = sb("Cbf", [8, C], BF16)
    prT = sb("prT", [8, 128], BF16)
    st = sb("st", [128, 16], F32)
    eurow = sb("eurow", [1, 16], F32)

    s2 = []

    def sb2(name, shape, dtype=F32):
        g = nc.sbuf_tensor(name, shape, dtype)
        t = g.__enter__()
        s2.append(g)
        return t
    zer_f = sb2("zer_f", [128, 128], F32)
    suo = sb2("suo_s", [128, 128], F32)
    impa = sb2("impa", [128, NTA], F32)
    tmpa = sb2("tmpa", [128, NTA], F32)
    mska = sb2("mska", [128, NTA], U8)
    bis = sb2("bis", [128, 24], F32)
    impt = sb2("impt", [128, TILES], F32)
    n1t = sb2("n1t", [128, TILES], F32)
    mskf = sb2("mskf", [128, TILES], F32)
    scl = sb2("scl", [128, TILES], F32)
    rnk = sb2("rnk", [128, TILES], F32)
    tt2b = sb2("tt2b", [128, TILES], F32)
    srbf = sb2("srbf", [128, kcap], F32)
    iotari = sb2("iotari", [128, kcap], I32)
    eqc = sb2("eqc", [128, 2, kcap], BF16)
    mov = sb2("mov", [128, 2 * TILES], BF16)
    movi = sb2("movi", [128, 2 * TILES], I32)
    sloti = sb2("sloti", [128, NG], I32)
    slotf = sb2("slotf", [128, NG], F32)
    idxacc = sb2("idxacc", [128, 2 * NG], F32)
    idxf = sb2("idxf", [128, NG], F32)
    af_acc = sb2("af_acc", [128, 16], F32)
    mskg = sb2("mskg", [128, NG], F32)
    bmf = sb2("bmf", [128, NG], F32)
    affd = sb2("affd", [128, 8], F32)
    afS = sb2("afS", [128, NG, 8], F32)
    nslot = sb2("nslot", [128, NG, 8], F32)
    lgs = sb2("lgs", [128, NG, 8], F32)
    lg2 = sb2("lg2", [128, NG, 8], F32)
    lga = sb2("lga", [128, TILES, 8], F32)
    lgb = sb2("lgb", [128, TILES, 8], F32)
    m1t = sb2("m1t", [128, TILES], F32)
    m2t = sb2("m2t", [128, TILES], F32)
    c1t = sb2("c1t", [128, TILES], F32)
    logc = sb2("logc", [128, 8], F32)
    affs = sb2("affs", [128, 16], F32)
    noi = sb2("noi", [128, TILES * 8], F32)

    s1 = []

    def sb1(name, shape, dtype=F32):
        g = nc.sbuf_tensor(name, shape, dtype)
        t = g.__enter__()
        s1.append(g)
        return t
    gb1r = sb1("gb1r", [1, C], F32)
    lnsr = sb1("lnsr", [1, C], F32)
    lnor = sb1("lnor", [1, C], F32)
    gb2r = sb1("gb2r", [1, 512], F32)
    gsmr = sb1("gsmr", [1, 16], F32)
    gw1s = sb1("gw1s", [128, 8, C], BF16)
    gw2s = sb1("gw2s", [128, 8, 512], BF16)
    gw3s = sb1("gw3s", [128, 4, 8], BF16)
    aws = sb1("aws", [128, 8, 8], BF16)
    gb1b = sb1("gb1b", [128, C], BF16)
    lnsb = sb1("lnsb", [128, C], BF16)
    lnob = sb1("lnob", [128, C], BF16)
    gb2b = sb1("gb2b", [128, 512], BF16)
    gb3b = sb1("gb3b", [128, 8], F32)
    abb = sb1("abb", [128, 8], F32)
    xgbf = sb1("xgbf", [128, C + 8], BF16)
    xtmp = sb1("xtmp", [128, C], F32)
    htmp = sb1("htmp", [128, C], F32)
    h2tmp = sb1("h2tmp", [128, 512], F32)
    sqs = sb1("sqs", [128, C], F32)
    hnT = sb1("hnT", [128, 8, 128], BF16)
    h2T = sb1("h2T", [128, 4, 128], BF16)
    xmtb = sb1("xmtb", [128, 2, C], BF16)
    xstb = sb1("xstb", [128, 2, C + 8], BF16)
    svb = sb1("svb", [128, 2, 2], BF16)
    ameanb = sb1("ameanb", [128, 16], BF16)

    pb = []
    for i in range(8):
        g = nc.psum_tensor(f"pb{i}", [128, 512], F32)
        pb.append(g.__enter__())
        ctx.append(g)

    blk = nc.Block()
    block = blk.__enter__()
    p = P(nc, ["dh", "dg", "do", "sv", "ss", "tt", "gi", "gp"])
    mile = {}

    def i_sync(s):
        s.dma_start(out=identf[:], in_=identf_d[:]).then_inc(p.c["dh"][0], 16)
        s.dma_start(out=suo[:], in_=suo_d[:]).then_inc(p.c["dh"][0], 16)
        s.dma_start(out=impa[:], in_=impa_d[:]).then_inc(p.c["dh"][0], 16)
        s.dma_start(out=impt[:], in_=impo_d[:]).then_inc(p.c["dh"][0], 16)
        s.dma_start(out=n1t[:], in_=n1o_d[:]).then_inc(p.c["dh"][0], 16)
        s.dma_start(out=noi[:], in_=noise_d[:]).then_inc(p.c["dh"][0], 16)
        s.dma_start(out=eurow[0:1, 0:8], in_=eu_d[:]).then_inc(p.c["dh"][0], 16)
        s.dma_start(out=gb1r[:], in_=gvec_d[0:1, :]).then_inc(p.c["dh"][0], 16)
        s.dma_start(out=lnsr[:], in_=gvec_d[1:2, :]).then_inc(p.c["dh"][0], 16)
        s.dma_start(out=lnor[:], in_=gvec_d[2:3, :]).then_inc(p.c["dh"][0], 16)
        s.dma_start(out=gb2r[:], in_=gsml_d[0:1, :]).then_inc(p.c["dh"][0], 16)
        s.dma_start(out=gsmr[:], in_=gsml_d[1:2, 0:16]).then_inc(p.c["dh"][0], 16)
        p.c["dh"][1] += 12 * 16
    block.sync(i_sync)

    def i_gp(g):
        g.dma_start(out=gw1s[:], in_=gw1_d.rearrange("(k q) n -> q k n", q=128)).then_inc(p.c["dg"][0], 16)
        g.dma_start(out=gw2s[:], in_=gw2_d.rearrange("(k q) n -> q k n", q=128)).then_inc(p.c["dg"][0], 16)
        g.dma_start(out=gw3s[:], in_=gw3_d.rearrange("(k q) n -> q k n", q=128)).then_inc(p.c["dg"][0], 16)
        g.dma_start(out=aws[:], in_=aw_d.rearrange("(k q) n -> q k n", q=128)).then_inc(p.c["dg"][0], 16)
        p.c["dg"][1] += 4 * 16
    block.gpsimd(i_gp)
    DG_W = p.val("dg")

    def i_gp2(g):
        g.iota(iotari[:], pattern=[[1, kcap]], base=0, channel_multiplier=0)
        g.iota(movi[:, 0:TILES], pattern=[[0, TILES]], base=0, channel_multiplier=1)
        g.iota(movi[:, TILES:2 * TILES], pattern=[[128, TILES]], base=0, channel_multiplier=0)
        g.iota(sloti[:], pattern=[[128, NG]], base=0, channel_multiplier=1)
        p.inc(g.drain(), "gp")
    block.gpsimd(i_gp2)
    GI_IOTA = p.val("gp")

    def i_v2(v):
        p.wait(v, "gp", GI_IOTA)
        v.tensor_copy(srbf[:], iotari[:])
        v.drain()
        mt = mov[:].rearrange("q (c t) -> q c t", t=2)
        v.tensor_copy(mt[:, :, 0:1], movi[:, 0:TILES].rearrange("q (c o) -> q c o", o=1))
        v.drain()
        v.tensor_copy(mt[:, :, 1:2], movi[:, TILES:2 * TILES].rearrange("q (c o) -> q c o", o=1))
        v.drain()
        v.tensor_copy(slotf[:], sloti[:])
        p.inc(v.drain(), "sv")
    block.vector(i_v2)

    def i_vec(v):
        for col, val in ((0, 1e-5), (1, 1e-8), (2, 1.0), (3, BIG), (4, 1e-12)):
            v.memset(cpool[:, col:col + 1], val)
        v.memset(onesf[:], 1.0)
        v.memset(zer_f[:], 0.0)
        v.memset(XgT[:, :, 0:128], 0.0)
        v.memset(prg[:].rearrange("q a e -> q (a e)"), 0.0)
        v.memset(comb[:, 0, :], 0.0)
        v.memset(bis[:, 0:2], 0.0)
        v.memset(bis[:, 2:4], 1024.0)
        v.memset(idxg[:], BIGI)
        p.inc(v.drain(), "sv")
    block.vector(i_vec)
    SV0 = p.val("sv")

    bc_list = [(gb1r, gb1b, 1024), (lnsr, lnsb, 1024), (lnor, lnob, 1024),
               (gb2r, gb2b, 512)]
    for bi, (srcr, dst, width) in enumerate(bc_list):
        def tsec(t, srcr=srcr, width=width, bi=bi):
            if bi == 0:
                p.wait(t, "dh")
                p.wait(t, "sv", SV0)
            else:
                p.wait(t, "ss")
            m = t.matmul(pb[0][:], onesf[0:1, :], srcr[0:1, 0:512], start=True, stop=True)
            p.inc(m, "tt")
            if width > 512:
                m = t.matmul(pb[1][:], onesf[0:1, :], srcr[0:1, 512:1024], start=True, stop=True)
                p.inc(m, "tt")
        block.tensor(tsec)

        def ssec(sc, dst=dst, width=width):
            p.wait(sc, "tt")
            sc.copy(dst[:, 0:512], pb[0][:])
            if width > 512:
                sc.copy(dst[:, 512:1024], pb[1][:])
            p.inc(sc.drain(), "ss")
        block.scalar(ssec)

    def t_small(t):
        p.wait(t, "ss")
        m = t.matmul(pb[0][:, 0:8], onesf[0:1, :], gsmr[0:1, 0:8], start=True, stop=True)
        p.inc(m, "tt")
        m = t.matmul(pb[1][:, 0:8], onesf[0:1, :], gsmr[0:1, 8:16], start=True, stop=True)
        p.inc(m, "tt")
    block.tensor(t_small)

    def s_small(sc):
        p.wait(sc, "tt")
        sc.copy(gb3b[:], pb[0][:, 0:8])
        sc.copy(abb[:], pb[1][:, 0:8])
        p.inc(sc.drain(), "ss")
    block.scalar(s_small)
    SS_BC = p.val("ss")

    def zf_s(s):
        p.wait(s, "sv", SV0)
        for a in range(TILES):
            s.dma_start(out=comb_dram[a * 128:(a + 1) * 128, :], in_=comb[:, 0, :]).then_inc(p.c["dh"][0], 16)
            p.c["dh"][1] += 16
    block.sync(zf_s)
    DH_CZ = p.val("dh")

    def s_ns(sc):
        p.wait(sc, "dh")
        sc.activation(eurow[0:1, 8:16], eurow[0:1, 0:8], AF.Ln, bias=cpool[0:1, 1:2])
        p.inc(sc.drain(), "ss")
    block.scalar(s_ns)

    def v_ns(v):
        p.wait(v, "ss")
        v.tensor_mul(eurow[0:1, 8:16], eurow[0:1, 8:16], eurow[0:1, 0:8])
        v.drain()
        v.reduce_sum(eurow[0:1, 0:1], eurow[0:1, 8:16], axis=AX.X, negate=True)
        v.drain()
        v.tensor_scalar(eurow[0:1, 1:2], eurow[0:1, 0:1], -0.01, 0.02, ALU.mult, ALU.add)
        v.drain()
        v.tensor_scalar_max(eurow[0:1, 1:2], eurow[0:1, 1:2], 1e-4)
        p.inc(v.drain(), "sv")
    block.vector(v_ns)
    SV_NS = p.val("sv")

    def t_ns(t):
        p.wait(t, "sv", SV_NS)
        m = t.matmul(pb[2][:, 0:1], onesf[0:1, :], eurow[0:1, 1:2], start=True, stop=True)
        p.inc(m, "tt")
    block.tensor(t_ns)
    TT_NS = p.val("tt")

    def v_ns2(v):
        p.wait(v, "tt", TT_NS)
        v.tensor_copy(bis[:, 12:13], pb[2][:, 0:1])
        p.inc(v.drain(), "sv")
    block.vector(v_ns2)

    ranks = (float(k_lo), float(k_lo + 1))
    for it in range(nbisect):
        def b_v(v, it=it):
            v.tensor_add(bis[:, 4:6], bis[:, 0:2], bis[:, 2:4])
            v.drain()
            v.tensor_scalar_mul(bis[:, 4:6], bis[:, 4:6], 0.5)
            v.drain()
            for kk in range(2):
                v.tensor_scalar(tmpa[:], impa[:], bis[:, 4 + kk:5 + kk], None, ALU.is_le)
                v.drain()
                v.reduce_sum(st[:, kk:kk + 1], tmpa[:], axis=AX.X)
                v.drain()
            p.inc(v.nop(), "sv")
        block.vector(b_v)
        SVB = p.val("sv")

        def b_t(t, it=it):
            p.wait(t, "sv", SVB)
            m = t.matmul(pb[4][:, 0:2], onesf[:], st[:, 0:2], start=True, stop=True)
            p.inc(m, "tt")
        block.tensor(b_t)
        TTB = p.val("tt")

        def b_u(v, it=it):
            p.wait(v, "tt", TTB)
            v.tensor_copy(bis[:, 6:8], pb[4][:, 0:2])
            v.drain()
            v.tensor_scalar(bis[:, 8:9], bis[:, 6:7], ranks[0], None, ALU.is_ge)
            v.tensor_scalar(bis[:, 9:10], bis[:, 7:8], ranks[1], None, ALU.is_ge)
            v.drain()
            v.tensor_sub(bis[:, 16:18], bis[:, 4:6], bis[:, 2:4])
            v.drain()
            v.tensor_mul(bis[:, 16:18], bis[:, 16:18], bis[:, 8:10])
            v.drain()
            v.tensor_add(bis[:, 2:4], bis[:, 2:4], bis[:, 16:18])
            v.tensor_sub(bis[:, 18:20], bis[:, 4:6], bis[:, 0:2])
            v.tensor_scalar(bis[:, 8:10], bis[:, 8:10], -1.0, 1.0, ALU.mult, ALU.add)
            v.drain()
            v.tensor_mul(bis[:, 18:20], bis[:, 18:20], bis[:, 8:10])
            v.drain()
            v.tensor_add(bis[:, 0:2], bis[:, 0:2], bis[:, 18:20])
            p.inc(v.drain(), "sv")
        block.vector(b_u)

    for kk in range(2):
        def vm_v(v, kk=kk):
            v.tensor_scalar(mska[:], impa[:], bis[:, kk:kk + 1], None, ALU.is_gt)
            v.drain()
            v.select(tmpa[:], mska[:], impa[:],
                     cpool[:, 3:4].broadcast_to((128, NTA)))
            v.drain()
            v.tensor_reduce(st[:, 4 + kk:5 + kk], tmpa[:], axis=AX.X, op=ALU.min)
            p.inc(v.drain(), "sv")
        block.vector(vm_v)
        SVV = p.val("sv")

        def vm_t(t, kk=kk):
            p.wait(t, "sv", SVV)
            i1 = t.transpose(pb[5][0:1, 0:128], st[:, 4 + kk:5 + kk], identf[:])
            p.inc(i1, "tt")
        block.tensor(vm_t)
        TTV = p.val("tt")

        def vm_v2(v, kk=kk):
            p.wait(v, "tt", TTV)
            v.tensor_reduce(st[0:1, 6 + kk:7 + kk], pb[5][0:1, 0:128], axis=AX.X, op=ALU.min)
            p.inc(v.drain(), "sv")
        block.vector(vm_v2)
        SVV2 = p.val("sv")

        def vm_t2(t, kk=kk):
            p.wait(t, "sv", SVV2)
            m = t.matmul(pb[5][:, 4 + kk:5 + kk], onesf[0:1, :], st[0:1, 6 + kk:7 + kk],
                         start=True, stop=True)
            p.inc(m, "tt")
        block.tensor(vm_t2)
        TTV2 = p.val("tt")

        def vm_v3(v, kk=kk):
            p.wait(v, "tt", TTV2)
            v.tensor_copy(bis[:, 14 + kk:15 + kk], pb[5][:, 4 + kk:5 + kk])
            p.inc(v.drain(), "sv")
        block.vector(vm_v3)

    def std_v(v):
        v.reduce_sum(st[:, 8:9], impa[:], axis=AX.X)
        p.inc(v.drain(), "sv")
    block.vector(std_v)
    SVS = p.val("sv")

    def std_t(t):
        p.wait(t, "sv", SVS)
        m = t.matmul(pb[5][:, 8:9], onesf[:], st[:, 8:9], start=True, stop=True)
        p.inc(m, "tt")
    block.tensor(std_t)
    TTS = p.val("tt")

    def std_v2(v):
        p.wait(v, "tt", TTS)
        v.tensor_scalar_mul(bis[:, 13:14], pb[5][:, 8:9], 1.0 / n_all)
        p.inc(v.drain(), "sv")
    block.vector(std_v2)

    def std_s(sc):
        p.wait(sc, "sv")
        sc.activation(tmpa[:], impa[:], AF.Square, bias=bis[:, 13:14], scale=-1.0,
                      accum_out=st[:, 9:10])
        sc.activation(st[:, 10:11], st[:, 9:10], AF.Copy, scale=1.0 / n_all)
        p.inc(sc.drain(), "ss")
    block.scalar(std_s)
    SSS = p.val("ss")

    def std_t2(t):
        p.wait(t, "ss", SSS)
        m = t.matmul(pb[5][:, 10:11], onesf[:], st[:, 10:11], start=True, stop=True)
        p.inc(m, "tt")
    block.tensor(std_t2)
    TTS2 = p.val("tt")

    def std_s2(sc):
        p.wait(sc, "tt", TTS2)
        sc.activation(bis[:, 11:12], pb[5][:, 10:11], AF.Sqrt)
        p.inc(sc.drain(), "ss")
    block.scalar(std_s2)

    def thr_v(v):
        p.wait(v, "ss")
        v.tensor_scalar(bis[:, 11:12], bis[:, 11:12], 0.1, 1.0, ALU.mult, ALU.add)
        v.tensor_sub(bis[:, 20:21], bis[:, 15:16], bis[:, 14:15])
        v.drain()
        v.tensor_scalar_mul(bis[:, 20:21], bis[:, 20:21], frac)
        v.drain()
        v.tensor_add(bis[:, 10:11], bis[:, 14:15], bis[:, 20:21])
        v.drain()
        v.tensor_scalar(msku[:], impt[:], bis[:, 10:11], None, ALU.is_gt)
        v.drain()
        v.tensor_copy(mskf[:], msku[:])
        v.drain()
        v.tensor_scalar_add(scl[:], n1t[:], 1e-8)
        v.drain()
        v.tensor_scalar_mul(scl[:], scl[:], bis[:, 11:12])
        v.drain()
        v.reciprocal(tt2b[:], scl[:])
        v.drain()
        v.tensor_mul(scl[:], tt2b[:], mskf[:])
        v.reduce_sum(st[:, 12:13], mskf[:], axis=AX.X)
        p.inc(v.drain(), "sv")
    block.vector(thr_v)
    SVT = p.val("sv")

    def pf_t(t):
        p.wait(t, "sv", SVT)
        m = t.matmul(pb[6][:, 0:1], suo[:], st[:, 12:13], start=True, stop=True)
        p.inc(m, "tt")
        m = t.matmul(pb[6][:, 4:5], onesf[:], st[:, 12:13], start=True, stop=True)
        p.inc(m, "tt")
    block.tensor(pf_t)
    TTP = p.val("tt")

    def pf_v(v):
        p.wait(v, "tt", TTP)
        v.tensor_copy(st[:, 13:14], pb[6][:, 0:1])
        v.tensor_copy(st[:, 14:15], pb[6][:, 4:5])
        v.drain()
        v.tensor_tensor_scan(rnk[:], mskf[:], zer_f[:, 0:TILES], 0.0, ALU.add, ALU.bypass)
        v.drain()
        v.tensor_sub(rnk[:], rnk[:], mskf[:])
        v.drain()
        v.tensor_scalar_add(rnk[:], rnk[:], st[:, 13:14])
        v.drain()
        v.tensor_scalar(tt2b[:], mskf[:], -BIG, BIG, ALU.mult, ALU.add)
        v.drain()
        v.tensor_add(tt2b[:], tt2b[:], rnk[:])
        p.inc(v.drain(), "sv")
    block.vector(pf_v)
    SVR = p.val("sv")

    for c in range(TILES):
        def eq_v(v, c=c):
            if c >= 2:
                p.wait(v, "tt", mile[("eqt", c - 2)])
            v.tensor_scalar(eqc[:, c % 2, :], srbf[:], tt2b[:, c:c + 1], None, ALU.is_equal)
            p.inc(v.drain(), "sv")
        block.vector(eq_v)
        SVE = p.val("sv")

        def eq_t(t, c=c, SVE=SVE):
            p.wait(t, "sv", SVE)
            if c >= 2:
                p.wait(t, "sv", mile[("eqa", c - 2)])
            m = None
            for gg in range(NG):
                m = t.matmul(pb[5 + c % 2][:, 2 * gg:2 * gg + 2], eqc[:, c % 2, gg * 128:(gg + 1) * 128],
                             mov[:, 2 * c:2 * c + 2], start=True, stop=True)
                p.inc(m, "tt")
            mile[("eqt", c)] = p.val("tt")
        block.tensor(eq_t)
        TT_EQ1 = p.val("tt")

        def eq_a(v, c=c, TT_EQ1=TT_EQ1):
            p.wait(v, "tt", TT_EQ1)
            if c == 0:
                v.tensor_copy(idxacc[:], pb[5][:, 0:2 * NG])
            else:
                v.tensor_add(idxacc[:], idxacc[:], pb[5 + c % 2][:, 0:2 * NG])
            v.drain()
            mile[("eqa", c)] = p.inc(v.nop(), "sv")
        block.vector(eq_a)
    TT_EQ = p.val("tt")

    def ix_v(v):
        pbv = idxacc[:].rearrange("q (g t) -> q t g", t=2)
        v.tensor_tensor(idxf[:], pbv[:, 0, :], pbv[:, 1, :], ALU.add)
        v.drain()
        v.tensor_scalar(mskg[:], slotf[:], st[:, 14:15], None, ALU.is_ge)
        v.drain()
        v.scalar_tensor_tensor(idxf[:], mskg[:], BIG, idxf[:], ALU.mult, ALU.add)
        v.drain()
        v.tensor_copy(idxg[:], idxf[:])
        v.drain()
        v.tensor_scalar(bmf[:], idxf[:], float(TOK // 2), None, ALU.is_ge)
        p.inc(v.drain(), "sv")
    block.vector(ix_v)
    SV_IDX = p.val("sv")

    for i in range(TILES):
        par = i % 2

        def af_s(s, i=i, par=par):
            if i >= 2:
                p.wait(s, "tt", mile[("afmm", i - 2)])
            s.dma_start(out=xmtb[:, par, :], in_=xm_d[i * 128:(i + 1) * 128, :]).then_inc(p.c["dh"][0], 16)
            p.c["dh"][1] += 16
        block.sync(af_s)
        DHA = p.val("dh")

        def af_v(v, i=i, par=par, DHA=DHA):
            p.wait(v, "dh", DHA)
            ba = 0 if i < HB else 1
            v.memset(svb[:, par, :], 0.0)
            v.drain()
            v.memset(svb[:, par, ba:ba + 1], 1.0)
            v.drain()
            v.tensor_scalar_mul(xstb[:, par, 0:C], xmtb[:, par, :], scl[:, i:i + 1])
            v.drain()
            v.tensor_copy(xstb[:, par, C:C + 8], noi[:, 8 * i:8 * i + 8])
            p.inc(v.drain(), "sv")
        block.vector(af_v)
        SVA = p.val("sv")

        def af_t(t, i=i, par=par, SVA=SVA):
            p.wait(t, "sv", SVA)
            if i >= 2:
                p.wait(t, "sv", mile[("afacc", i - 2)])
            m = None
            for c in range(8):
                m = t.matmul(pb[7][:, 16 * par + 2 * c:16 * par + 2 * c + 2],
                             xstb[:, par, c * 128:(c + 1) * 128],
                             svb[:, par, :], start=True, stop=True)
                p.inc(m, "tt")
            mile[("afmm", i)] = p.val("tt")
        block.tensor(af_t)
        TTAF1 = p.val("tt")

        def af_a(v, i=i, par=par, TTAF1=TTAF1):
            p.wait(v, "tt", TTAF1)
            if i == 0:
                v.tensor_copy(af_acc[:], pb[7][:, 0:16])
            else:
                v.tensor_add(af_acc[:], af_acc[:], pb[7][:, 16 * par:16 * par + 16])
            v.drain()
            mile[("afacc", i)] = p.inc(v.nop(), "sv")
        block.vector(af_a)

        def af_sw(s, i=i, par=par, SVA=SVA):
            p.wait(s, "sv", SVA)
            s.dma_start(out=xs_dram[i * 128:(i + 1) * 128, :], in_=xstb[:, par, :]).then_inc(p.c["dh"][0], 16)
            p.c["dh"][1] += 16
        block.sync(af_sw)
    TT_AF = p.val("tt")
    DH_XS = p.val("dh")

    def af_e(v):
        v.tensor_scalar_mul(ameanb[:], af_acc[:], 1.0 / T)
        p.inc(v.drain(), "sv")
    block.vector(af_e)
    SV_AM = p.val("sv")

    def af_t2(t):
        p.wait(t, "sv", SV_AM)
        m0 = m1 = None
        for c in range(8):
            m0 = t.matmul(pb[6][0:1, 8:16], ameanb[:, 2 * c:2 * c + 1], aws[:, c, :],
                          start=(c == 0), stop=(c == 7))
            m1 = t.matmul(pb[5][0:1, 0:8], ameanb[:, 2 * c + 1:2 * c + 2], aws[:, c, :],
                          start=(c == 0), stop=(c == 7))
        p.inc(m0, "tt")
        p.inc(m1, "tt")
    block.tensor(af_t2)
    TT_AF2 = p.val("tt")

    def af_e2(v):
        p.wait(v, "tt", TT_AF2)
        v.tensor_copy(st[0:1, 0:8], pb[6][0:1, 8:16])
        v.tensor_copy(st[0:1, 8:16], pb[5][0:1, 0:8])
        p.inc(v.drain(), "sv")
    block.vector(af_e2)
    SV_AF2 = p.val("sv")

    def af_t3(t):
        p.wait(t, "sv", SV_AF2)
        m = t.matmul(pb[6][:, 0:16], onesf[0:1, :], st[0:1, 0:16], start=True, stop=True)
        p.inc(m, "tt")
    block.tensor(af_t3)
    TT_AF3 = p.val("tt")

    def af_e3(v):
        p.wait(v, "tt", TT_AF3)
        v.tensor_add(affs[:, 0:8], pb[6][:, 0:8], abb[:])
        v.tensor_add(affs[:, 8:16], pb[6][:, 8:16], abb[:])
        v.drain()
        v.tensor_scalar_mul(affs[:], affs[:], 0.3)
        p.inc(v.drain(), "sv")
    block.vector(af_e3)

    for gti in range(NG):
        def xg_g(g, gti=gti, DH_XS=DH_XS):
            if gti == 0:
                p.wait(g, "sv", SV_IDX)
                p.wait(g, "dh", DH_XS)
            else:
                p.wait(g, "sv", mile[("xgv", gti - 1)])
            g.memset(xgbf[:], 0.0)
            g.indirect_dma_start(
                out=xgbf[:], out_offset=None,
                in_=xs_dram[:], in_offset=bass.IndirectOffsetOnAxis(ap=idxg[:, gti:gti + 1], axis=0),
                bounds_check=TOK - 1, oob_is_err=False,
            ).then_inc(p.c["gi"][0], 16)
            p.c["gi"][1] += 16
            g.wait_ge(p.c["gi"][0], p.c["gi"][1])
        block.gpsimd(xg_g)
        GI_XG = p.val("gi")

        def xg_v(v, gti=gti):
            p.wait(v, "gi", GI_XG)
            v.tensor_copy(xtmp[:], xgbf[:, 0:C])
            v.drain()
            v.tensor_copy(nslot[:, gti, :], xgbf[:, C:C + 8])
            v.drain()
            v.scalar_tensor_tensor(comb[:, gti + 1, :], xtmp[:], 0.1, zer_f[:, 0:1].broadcast_to((128, C)), ALU.mult, ALU.add)
            mile[("xgv", gti)] = p.inc(v.drain(), "sv")
        block.vector(xg_v)
        SVXG = p.val("sv")
        svend = p.val("sv") + 8

        def xg_t(t, gti=gti, SVXG=SVXG, svend=svend):
            p.wait(t, "sv", SVXG)
            for c in range(8):
                if c >= 2:
                    p.wait(t, "sv", svend - 8 + c - 1)
                i1 = t.transpose(pb[2 + (c % 2)][:, 0:128], xtmp[:, c * 128:(c + 1) * 128], identf[:])
                p.inc(i1, "tt")
        block.tensor(xg_t)
        TTXG = p.val("tt")

        for c in range(8):
            def xg_e(v, c=c, gti=gti, TTXG=TTXG):
                p.wait(v, "tt", TTXG - 8 + c + 1)
                i1 = v.tensor_copy(XgT[:, c, (gti + 1) * 128:(gti + 2) * 128], pb[2 + (c % 2)][:, 0:128])
                v.drain()
                p.inc(v.nop(), "sv")
            block.vector(xg_e)
        assert p.val("sv") == svend
    SV_XGT = p.val("sv")

    for gti in range(GT):
        def gp_t1(t, gti=gti):
            if gti == 0:
                p.wait(t, "sv")
                p.wait(t, "dg", DG_W)
                p.wait(t, "ss", SS_BC)
            else:
                p.wait(t, "sv", mile[("gpfree", gti - 1)])
            m0 = m1 = None
            for c in range(8):
                m0 = t.matmul(pb[0][:], XgT[:, c, gti * 128:(gti + 1) * 128], gw1s[:, c, 0:512],
                              start=(c == 0), stop=(c == 7))
                m1 = t.matmul(pb[1][:], XgT[:, c, gti * 128:(gti + 1) * 128], gw1s[:, c, 512:1024],
                              start=(c == 0), stop=(c == 7))
            p.inc(m0, "tt")
            p.inc(m1, "tt")
        block.tensor(gp_t1)
        TTG1 = p.val("tt")

        def gp_v1(v, gti=gti):
            p.wait(v, "tt", TTG1)
            v.tensor_add(htmp[:, 0:512], pb[0][:], gb1b[:, 0:512])
            v.tensor_add(htmp[:, 512:1024], pb[1][:], gb1b[:, 512:1024])
            p.inc(v.drain(), "sv")
        block.vector(gp_v1)
        SVG1 = p.val("sv")

        def gp_s1(sc, gti=gti):
            p.wait(sc, "sv", SVG1)
            sc.activation(xtmp[:], htmp[:], AF.Silu)
            sc.drain()
            sc.activation(sqs[:], xtmp[:], AF.Square, accum_out=st[:, 4:5])
            sc.activation(st[:, 5:6], st[:, 4:5], AF.Copy, scale=INV_C)
            p.inc(sc.drain(), "ss")
        block.scalar(gp_s1)

        def gp_v2(v):
            p.wait(v, "ss")
            v.reduce_sum(st[:, 0:1], xtmp[:], axis=AX.X)
            v.drain()
            v.tensor_scalar_mul(st[:, 2:3], st[:, 0:1], INV_C)
            v.drain()
            v.tensor_mul(st[:, 1:2], st[:, 2:3], st[:, 2:3])
            v.drain()
            v.tensor_sub(st[:, 3:4], st[:, 5:6], st[:, 1:2])
            p.inc(v.drain(), "sv")
        block.vector(gp_v2)

        def gp_s2(sc):
            p.wait(sc, "sv")
            sc.activation(st[:, 7:8], st[:, 3:4], AF.Ln, bias=cpool[:, 0:1])
            sc.drain()
            sc.activation(st[:, 6:7], st[:, 7:8], AF.Exp, scale=-0.5)
            p.inc(sc.drain(), "ss")
        block.scalar(gp_s2)

        def gp_v3(v):
            p.wait(v, "ss")
            v.tensor_scalar(htmp[:], xtmp[:], st[:, 2:3], st[:, 6:7], ALU.subtract, ALU.mult)
            v.drain()
            v.tensor_mul(htmp[:], htmp[:], lnsb[:])
            v.drain()
            v.tensor_add(htmp[:], htmp[:], lnob[:])
            p.inc(v.drain(), "sv")
        block.vector(gp_v3)
        SVG3 = p.val("sv")
        svend = p.val("sv") + 8

        def gp_t2(t, SVG3=SVG3, svend=svend):
            p.wait(t, "sv", SVG3)
            for c in range(8):
                if c >= 2:
                    p.wait(t, "sv", svend - 8 + c - 1)
                i1 = t.transpose(pb[2 + (c % 2)][:, 0:128], htmp[:, c * 128:(c + 1) * 128], identf[:])
                p.inc(i1, "tt")
        block.tensor(gp_t2)
        TTG2 = p.val("tt")

        for c in range(8):
            def gp_e(v, c=c, TTG2=TTG2):
                p.wait(v, "tt", TTG2 - 8 + c + 1)
                v.tensor_copy(hnT[:, c, :], pb[2 + (c % 2)][:, 0:128])
                v.drain()
                p.inc(v.nop(), "sv")
            block.vector(gp_e)
        assert p.val("sv") == svend

        def gp_t3(t):
            p.wait(t, "sv", svend)
            m = None
            for c in range(8):
                m = t.matmul(pb[4][:], hnT[:, c, :], gw2s[:, c, :], start=(c == 0), stop=(c == 7))
            p.inc(m, "tt")
        block.tensor(gp_t3)
        TTG3 = p.val("tt")

        def gp_v4(v, TTG3=TTG3):
            p.wait(v, "tt", TTG3)
            v.tensor_add(h2tmp[:], pb[4][:], gb2b[:])
            p.inc(v.drain(), "sv")
        block.vector(gp_v4)
        SVG4 = p.val("sv")

        def gp_s3(sc, SVG4=SVG4):
            p.wait(sc, "sv", SVG4)
            sc.activation(h2tmp[:], h2tmp[:], AF.Silu)
            p.inc(sc.drain(), "ss")
        block.scalar(gp_s3)
        SSG3 = p.val("ss")
        svend2 = p.val("sv") + 4

        def gp_t4(t, SSG3=SSG3, svend2=svend2):
            p.wait(t, "ss", SSG3)
            for c in range(4):
                if c >= 2:
                    p.wait(t, "sv", svend2 - 4 + c - 1)
                i1 = t.transpose(pb[2 + (c % 2)][:, 0:128], h2tmp[:, c * 128:(c + 1) * 128], identf[:])
                p.inc(i1, "tt")
        block.tensor(gp_t4)
        TTG4 = p.val("tt")

        for c in range(4):
            def gp_e2(v, c=c, TTG4=TTG4):
                p.wait(v, "tt", TTG4 - 4 + c + 1)
                v.tensor_copy(h2T[:, c, :], pb[2 + (c % 2)][:, 0:128])
                v.drain()
                p.inc(v.nop(), "sv")
            block.vector(gp_e2)
        assert p.val("sv") == svend2

        def gp_t5(t, svend2=svend2):
            p.wait(t, "sv", svend2)
            m = None
            for c in range(4):
                m = t.matmul(pb[5][:, 0:8], h2T[:, c, :], gw3s[:, c, :], start=(c == 0), stop=(c == 3))
            p.inc(m, "tt")
        block.tensor(gp_t5)
        TTG5 = p.val("tt")

        def gp_v5(v, gti=gti, TTG5=TTG5):
            p.wait(v, "tt", TTG5)
            if gti == 0:
                v.tensor_add(logc[:], pb[5][:, 0:8], gb3b[:])
            else:
                v.tensor_add(lgs[:, gti - 1, :], pb[5][:, 0:8], gb3b[:])
            v.drain()
            mile[("gpfree", gti)] = p.inc(v.nop(), "sv")
        block.vector(gp_v5)

    def sp_v(v):
        v.tensor_sub(affd[:], affs[:, 8:16], affs[:, 0:8])
        v.drain()
        for gg in range(NG):
            v.tensor_scalar_mul(afS[:, gg, :], affd[:], bmf[:, gg:gg + 1])
            v.drain()
            v.tensor_add(afS[:, gg, :], afS[:, gg, :], affs[:, 0:8])
            v.drain()
        v.tensor_tensor(lgs[:], lgs[:], afS[:], ALU.add)
        v.drain()
        v.scalar_tensor_tensor(lgs[:].rearrange("q g e -> q (g e)"),
                               nslot[:].rearrange("q g e -> q (g e)"), bis[:, 12:13],
                               lgs[:].rearrange("q g e -> q (g e)"), ALU.mult, ALU.add)
        v.drain()
        v.tensor_scalar_mul(lgs[:].rearrange("q g e -> q (g e)"),
                            lgs[:].rearrange("q g e -> q (g e)"), 1.0 / 1.5)
        v.drain()
        v.reduce_max(m1t[:, 0:NG], lgs[:], axis=AX.X)
        v.drain()
        m1b = m1t[:, 0:NG].rearrange("q (g o) -> q g o", o=1).broadcast_to((128, NG, 8))
        v.tensor_tensor(lg2[:], lgs[:], m1b, ALU.is_equal)
        v.drain()
        v.reduce_sum(c1t[:, 0:NG], lg2[:], axis=AX.X)
        v.drain()
        v.scalar_tensor_tensor(lg2[:].rearrange("q g e -> q (g e)"),
                               lg2[:].rearrange("q g e -> q (g e)"), -BIG,
                               lgs[:].rearrange("q g e -> q (g e)"), ALU.mult, ALU.add)
        v.drain()
        v.reduce_max(m2t[:, 0:NG], lg2[:], axis=AX.X)
        v.drain()
        v.tensor_scalar(c1t[:, 0:NG], c1t[:, 0:NG], 2.0, None, ALU.is_ge)
        v.drain()
        v.tensor_sub(tt2b[:, 0:NG], m1t[:, 0:NG], m2t[:, 0:NG])
        v.drain()
        v.tensor_mul(tt2b[:, 0:NG], tt2b[:, 0:NG], c1t[:, 0:NG])
        v.drain()
        v.tensor_add(m2t[:, 0:NG], m2t[:, 0:NG], tt2b[:, 0:NG])
        v.drain()
        v.tensor_tensor(lg2[:], lgs[:], m1b, ALU.subtract)
        p.inc(v.drain(), "sv")
    block.vector(sp_v)
    SVS1 = p.val("sv")

    def sp_s(sc):
        p.wait(sc, "sv", SVS1)
        sc.activation(lg2[:].rearrange("q g e -> q (g e)"),
                      lg2[:].rearrange("q g e -> q (g e)"), AF.Exp)
        p.inc(sc.drain(), "ss")
    block.scalar(sp_s)
    SSS1 = p.val("ss")

    def sp_v2(v):
        p.wait(v, "ss", SSS1)
        m2b = m2t[:, 0:NG].rearrange("q (g o) -> q g o", o=1).broadcast_to((128, NG, 8))
        v.tensor_tensor(lgs[:], lgs[:], m2b, ALU.is_ge)
        v.drain()
        v.tensor_mul(lg2[:].rearrange("q g e -> q (g e)"), lg2[:].rearrange("q g e -> q (g e)"),
                     lgs[:].rearrange("q g e -> q (g e)"))
        v.drain()
        v.reduce_sum(m1t[:, 0:NG], lg2[:], axis=AX.X)
        v.drain()
        v.reciprocal(c1t[:, 0:NG], m1t[:, 0:NG])
        v.drain()
        rb = c1t[:, 0:NG].rearrange("q (g o) -> q g o", o=1).broadcast_to((128, NG, 8))
        v.tensor_tensor(prg[:, 1:GT, :], lg2[:], rb, ALU.mult)
        p.inc(v.drain(), "sv")
    block.vector(sp_v2)

    def dg_v(v):
        cb = logc[:].rearrange("q (o e) -> q o e", o=1).broadcast_to((128, TILES, 8))
        v.tensor_copy(lga[:], cb)
        v.drain()
        ab_ = affs[:, 0:8].rearrange("q (o e) -> q o e", o=1).broadcast_to((128, HB, 8))
        bb_ = affs[:, 8:16].rearrange("q (o e) -> q o e", o=1).broadcast_to((128, TILES - HB, 8))
        v.tensor_tensor(lga[:, 0:HB, :], lga[:, 0:HB, :], ab_, ALU.add)
        v.tensor_tensor(lga[:, HB:TILES, :], lga[:, HB:TILES, :], bb_, ALU.add)
        v.drain()
        v.scalar_tensor_tensor(lga[:].rearrange("q a e -> q (a e)"), noi[:], bis[:, 12:13],
                               lga[:].rearrange("q a e -> q (a e)"), ALU.mult, ALU.add)
        v.drain()
        v.tensor_scalar_mul(lga[:].rearrange("q a e -> q (a e)"),
                            lga[:].rearrange("q a e -> q (a e)"), 1.0 / 1.5)
        v.drain()
        v.reduce_max(m1t[:], lga[:], axis=AX.X)
        v.drain()
        m1b = m1t[:].rearrange("q (a o) -> q a o", o=1).broadcast_to((128, TILES, 8))
        v.tensor_tensor(lgb[:], lga[:], m1b, ALU.is_equal)
        v.drain()
        v.reduce_sum(c1t[:], lgb[:], axis=AX.X)
        v.drain()
        v.scalar_tensor_tensor(lgb[:].rearrange("q a e -> q (a e)"),
                               lgb[:].rearrange("q a e -> q (a e)"), -BIG,
                               lga[:].rearrange("q a e -> q (a e)"), ALU.mult, ALU.add)
        v.drain()
        v.reduce_max(m2t[:], lgb[:], axis=AX.X)
        v.drain()
        v.tensor_scalar(c1t[:], c1t[:], 2.0, None, ALU.is_ge)
        v.drain()
        v.tensor_sub(tt2b[:], m1t[:], m2t[:])
        v.drain()
        v.tensor_mul(tt2b[:], tt2b[:], c1t[:])
        v.drain()
        v.tensor_add(m2t[:], m2t[:], tt2b[:])
        v.drain()
        v.tensor_tensor(lgb[:], lga[:], m1b, ALU.subtract)
        p.inc(v.drain(), "sv")
    block.vector(dg_v)
    SVD1 = p.val("sv")

    def dg_sc(sc):
        p.wait(sc, "sv", SVD1)
        sc.activation(lgb[:].rearrange("q a e -> q (a e)"),
                      lgb[:].rearrange("q a e -> q (a e)"), AF.Exp)
        p.inc(sc.drain(), "ss")
    block.scalar(dg_sc)
    SSD = p.val("ss")

    def dg_v2(v):
        p.wait(v, "ss", SSD)
        m2b = m2t[:].rearrange("q (a o) -> q a o", o=1).broadcast_to((128, TILES, 8))
        v.tensor_tensor(lga[:], lga[:], m2b, ALU.is_ge)
        v.drain()
        v.tensor_mul(lgb[:].rearrange("q a e -> q (a e)"), lgb[:].rearrange("q a e -> q (a e)"),
                     lga[:].rearrange("q a e -> q (a e)"))
        v.drain()
        v.reduce_sum(m1t[:], lgb[:], axis=AX.X)
        v.drain()
        v.reciprocal(c1t[:], m1t[:])
        v.drain()
        rb = c1t[:].rearrange("q (a o) -> q a o", o=1).broadcast_to((128, TILES, 8))
        v.tensor_tensor(probs[:], lgb[:], rb, ALU.mult)
        p.inc(v.drain(), "sv")
    block.vector(dg_v2)
    GI_PRG = p.val("gi")

    return _build_l2_experts(nc, blk, block, p, mile, ctx, s1, locals())


def _build_l2_experts(nc, blk, block, p, mile, ctx, s1, env):
    g_ = env
    TOK, TILES, GT, NG, TCH = g_["TOK"], g_["TILES"], g_["GT"], g_["NG"], g_["TCH"]
    pb, XgT, comb, Cbf, prg, prT = (g_[k] for k in
        ("pb", "XgT", "comb", "Cbf", "prg", "prT"))
    ew_d, eb_d = g_["ew_d"], g_["eb_d"]
    identf, onesf, msku, probs, st, cpool = (g_[k] for k in
        ("identf", "onesf", "msku", "probs", "st", "cpool"))
    x1_d, y_d, comb_dram, idxg = (g_[k] for k in
        ("x1_d", "y_d", "comb_dram", "idxg"))
    sb = g_["sb"]
    s2 = g_["s2"]
    GI_PRG = g_["GI_PRG"]

    for gg in reversed(s1):
        gg.__exit__(None, None, None)
    for gg in reversed(s2):
        gg.__exit__(None, None, None)

    s3 = []

    def sb3(name, shape, dtype=F32):
        g = nc.sbuf_tensor(name, shape, dtype)
        t = g.__enter__()
        s3.append(g)
        return t
    wq1 = sb3("wq1", [128, 8, 512], BF16)
    wq2 = sb3("wq2", [128, 16, 512], BF16)
    wq3f = sb3("wq3f", [128, 4096], BF16)
    bq1 = sb3("bq1", [128, 16], F32)
    bq2 = sb3("bq2", [128, 4], F32)
    bqs = sb3("bqs", [4, 128], F32)
    h1q = sb3("h1q", [128, 16, GT * 128], BF16)
    h2q = sb3("h2q", [128, 4, GT * 128], BF16)
    zf = sb3("zf", [128, 512], F32)
    z2f = sb3("z2f", [128, 512], F32)
    sgf = sb3("sgf", [128, 512], F32)
    Cacc = sb3("Cacc", [128, C], F32)
    bxb = sb3("bxb", [128, C], F32)
    ebrow = sb3("ebrow", [1, C], F32)
    crow = sb3("crow", [1, C], BF16)
    wq3_8 = wq3f[:].rearrange("q (k n) -> q k n", k=8)
    wq3_4 = wq3f[:].rearrange("q (k n) -> q k n", k=4)

    BAR_SV = p.val("sv")
    BAR_SS = p.val("ss")
    BAR_TT = p.val("tt")

    def bar_g(g):
        p.wait(g, "sv", BAR_SV)
        p.wait(g, "ss", BAR_SS)
        p.wait(g, "tt", BAR_TT)
        p.wait(g, "gi", GI_PRG)
    block.gpsimd(bar_g)
    GI_BAR = p.val("gi")

    def mm1_stage(t_, nck, ncols_h, wsrc, hdst, psum_tiles, svbase):
        pass

    for e in range(8):
        fam, idx = SLOT_FAM[e]
        if fam == 0:
            parts, hw, act = 4, 1024, "gelu"
        elif fam == 2:
            parts, hw, act = 3, 1024, "ssig"
        else:
            parts, hw, act = 2, 1024, None

        if fam in (0, 2):
            for q in range(parts):
                for dh_ in range(2):
                    def wl1_g(g, fam=fam, idx=idx, q=q, e=e, dh_=dh_):
                        if e == 0 and q == 0 and dh_ == 0:
                            p.wait(g, "gi", GI_BAR)
                        p.wait(g, "sv")
                        off = q * 1024 + dh_ * 512
                        g.dma_start(out=wq1[:], in_=ew_d[(fam, 0)][idx, :, off:off + 512]
                                    .rearrange("(k q2) n -> q2 k n", q2=128)).then_inc(p.c["gi"][0], 16)
                        g.dma_start(out=bqs[:], in_=eb_d[(fam, 0)][idx, off:off + 512]
                                    .rearrange("(h q2) -> h q2", h=4)).then_inc(p.c["gi"][0], 16)
                        p.c["gi"][1] += 32
                    block.gpsimd(wl1_g)
                    GIW = p.val("gi")

                    def bqt_t(t, GIW=GIW):
                        p.wait(t, "gi", GIW)
                        i1 = t.transpose(pb[3][:, 0:4], bqs[0:4, 0:128], identf[0:4, 0:4])
                        p.inc(i1, "tt")
                    block.tensor(bqt_t)
                    TTBQ = p.val("tt")

                    def bqt_v(v, dh_=dh_, TTBQ=TTBQ):
                        p.wait(v, "tt", TTBQ)
                        v.tensor_copy(bq1[:, dh_ * 4:(dh_ + 1) * 4], pb[3][:, 0:4])
                        v.drain()
                        p.inc(v.nop(), "sv")
                    block.vector(bqt_v)

                    for ht in range(4):
                        hb = dh_ * 4 + ht
                        for tc in range(TCH):
                            bank = pb[(ht * TCH + tc) % 2]
                            svh = p.val("sv")

                            def m1_t(t, ht=ht, tc=tc, bank=bank, GIW=GIW, svh=svh):
                                if ht == 0 and tc == 0:
                                    p.wait(t, "gi", GIW)
                                if ht * TCH + tc >= 2:
                                    p.wait(t, "sv", svh - 1)
                                m = None
                                for c in range(8):
                                    m = t.matmul(bank[:], wq1[:, c, ht * 128:(ht + 1) * 128],
                                                 XgT[:, c, tc * 512:(tc + 1) * 512],
                                                 start=(c == 0), stop=(c == 7))
                                p.inc(m, "tt")
                            block.tensor(m1_t)
                            TTM = p.val("tt")

                            def m1_v(v, hb=hb, bank=bank, TTM=TTM, act=act):
                                p.wait(v, "tt", TTM)
                                v.tensor_scalar_add(zf[:], bank[:], bq1[:, hb:hb + 1])
                                p.inc(v.nop(), "sv")
                                if act == "gelu":
                                    v.tensor_mul(z2f[:], zf[:], zf[:])
                                    v.drain()
                                    v.tensor_scalar(z2f[:], z2f[:], 0.07135481282929373,
                                                    1.5957691216057308, ALU.mult, ALU.add)
                                    v.drain()
                                    v.tensor_mul(z2f[:], z2f[:], zf[:])
                                p.inc(v.drain(), "sv")
                            block.vector(m1_v)
                            SVZ = p.val("sv")

                            def m1_s(sc, act=act, SVZ=SVZ):
                                p.wait(sc, "sv", SVZ)
                                if act == "gelu":
                                    sc.activation(sgf[:], z2f[:], AF.Sigmoid)
                                else:
                                    sc.activation(z2f[:], zf[:], AF.Silu)
                                    sc.drain()
                                    sc.activation(sgf[:], zf[:], AF.Sigmoid)
                                p.inc(sc.drain(), "ss")
                            block.scalar(m1_s)
                            SSZ = p.val("ss")

                            def m1_v2(v, hb=hb, tc=tc, act=act, SSZ=SSZ):
                                p.wait(v, "ss", SSZ)
                                if act == "gelu":
                                    v.tensor_mul(h1q[:, hb, tc * 512:(tc + 1) * 512], zf[:], sgf[:])
                                else:
                                    v.tensor_mul(h1q[:, hb, tc * 512:(tc + 1) * 512], z2f[:], sgf[:])
                                v.drain()
                                p.inc(v.nop(), "sv")
                            block.vector(m1_v2)
                SV_H1 = p.val("sv")

                for dc in range(2):
                    def wl3_g(g, fam=fam, idx=idx, q=q, dc=dc):
                        p.wait(g, "sv")
                        g.dma_start(out=wq3_8, in_=ew_d[(fam, 1)][idx, q * 1024:(q + 1) * 1024,
                                    dc * 512:(dc + 1) * 512]
                                    .rearrange("(k q2) n -> q2 k n", q2=128)).then_inc(p.c["gi"][0], 16)
                        p.c["gi"][1] += 16
                    block.gpsimd(wl3_g)
                    GIW3 = p.val("gi")

                    for tt_ in range(GT):
                        bank = pb[4 + tt_ % 2]
                        svh = p.val("sv")

                        def m2_t(t, tt_=tt_, dc=dc, bank=bank, svh=svh, GIW3=GIW3, SV_H1=SV_H1):
                            if tt_ == 0:
                                p.wait(t, "gi", GIW3)
                                if dc == 0:
                                    p.wait(t, "sv", SV_H1)
                            if tt_ >= 2:
                                p.wait(t, "sv", svh)
                            m = None
                            for hc in range(8):
                                m = t.matmul(bank[:], h1q[:, hc, tt_ * 128:(tt_ + 1) * 128],
                                             wq3_8[:, hc, :],
                                             start=(hc == 0), stop=(hc == 7))
                            p.inc(m, "tt")
                        block.tensor(m2_t)
                        TTM2 = p.val("tt")

                        def m2_v(v, tt_=tt_, dc=dc, q=q, e=e, bank=bank, TTM2=TTM2):
                            p.wait(v, "tt", TTM2)
                            if tt_ == 0:
                                if q == 0:
                                    v.tensor_copy(Cacc[:, dc * 512:(dc + 1) * 512], bank[:])
                                else:
                                    v.tensor_add(Cacc[:, dc * 512:(dc + 1) * 512],
                                                 Cacc[:, dc * 512:(dc + 1) * 512], bank[:])
                            else:
                                v.scalar_tensor_tensor(comb[:, tt_, dc * 512:(dc + 1) * 512],
                                                       bank[:], prg[:, tt_, e:e + 1],
                                                       comb[:, tt_, dc * 512:(dc + 1) * 512],
                                                       ALU.mult, ALU.add)
                            v.drain()
                            p.inc(v.nop(), "sv")
                        block.vector(m2_v)
        else:
            for ch in range(4):
                def wl1_g(g, idx=idx, ch=ch, e=e):
                    if e == 0 and ch == 0:
                        p.wait(g, "gi", GI_BAR)
                    p.wait(g, "sv")
                    off = ch * 512
                    g.dma_start(out=wq1[:], in_=ew_d[(1, 0)][idx, :, off:off + 512]
                                .rearrange("(k q2) n -> q2 k n", q2=128)).then_inc(p.c["gi"][0], 16)
                    g.dma_start(out=bqs[:], in_=eb_d[(1, 0)][idx, off:off + 512]
                                .rearrange("(h q2) -> h q2", h=4)).then_inc(p.c["gi"][0], 16)
                    p.c["gi"][1] += 32
                block.gpsimd(wl1_g)
                GIW = p.val("gi")

                def bqt_t(t, GIW=GIW):
                    p.wait(t, "gi", GIW)
                    i1 = t.transpose(pb[3][:, 0:4], bqs[0:4, 0:128], identf[0:4, 0:4])
                    p.inc(i1, "tt")
                block.tensor(bqt_t)
                TTBQ = p.val("tt")

                def bqt_v(v, ch=ch, TTBQ=TTBQ):
                    p.wait(v, "tt", TTBQ)
                    v.tensor_copy(bq1[:, ch * 4:(ch + 1) * 4], pb[3][:, 0:4])
                    v.drain()
                    p.inc(v.nop(), "sv")
                block.vector(bqt_v)
                SVBQ = p.val("sv")

                for ht in range(4):
                    hb = ch * 4 + ht
                    for tc in range(TCH):
                        bank = pb[(ht * TCH + tc) % 2]
                        svh = p.val("sv")

                        def m1_t(t, ht=ht, tc=tc, bank=bank, GIW=GIW, svh=svh):
                            if ht == 0 and tc == 0:
                                p.wait(t, "gi", GIW)
                            if ht * TCH + tc >= 2:
                                p.wait(t, "sv", svh)
                            m = None
                            for c in range(8):
                                m = t.matmul(bank[:], wq1[:, c, ht * 128:(ht + 1) * 128],
                                             XgT[:, c, tc * 512:(tc + 1) * 512],
                                             start=(c == 0), stop=(c == 7))
                            p.inc(m, "tt")
                        block.tensor(m1_t)
                        TTM = p.val("tt")

                        def m1_s(sc, ht=ht, tc=tc, hb=hb, bank=bank, TTM=TTM, SVBQ=SVBQ):
                            p.wait(sc, "tt", TTM)
                            if ht == 0 and tc == 0:
                                p.wait(sc, "sv", SVBQ)
                            sc.activation(h1q[:, hb, tc * 512:(tc + 1) * 512],
                                          bank[:], AF.Silu, bias=bq1[:, hb:hb + 1])
                            sc.drain()
                            p.inc(sc.nop(), "sv")
                        block.scalar(m1_s)
            SV_H1 = p.val("sv")
            for wq in range(4):
                def wl2_g(g, idx=idx, wq=wq, SV_H1=SV_H1):
                    p.wait(g, "sv")
                    g.dma_start(out=wq2[:], in_=ew_d[(1, 1)][idx, :, wq * 512:(wq + 1) * 512]
                                .rearrange("(k q2) n -> q2 k n", q2=128)).then_inc(p.c["gi"][0], 16)
                    g.dma_start(out=bqs[:], in_=eb_d[(1, 1)][idx, wq * 512:(wq + 1) * 512]
                                .rearrange("(h q2) -> h q2", h=4)).then_inc(p.c["gi"][0], 16)
                    g.dma_start(out=wq3_4, in_=ew_d[(1, 2)][idx, wq * 512:(wq + 1) * 512, :]
                                .rearrange("(k q2) n -> q2 k n", q2=128)).then_inc(p.c["gi"][0], 16)
                    p.c["gi"][1] += 48
                block.gpsimd(wl2_g)
                GIW2 = p.val("gi")

                def bqt2_t(t, GIW2=GIW2):
                    p.wait(t, "gi", GIW2)
                    i1 = t.transpose(pb[3][:, 0:4], bqs[0:4, 0:128], identf[0:4, 0:4])
                    p.inc(i1, "tt")
                block.tensor(bqt2_t)
                TTBQ2 = p.val("tt")

                def bqt2_v(v, TTBQ2=TTBQ2):
                    p.wait(v, "tt", TTBQ2)
                    v.tensor_copy(bq2[:, 0:4], pb[3][:, 0:4])
                    v.drain()
                    p.inc(v.nop(), "sv")
                block.vector(bqt2_v)
                SVBQ2 = p.val("sv")

                for ht in range(4):
                    for tc in range(TCH):
                        bank = pb[(ht * TCH + tc) % 2]
                        svh = p.val("sv")

                        def m2e_t(t, ht=ht, tc=tc, bank=bank, GIW2=GIW2, svh=svh, SV_H1=SV_H1):
                            if ht == 0 and tc == 0:
                                p.wait(t, "gi", GIW2)
                                p.wait(t, "sv", SV_H1)
                            if ht * TCH + tc >= 2:
                                p.wait(t, "sv", svh)
                            m = None
                            for hc in range(16):
                                m = t.matmul(bank[:], wq2[:, hc, ht * 128:(ht + 1) * 128],
                                             h1q[:, hc, tc * 512:(tc + 1) * 512],
                                             start=(hc == 0), stop=(hc == 15))
                            p.inc(m, "tt")
                        block.tensor(m2e_t)
                        TTM = p.val("tt")

                        def m2e_s(sc, ht=ht, tc=tc, bank=bank, TTM=TTM, SVBQ2=SVBQ2):
                            p.wait(sc, "tt", TTM)
                            if ht == 0 and tc == 0:
                                p.wait(sc, "sv", SVBQ2)
                            sc.activation(h2q[:, ht, tc * 512:(tc + 1) * 512],
                                          bank[:], AF.Silu, bias=bq2[:, ht:ht + 1])
                            sc.drain()
                            p.inc(sc.nop(), "sv")
                        block.scalar(m2e_s)
                SV_H2 = p.val("sv")
                for tt_ in range(GT):
                    for dc in range(2):
                        bank = pb[4 + (tt_ * 2 + dc) % 2]
                        svh = p.val("sv")

                        def m3_t(t, tt_=tt_, dc=dc, bank=bank, svh=svh, SV_H2=SV_H2):
                            if tt_ == 0 and dc == 0:
                                p.wait(t, "sv", SV_H2)
                            if tt_ * 2 + dc >= 2:
                                p.wait(t, "sv", svh)
                            m = None
                            for hc in range(4):
                                m = t.matmul(bank[:], h2q[:, hc, tt_ * 128:(tt_ + 1) * 128],
                                             wq3_4[:, hc, dc * 512:(dc + 1) * 512],
                                             start=(hc == 0), stop=(hc == 3))
                            p.inc(m, "tt")
                        block.tensor(m3_t)
                        TTM3 = p.val("tt")

                        def m3_v(v, tt_=tt_, dc=dc, wq=wq, e=e, bank=bank, TTM3=TTM3):
                            p.wait(v, "tt", TTM3)
                            if tt_ == 0:
                                if wq == 0:
                                    v.tensor_copy(Cacc[:, dc * 512:(dc + 1) * 512], bank[:])
                                else:
                                    v.tensor_add(Cacc[:, dc * 512:(dc + 1) * 512],
                                                 Cacc[:, dc * 512:(dc + 1) * 512], bank[:])
                            else:
                                v.scalar_tensor_tensor(comb[:, tt_, dc * 512:(dc + 1) * 512],
                                                       bank[:], prg[:, tt_, e:e + 1],
                                                       comb[:, tt_, dc * 512:(dc + 1) * 512],
                                                       ALU.mult, ALU.add)
                            v.drain()
                            p.inc(v.nop(), "sv")
                        block.vector(m3_v)

        def eb_s(s, fam=fam, idx=idx):
            lastb = eb_d[(fam, len(FAMW[fam]) - 1)]
            p.wait(s, "sv")
            s.dma_start(out=ebrow[0:1, 0:1024], in_=lastb[idx:idx + 1, :]).then_inc(p.c["dh"][0], 16)
            p.c["dh"][1] += 16
        block.sync(eb_s)
        DHB = p.val("dh")

        def eb_t(t, DHB=DHB):
            p.wait(t, "dh", DHB)
            m = t.matmul(pb[6][:], onesf[0:1, :], ebrow[0:1, 0:512], start=True, stop=True)
            p.inc(m, "tt")
            m = t.matmul(pb[7][:], onesf[0:1, :], ebrow[0:1, 512:1024], start=True, stop=True)
            p.inc(m, "tt")
        block.tensor(eb_t)
        TTB2 = p.val("tt")

        def eb_v(v, e=e, TTB2=TTB2):
            p.wait(v, "tt", TTB2)
            v.tensor_copy(bxb[:, 0:512], pb[6][:])
            v.tensor_copy(bxb[:, 512:1024], pb[7][:])
            v.drain()
            for tt_ in range(1, GT):
                v.scalar_tensor_tensor(comb[:, tt_, :], bxb[:], prg[:, tt_, e:e + 1],
                                       comb[:, tt_, :], ALU.mult, ALU.add)
            v.tensor_add(Cacc[:], Cacc[:], bxb[:])
            v.drain()
            v.tensor_copy(crow[:], Cacc[0:1, :])
            v.drain()
            p.inc(v.nop(), "sv")
        block.vector(eb_v)
        SVEB = p.val("sv")

        def eb_g(g, e=e, SVEB=SVEB):
            p.wait(g, "sv", SVEB)
            g.dma_start(out=Cbf[e:e + 1, :], in_=crow[:]).then_inc(p.c["gi"][0], 16)
            p.c["gi"][1] += 16
            g.wait_ge(p.c["gi"][0], p.c["gi"][1])
        block.gpsimd(eb_g)

    SV_EXP = p.val("sv")

    DH_CZ = g_["DH_CZ"]

    def sc_g(g):
        p.wait(g, "sv", SV_EXP)
        g.wait_ge(p.c["dh"][0], DH_CZ)
        for gti in range(NG):
            g.indirect_dma_start(
                out=comb_dram[:], out_offset=bass.IndirectOffsetOnAxis(ap=idxg[:, gti:gti + 1], axis=0),
                in_=comb[:, gti + 1, :], in_offset=None,
                bounds_check=TOK - 1, oob_is_err=False,
            ).then_inc(p.c["gi"][0], 16)
            p.c["gi"][1] += 16
        g.wait_ge(p.c["gi"][0], p.c["gi"][1])
    block.gpsimd(sc_g)
    GI_SC = p.val("gi")

    for gg in reversed(s3):
        gg.__exit__(None, None, None)
    fin = sb("fin", [128, C], F32)
    x1t = sb("x1t", [128, 2, C], F32)
    cmt = sb("cmt", [128, 2, C], F32)
    ysb = sb("ysb", [128, 2, C], F32)
    sq2 = sb("sq2", [128, C], F32)

    for i in range(TILES):
        par = i % 2

        def f_s(s, i=i, par=par):
            if i == 0:
                p.wait(s, "gi", GI_SC)
            if i >= 2:
                p.wait(s, "sv", mile[("finfree", i - 2)])
                s.wait_ge(p.c["do"][0], mile[("ydma", i - 2)])
            s.dma_start(out=x1t[:, par, :], in_=x1_d[i * 128:(i + 1) * 128, :]).then_inc(p.c["dh"][0], 16)
            s.dma_start(out=cmt[:, par, :], in_=comb_dram[i * 128:(i + 1) * 128, :]).then_inc(p.c["dh"][0], 16)
            p.c["dh"][1] += 32
        block.sync(f_s)
        DHF = p.val("dh")

        def f_t0(t, i=i):
            if i == 0:
                p.wait(t, "sv", SV_EXP)
            else:
                p.wait(t, "sv", mile[("prT", i - 1)])
            i1 = t.transpose(pb[2][0:8, 0:128], probs[:, i, :], identf[:])
            p.inc(i1, "tt")
        block.tensor(f_t0)
        TTF0 = p.val("tt")

        def f_v0(v, TTF0=TTF0, i=i):
            p.wait(v, "tt", TTF0)
            v.tensor_copy(prT[:], pb[2][0:8, 0:128])
            v.drain()
            mile[("prT", i)] = p.inc(v.nop(), "sv")
        block.vector(f_v0)
        SVF0 = p.val("sv")

        def f_t1(t, i=i, SVF0=SVF0):
            p.wait(t, "sv", SVF0)
            if i == 0:
                p.wait(t, "gi", GI_SC)
            if i > 0:
                p.wait(t, "sv", mile[("basefree", i - 1)])
            m = t.matmul(pb[0][:], prT[:], Cbf[:, 0:512], start=True, stop=True)
            p.inc(m, "tt")
            m = t.matmul(pb[1][:], prT[:], Cbf[:, 512:1024], start=True, stop=True)
            p.inc(m, "tt")
        block.tensor(f_t1)
        TTF1 = p.val("tt")

        def f_v1(v, i=i, par=par, TTF1=TTF1, DHF=DHF):
            p.wait(v, "tt", TTF1)
            p.wait(v, "dh", DHF)
            mb = msku[:, i:i + 1].broadcast_to((128, 512))
            v.select(fin[:, 0:512], mb, cmt[:, par, 0:512], pb[0][:])
            v.select(fin[:, 512:1024], mb, cmt[:, par, 512:1024], pb[1][:])
            v.drain()
            mile[("basefree", i)] = p.inc(v.nop(), "sv")
        block.vector(f_v1)

        def f_s1(sc):
            p.wait(sc, "sv")
            sc.activation(sq2[:], fin[:], AF.Square, accum_out=st[:, 4:5])
            sc.activation(st[:, 5:6], st[:, 4:5], AF.Copy, scale=INV_C)
            p.inc(sc.drain(), "ss")
        block.scalar(f_s1)

        def f_v2(v):
            p.wait(v, "ss")
            v.reduce_sum(st[:, 0:1], fin[:], axis=AX.X)
            v.drain()
            v.tensor_scalar_mul(st[:, 2:3], st[:, 0:1], INV_C)
            v.drain()
            v.tensor_mul(st[:, 1:2], st[:, 2:3], st[:, 2:3])
            v.drain()
            v.tensor_sub(st[:, 3:4], st[:, 5:6], st[:, 1:2])
            p.inc(v.drain(), "sv")
        block.vector(f_v2)

        def f_s2(sc):
            p.wait(sc, "sv")
            sc.activation(st[:, 7:8], st[:, 3:4], AF.Ln, bias=cpool[:, 0:1])
            sc.drain()
            sc.activation(st[:, 6:7], st[:, 7:8], AF.Exp, scale=-0.5)
            p.inc(sc.drain(), "ss")
        block.scalar(f_s2)

        def f_v3(v, i=i, par=par):
            p.wait(v, "ss")
            v.tensor_scalar(fin[:], fin[:], st[:, 2:3], st[:, 6:7], ALU.subtract, ALU.mult)
            v.drain()
            v.tensor_add(ysb[:, par, :], x1t[:, par, :], fin[:])
            v.drain()
            mile[("finfree", i)] = p.inc(v.nop(), "sv")
        block.vector(f_v3)
        SVF3 = p.val("sv")

        def f_o(s, i=i, par=par, SVF3=SVF3):
            p.wait(s, "sv", SVF3)
            s.dma_start(out=y_d[i * 128:(i + 1) * 128, :], in_=ysb[:, par, :]).then_inc(p.c["do"][0], 16)
            p.c["do"][1] += 16
            mile[("ydma", i)] = p.c["do"][1]
        block.sync(f_o)

    def tail(s):
        s.wait_ge(p.c["do"][0], p.c["do"][1])
    block.sync(tail)

    blk.__exit__(None, None, None)
    for gg in reversed(ctx):
        gg.__exit__(None, None, None)
    for gg in p.ctxs:
        gg.__exit__(None, None, None)
    return nc




def l2_inputs(inp, x1, xm16, imp_own, n1_own, imp_all, noise_perm):
    A = lambda k: np.ascontiguousarray(np.asarray(inp["L0_ffn_" + k], dtype=np.float32))
    gvec = np.zeros((4, C), np.float32)
    gvec[0] = A("gate_b1").reshape(-1)
    gvec[1] = A("ln_scale").reshape(-1)
    gvec[2] = A("ln_offset").reshape(-1)
    gsml = np.zeros((2, 512), np.float32)
    gsml[0] = A("gate_b2").reshape(-1)
    gsml[1, 0:8] = A("gate_b3").reshape(-1)
    gsml[1, 8:16] = A("aff_b").reshape(-1)
    pp = np.arange(128)[:, None]
    ii = np.arange(128)[None, :]
    d = {
        "x1": np.ascontiguousarray(x1), "xm16": np.ascontiguousarray(xm16),
        "imp_own": np.ascontiguousarray(imp_own), "n1_own": np.ascontiguousarray(n1_own),
        "imp_all": np.ascontiguousarray(imp_all), "noise": np.ascontiguousarray(noise_perm),
        "eu": A("expert_usage").reshape(1, 8),
        "gw1": A("gate_w1"), "gw2": A("gate_w2"), "gw3": A("gate_w3"),
        "aw": A("aff_w"), "gvec": gvec, "gsml": gsml,
        "suo": (pp < ii).astype(np.float32),
        "identf": np.eye(128, dtype=np.float32),
    }
    for f in (0, 1, 2):
        for li in range(len(FAMW[f])):
            d[f"e{f}w{li + 1}"] = A(f"e{f}_w{li + 1}")
            d[f"e{f}b{li + 1}"] = A(f"e{f}_b{li + 1}")
    return d


def _noise_perm(noise_tok):
    TOK = noise_tok.shape[0]
    TILES = TOK // 128
    return np.ascontiguousarray(
        noise_tok.reshape(TILES, 128, 8).transpose(1, 0, 2).reshape(128, TILES * 8))


def kernel(**inputs):
    import jax
    T = 4096
    x = np.asarray(inputs["L0_attn_x"], dtype=np.float32)
    BT = x.shape[0]
    with jax.default_device(jax.devices("cpu")[0]):
        noise = np.asarray(jax.random.normal(jax.random.key(1), (BT, T, 8)), np.float32)

    nc1 = build_l1(T)
    maps1 = []
    for c in range(NCORES):
        xc = x[B_CORE * c:B_CORE * (c + 1)].reshape(-1, C)
        maps1.append(l1_inputs(xc, inputs))
    r1 = run_bass_kernel_spmd(nc1, maps1, core_ids=list(range(NCORES))).results

    imp_all = np.concatenate([r1[c]["impo"] for c in range(NCORES)], axis=1)
    n_all = imp_all.size
    nc2 = build_l2(T, n_all)
    maps2 = []
    for c in range(NCORES):
        nz = noise[B_CORE * c:B_CORE * (c + 1)].reshape(-1, 8)
        maps2.append(l2_inputs(inputs, r1[c]["x1o"], r1[c]["xmo"], r1[c]["impo"],
                               r1[c]["n1o"], imp_all, _noise_perm(nz)))
    r2 = run_bass_kernel_spmd(nc2, maps2, core_ids=list(range(NCORES))).results
    y = np.concatenate([r2[c]["y"].reshape(B_CORE, T, C) for c in range(NCORES)], axis=0)
    return y



# version 03
# speedup vs baseline: 1.2461x; 1.2461x over previous
import numpy as np
import concourse.bass as bass
import concourse.mybir as mybir
from concourse.bass_utils import run_bass_kernel_spmd

F32 = mybir.dt.float32
BF16 = mybir.dt.bfloat16
I32 = mybir.dt.int32
U8 = mybir.dt.uint8
AF = mybir.ActivationFunctionType
ALU = mybir.AluOpType
AX = mybir.AxisListType

C = 1024
H = 16
N = 64
NCORES = 8
B_CORE = 2
L = 128
INV_C = 1.0 / C


class P:
    def __init__(self, nc, semnames):
        self.nc = nc
        self.c = {}
        self.ctxs = []
        for nm in semnames:
            g = nc.semaphore(nm)
            self.c[nm] = [g.__enter__(), 0]
            self.ctxs.append(g)

    def inc(self, inst, nm, n=1):
        self.c[nm][1] += n
        inst.then_inc(self.c[nm][0], n)
        return self.c[nm][1]

    def val(self, nm):
        return self.c[nm][1]

    def wait(self, eng, nm, v=None):
        eng.wait_ge(self.c[nm][0], self.c[nm][1] if v is None else v)


def build_l1(T, debug=()):
    NT = T // L
    TILES = B_CORE * NT
    nc = bass.Bass("TRN2", target_bir_lowering=False, detect_race_conditions=False)

    x_d = nc.dram_tensor("x", [TILES * L, C], F32, kind="ExternalInput").ap()
    Wd = {w: nc.dram_tensor(w, [C, C], F32, kind="ExternalInput").ap()
          for w in ("Wr", "Wk", "Wv", "Wo")}
    w1_d = nc.dram_tensor("w1", [C, 64], F32, kind="ExternalInput").ap()
    a1_d = nc.dram_tensor("a1", [C, 64], F32, kind="ExternalInput").ap()
    g1_d = nc.dram_tensor("g1", [C, 128], F32, kind="ExternalInput").ap()
    w2_d = nc.dram_tensor("w2", [64, C], F32, kind="ExternalInput").ap()
    a2_d = nc.dram_tensor("a2", [64, C], F32, kind="ExternalInput").ap()
    g2_d = nc.dram_tensor("g2", [128, C], F32, kind="ExternalInput").ap()
    vecs_d = nc.dram_tensor("vecs", [13, C], F32, kind="ExternalInput").ap()
    identf_d = nc.dram_tensor("identf", [128, 128], F32, kind="ExternalInput").ap()
    masks_d = nc.dram_tensor("masks", [128, 384], U8, kind="ExternalInput").ap()

    x1_d = nc.dram_tensor("x1o", [TILES * L, C], F32, kind="ExternalOutput").ap()
    xm_d = nc.dram_tensor("xmo", [TILES * L, C], BF16, kind="ExternalOutput").ap()
    imp_d = nc.dram_tensor("impo", [128, TILES], F32, kind="ExternalOutput").ap()
    n1_d = nc.dram_tensor("n1o", [128, TILES], F32, kind="ExternalOutput").ap()
    dbg_d = {nm: nc.dram_tensor("dbg_" + nm, [TILES * L, C], F32, kind="ExternalOutput").ap()
             for nm in debug}

    ctx = []

    def sb(name, shape, dtype=F32):
        g = nc.sbuf_tensor(name, shape, dtype)
        t = g.__enter__()
        ctx.append(g)
        return t

    Ws = {w: sb(w + "_s", [128, 8, C], BF16) for w in ("Wr", "Wk", "Wv", "Wo")}
    lora1 = sb("lora1", [128, 16, 256], BF16)
    w2s = sb("w2s", [64, C], BF16)
    a2s = sb("a2s", [64, C], BF16)
    g2s = sb("g2s", [128, C], BF16)
    lam = sb("lam", [128, 8, 8], F32)
    identf = sb("identf_s", [128, 128], F32)
    masks = sb("masks_s", [128, 384], U8)
    zer_f = sb("zer_f", [128, 128], F32)
    onesf = sb("onesf", [128, 128], F32)
    cpool = sb("cpool", [128, 8], F32)
    kkb = sb("kkb", [128, C], BF16)
    kab = sb("kab", [128, C], BF16)
    omkab = sb("omkab", [128, C], BF16)
    rkb = sb("rkb", [128, C], BF16)
    gab = sb("gab", [128, C], BF16)
    beb = sb("beb", [128, C], BF16)
    w0b = sb("w0b", [128, C], BF16)
    a0b = sb("a0b", [128, C], BF16)
    ictx = []

    def sbi(name, shape, dtype=F32):
        g = nc.sbuf_tensor(name, shape, dtype)
        t = g.__enter__()
        ictx.append(g)
        return t
    l1f = sbi("l1f", [128, 8, 256], F32)
    vrow = {nm: sbi("vr_" + nm, [1, C], F32)
            for nm in ("k_k", "k_a", "rk", "gam", "bet", "w0", "a0")}

    pb = []
    for i in range(8):
        g = nc.psum_tensor(f"pb{i}", [128, 512], F32)
        pb.append(g.__enter__())
        ctx.append(g)
    MSL = masks[:, 0:128]
    MSU = masks[:, 128:256]
    MIU = masks[:, 256:384]

    blk = nc.Block()
    block = blk.__enter__()
    p = P(nc, ["dx0", "dx1", "dw", "dg", "do0", "do1", "dm", "dbgs", "sv", "ss", "tt"])
    dxn = ["dx0", "dx1"]
    don = ["do0", "do1"]
    mile = {}

    def init_sync(s):
        s.dma_start(out=identf[:], in_=identf_d[:]).then_inc(p.c["dm"][0], 16)
        s.dma_start(out=masks[:], in_=masks_d[:]).then_inc(p.c["dm"][0], 16)
        for j in range(6):
            s.dma_start(out=lam[:, :, j:j + 1],
                        in_=vecs_d[j:j + 1, :].rearrange("o (k q) -> q k o", q=128),
                        allow_slow_non_contiguous=True).then_inc(p.c["dm"][0], 16)
        p.c["dm"][1] += 5 * 16
        for j, nm in enumerate(("w0", "a0", "k_k", "k_a", "rk", "gam", "bet")):
            s.dma_start(out=vrow[nm][:], in_=vecs_d[6 + j:7 + j, :]).then_inc(p.c["dm"][0], 16)
        p.c["dm"][1] += 10 * 16
    block.sync(init_sync)

    def init_gp(g):
        for w in ("Wr", "Wk", "Wv", "Wo"):
            g.dma_start(out=Ws[w][:], in_=Wd[w].rearrange("(k q) n -> q k n", q=128)).then_inc(p.c["dw"][0], 16)
        g.dma_start(out=w2s[:], in_=w2_d[:]).then_inc(p.c["dw"][0], 16)
        g.dma_start(out=a2s[:], in_=a2_d[:]).then_inc(p.c["dw"][0], 16)
        g.dma_start(out=g2s[:], in_=g2_d[:]).then_inc(p.c["dw"][0], 16)
        g.dma_start(out=l1f[:, :, 0:64], in_=w1_d.rearrange("(k q) n -> q k n", q=128)).then_inc(p.c["dw"][0], 16)
        g.dma_start(out=l1f[:, :, 64:128], in_=a1_d.rearrange("(k q) n -> q k n", q=128)).then_inc(p.c["dw"][0], 16)
        g.dma_start(out=l1f[:, :, 128:256], in_=g1_d.rearrange("(k q) n -> q k n", q=128)).then_inc(p.c["dw"][0], 16)
        p.c["dw"][1] += 10 * 16
    block.gpsimd(init_gp)

    def init_vec(v):
        for col, val in ((0, 1e-5), (1, -0.5), (2, 1.0), (3, 6.4e-4)):
            v.memset(cpool[:, col:col + 1], val)
        v.memset(onesf[:], 1.0)
        v.memset(zer_f[:], 0.0)
        p.wait(v, "dw")
        p.wait(v, "dm")
        for k in range(8):
            v.tensor_copy(lora1[:, k, :], l1f[:, k, :])
            v.tensor_scalar_mul(lora1[:, 8 + k, 0:64], l1f[:, k, 0:64], lam[:, k, 3:4])
            v.tensor_scalar_mul(lora1[:, 8 + k, 64:128], l1f[:, k, 64:128], lam[:, k, 4:5])
            v.tensor_scalar_mul(lora1[:, 8 + k, 128:256], l1f[:, k, 128:256], lam[:, k, 5:6])
        p.inc(v.drain(), "sv")
    block.vector(init_vec)
    SV_ONES = p.val("sv")

    bnames = (("k_k", kkb), ("k_a", kab), ("rk", rkb), ("gam", gab),
              ("bet", beb), ("w0", w0b), ("a0", a0b))
    for bi, (nm, dst) in enumerate(bnames):
        def tsec(t, nm=nm, bi=bi):
            if bi == 0:
                p.wait(t, "dm")
                p.wait(t, "sv", SV_ONES)
            else:
                p.wait(t, "ss")
            m = t.matmul(pb[0][:], onesf[0:1, :], vrow[nm][0:1, 0:512], start=True, stop=True)
            p.inc(m, "tt")
            m = t.matmul(pb[1][:], onesf[0:1, :], vrow[nm][0:1, 512:1024], start=True, stop=True)
            p.inc(m, "tt")
        block.tensor(tsec)

        def ssec(sc, dst=dst):
            p.wait(sc, "tt")
            sc.copy(dst[:, 0:512], pb[0][:])
            sc.copy(dst[:, 512:1024], pb[1][:])
            p.inc(sc.drain(), "ss")
        block.scalar(ssec)
    SS_BC = p.val("ss")

    def init_vec2(v):
        p.wait(v, "ss", SS_BC)
        v.tensor_scalar(omkab[:], kab[:], -1.0, 1.0, ALU.mult, ALU.add)
        p.inc(v.drain(), "sv")
    block.vector(init_vec2)

    for g in reversed(ictx):
        g.__exit__(None, None, None)
    xt = sb("xt", [128, 2, C])
    st = sb("st", [128, 16])
    ln0 = sb("ln0", [128, C])
    xreg = sb("xreg", [128, C])
    sqs = xreg
    lwp = xreg
    lnat = sb("lnat", [128, 16, 128], BF16)
    lnxx = lnat
    prevc = sb("prevc", [128, 8])
    mixr = sb("mixr", [128, 8, 128], BF16)
    mixk = sb("mixk", [128, 8, 128], BF16)
    mixv = sb("mixv", [128, 8, 128], BF16)
    r_s = sb("r_s", [128, C])
    k_s = sb("k_s", [128, C])
    v_s = sb("v_s", [128, C])
    t1_s = sb("t1_s", [128, C])
    t2_s = sb("t2_s", [128, C])
    asig = sb("asig", [128, C])
    g_s = sb("g_s", [128, C])
    win_b = sb("win_b", [128, 64])
    ain_b = sb("ain_b", [128, 64])
    gin_b = sb("gin_b", [128, 128])
    winT = sb("winT", [64, 128], BF16)
    ainT = sb("ainT", [64, 128], BF16)
    ginT = sb("ginT", [128, 128], BF16)
    kkn = sb("kkn", [128, C])
    kt_s = sb("kt_s", [128, C])
    bsc = sb("bsc", [128, C])
    ss16 = sb("ss16", [128, 16])
    rn16 = sb("rn16", [128, 16])
    cum = sb("cum", [128, 8, 128])
    lncm = cum
    dd = sb("dd", [128, 8, 128], BF16)
    dLf = sb("dLf", [128, 8])
    ddi = sb("ddi", [128, 8, 128], BF16)
    dldc = cum
    bh_c = mixr
    kh_c = mixk
    rt_c = mixv
    btl = sb("btl", [128, C], BF16)
    ktl = sb("ktl", [128, C], BF16)
    vyc = sb("vyc", [128, C], BF16)
    v_b = vyc
    o_s = sb("o_s", [128, C])
    y_s = o_s
    ao_s = ln0
    x1_s = sb("x1_s", [128, 2, C])
    xm_s = t1_s
    impacc = sb("impacc", [128, TILES])
    n1acc = sb("n1acc", [128, TILES])
    stf2 = sb("stf2", [128, H * 64])
    stb2 = sb("stb2", [128, H * 64], BF16)
    wsn128 = ("A", "AT", "F1T", "CT", "GbT", "GkT", "F2", "F3", "F4", "F5", "F6",
              "F7", "Q0", "Q1", "TT")
    wsn64 = ("CV", "M0", "U")
    ws = [dict([(n, sb(f"ws{i}{n}", [128, 128], BF16)[:]) for n in wsn128]
               + [(n, sb(f"ws{i}{n}", [128, 64], BF16)[:]) for n in wsn64])
          for i in range(2)]

    for i in range(TILES):
        b = i // NT
        chunk = i % NT
        par = i % 2

        def s_load(s, i=i, par=par):
            if i == 0:
                p.wait(s, "ss", SS_BC)
                p.wait(s, "sv", SV_ONES)
            if i >= 2:
                p.wait(s, "sv", mile[("xfree", i - 2)])
            s.dma_start(out=xt[:, par, :], in_=x_d[i * L:(i + 1) * L, :]).then_inc(p.c[dxn[par]][0], 16)
            p.c[dxn[par]][1] += 16
        block.sync(s_load)
        xv = p.val(dxn[par])

        def s_sumsq(sc, par=par, xv=xv, i=i):
            p.wait(sc, dxn[par], xv)
            if i > 0:
                p.wait(sc, "sv", mile[("mix", i - 1)])
            sc.activation(sqs[:], xt[:, par, :], AF.Square, accum_out=st[:, 4:5])
            sc.activation(st[:, 5:6], st[:, 4:5], AF.Copy, scale=INV_C)
            p.inc(sc.drain(), "ss")
        block.scalar(s_sumsq)

        def s_var(v, par=par, xv=xv, i=i):
            p.wait(v, dxn[par], xv)
            if i > 0:
                p.wait(v, "tt", mile[("trdone", i - 1)])
            v.reduce_sum(st[:, 0:1], xt[:, par, :], axis=AX.X)
            p.wait(v, "ss")
            v.tensor_scalar_mul(st[:, 2:3], st[:, 0:1], INV_C)
            v.drain()
            v.tensor_mul(st[:, 1:2], st[:, 2:3], st[:, 2:3])
            v.drain()
            v.tensor_sub(st[:, 3:4], st[:, 5:6], st[:, 1:2])
            p.inc(v.drain(), "sv")
        block.vector(s_var)

        def s_rstd(sc):
            p.wait(sc, "sv")
            sc.activation(st[:, 7:8], st[:, 3:4], AF.Ln, bias=cpool[:, 0:1])
            sc.drain()
            sc.activation(st[:, 6:7], st[:, 7:8], AF.Exp, scale=-0.5)
            p.inc(sc.drain(), "ss")
        block.scalar(s_rstd)

        def s_ln0(v, par=par):
            p.wait(v, "ss")
            i1 = v.tensor_scalar(ln0[:], xt[:, par, :], st[:, 2:3], st[:, 6:7],
                                 ALU.subtract, ALU.mult)
            p.inc(i1, "sv")
        block.vector(s_ln0)
        SV_LN0 = p.val("sv")

        tt0 = p.val("tt")
        sv0 = p.val("sv")
        ss0 = p.val("ss")
        trev = [sv0 + k + 1 for k in range(8)]
        trevs = [ss0 + k + 1 for k in range(8)]

        def s_tr(t, tt0=tt0, trev=trev, trevs=trevs, i=i):
            p.wait(t, "sv", SV_LN0)
            for k in range(8):
                if k >= 2:
                    p.wait(t, "sv", trev[k - 2])
                    p.wait(t, "ss", trevs[k - 2])
                i1 = t.transpose(pb[2 + (k % 2)][:, 0:128], ln0[:, k * 128:(k + 1) * 128], identf[:])
                p.inc(i1, "tt")
            mile[("trdone", i)] = p.val("tt")
        block.tensor(s_tr)

        for k in range(8):
            def s_trev_v(v, k=k, tt0=tt0):
                p.wait(v, "tt", tt0 + k + 1)
                i1 = v.tensor_copy(lncm[:, k, :], pb[2 + (k % 2)][:, 0:128])
                assert p.inc(i1, "sv") == trev[k]
            block.vector(s_trev_v)

            def s_trev_s(sc, k=k, tt0=tt0):
                p.wait(sc, "tt", tt0 + k + 1)
                sc.copy(lnxx[:, k, :], pb[2 + (k % 2)][:, 0:128])
                sc.drain()
                assert p.inc(sc.nop(), "ss") == trevs[k]
            block.scalar(s_trev_s)

        def s_mix(v, chunk=chunk, i=i):
            p.wait(v, "sv", trev[7])
            if i > 0:
                p.wait(v, "tt", mile[("projdone", i - 1)])
            for k in range(8):
                v.tensor_sub(xreg[:, k * 128 + 1:k * 128 + 128], lncm[:, k, 0:127], lncm[:, k, 1:128])
                if chunk == 0:
                    v.tensor_scalar_mul(xreg[:, k * 128:k * 128 + 1], lncm[:, k, 0:1], -1.0)
                else:
                    v.tensor_sub(xreg[:, k * 128:k * 128 + 1], prevc[:, k:k + 1], lncm[:, k, 0:1])
            v.drain()
            v.tensor_copy(prevc[:], lncm[:, :, 127])
            for k in range(8):
                v.scalar_tensor_tensor(mixr[:, k, :], xreg[:, k * 128:(k + 1) * 128], lam[:, k, 0:1],
                                       lncm[:, k, :], ALU.mult, ALU.add)
                v.scalar_tensor_tensor(mixk[:, k, :], xreg[:, k * 128:(k + 1) * 128], lam[:, k, 1:2],
                                       lncm[:, k, :], ALU.mult, ALU.add)
                v.scalar_tensor_tensor(mixv[:, k, :], xreg[:, k * 128:(k + 1) * 128], lam[:, k, 2:3],
                                       lncm[:, k, :], ALU.mult, ALU.add)
                i1 = v.tensor_copy(lnxx[:, 8 + k, :], xreg[:, k * 128:(k + 1) * 128])
            p.inc(i1, "sv")
        block.vector(s_mix)
        MIX_SV = p.val("sv")
        mile[("mix", i)] = MIX_SV

        def s_proj(t, i=i):
            p.wait(t, "sv", MIX_SV)
            p.wait(t, "ss", trevs[7])
            if i == 0:
                p.wait(t, "dw")
                p.wait(t, "ss", SS_BC)
            else:
                p.wait(t, "ss", mile[("projfree", i - 1)])
                p.wait(t, "sv", mile[("projfreev", i - 1)])
                p.wait(t, "ss", mile[("lorafree", i - 1)])
            for pi, (mix, wname) in enumerate(((mixr, "Wr"), (mixk, "Wk"), (mixv, "Wv"))):
                b0, b1 = pb[2 * pi], pb[2 * pi + 1]
                m0 = m1 = None
                for k in range(8):
                    m0 = t.matmul(b0[:], mix[:, k, :], Ws[wname][:, k, 0:512],
                                  start=(k == 0), stop=(k == 7))
                    m1 = t.matmul(b1[:], mix[:, k, :], Ws[wname][:, k, 512:1024],
                                  start=(k == 0), stop=(k == 7))
                p.inc(m0, "tt")
                p.inc(m1, "tt")
            mlast = None
            for k in range(16):
                mlast = t.matmul(pb[6][:, 0:256], lnxx[:, k, :], lora1[:, k, :],
                                 start=(k == 0), stop=(k == 15))
            p.inc(mlast, "tt")
            mile[("projdone", i)] = p.val("tt")
        block.tensor(s_proj)
        TT_PROJ = p.val("tt")

        def s_pev(sc, i=i):
            p.wait(sc, "tt", TT_PROJ - 5)
            sc.copy(r_s[:, 0:512], pb[0][:])
            sc.copy(r_s[:, 512:1024], pb[1][:])
            p.wait(sc, "tt", TT_PROJ - 3)
            sc.copy(k_s[:, 0:512], pb[2][:])
            i1 = sc.copy(k_s[:, 512:1024], pb[3][:])
            mile[("projfree", i)] = p.inc(i1, "ss")
            p.wait(sc, "tt", TT_PROJ)
            sc.activation(win_b[:], pb[6][:, 0:64], AF.Tanh)
            sc.copy(ain_b[:], pb[6][:, 64:128])
            i1 = sc.activation(gin_b[:], pb[6][:, 128:256], AF.Sigmoid)
            mile[("lorafree", i)] = p.inc(i1, "ss")
        block.scalar(s_pev)
        SS_LORIN = p.val("ss")

        def s_pev_v(v, i=i):
            p.wait(v, "tt", TT_PROJ - 1)
            v.tensor_copy(v_s[:, 0:512], pb[4][:])
            i1 = v.tensor_copy(v_s[:, 512:1024], pb[5][:])
            mile[("projfreev", i)] = p.inc(i1, "sv")
        block.vector(s_pev_v)

        def s_l2(t):
            p.wait(t, "ss", SS_LORIN)
            i1 = t.transpose(pb[6][0:64, 0:128], win_b[:], identf[:])
            p.inc(i1, "tt")
            i2 = t.transpose(pb[7][0:64, 0:128], ain_b[:], identf[:])
            p.inc(i2, "tt")
            i3 = t.transpose(pb[6][:, 128:256], gin_b[:], identf[:])
            p.inc(i3, "tt")
        block.tensor(s_l2)
        TT_LT = p.val("tt")

        def s_l2ev(v):
            p.wait(v, "tt", TT_LT)
            v.tensor_copy(winT[:], pb[6][0:64, 0:128])
            v.tensor_copy(ainT[:], pb[7][0:64, 0:128])
            i1 = v.tensor_copy(ginT[:], pb[6][:, 128:256])
            p.inc(i1, "sv")
        block.vector(s_l2ev)
        SV_LT = p.val("sv")

        def s_l3(t, i=i):
            p.wait(t, "sv", SV_LT)
            for (lhsT, rhs, bk) in ((winT, w2s, 0), (ainT, a2s, 2), (ginT, g2s, 4)):
                m = t.matmul(pb[bk][:], lhsT[:], rhs[:, 0:512], start=True, stop=True)
                p.inc(m, "tt")
                m = t.matmul(pb[bk + 1][:], lhsT[:], rhs[:, 512:1024], start=True, stop=True)
                p.inc(m, "tt")
        block.tensor(s_l3)
        TT_L3 = p.val("tt")

        def s_wag_v(v, i=i):
            p.wait(v, "tt", TT_L3)
            if i > 0:
                v.wait_ge(p.c["dg"][0], mile[("xmdma", i - 1)])
            v.tensor_add(t1_s[:, 0:512], pb[0][:], w0b[:, 0:512])
            v.tensor_add(t1_s[:, 512:1024], pb[1][:], w0b[:, 512:1024])
            v.tensor_add(t2_s[:, 0:512], pb[2][:], a0b[:, 0:512])
            v.tensor_add(t2_s[:, 512:1024], pb[3][:], a0b[:, 512:1024])
            v.tensor_copy(g_s[:, 0:512], pb[4][:])
            i1 = v.tensor_copy(g_s[:, 512:1024], pb[5][:])
            p.inc(i1, "sv")
        block.vector(s_wag_v)
        SV_WAG = p.val("sv")

        def s_wag_s(sc):
            p.wait(sc, "sv", SV_WAG)
            sc.activation(lwp[:], t1_s[:], AF.Exp, scale=-1.0)
            sc.activation(t1_s[:], lwp[:], AF.Ln, bias=cpool[:, 2:3])
            sc.activation(lwp[:], t1_s[:], AF.Exp, scale=-1.0, bias=cpool[:, 1:2])
            i1 = sc.activation(asig[:], t2_s[:], AF.Sigmoid)
            p.inc(i1, "ss")
        block.scalar(s_wag_s)
        SS_WAG = p.val("ss")

        def s_sp1(v):
            p.wait(v, "ss", SS_WAG)
            v.tensor_mul(t1_s[:], k_s[:], kkb[:])
            v.tensor_mul(t2_s[:], t1_s[:], t1_s[:])
            v.reduce_sum(ss16[:], t2_s[:].rearrange("q (h n) -> q h n", h=H), axis=AX.X)
            p.inc(v.drain(), "sv")
        block.vector(s_sp1)

        def s_sp1s(sc):
            p.wait(sc, "sv")
            sc.activation(rn16[:], ss16[:], AF.Sqrt)
            p.inc(sc.drain(), "ss")
        block.scalar(s_sp1s)

        def s_sp2(v):
            p.wait(v, "ss")
            v.tensor_scalar_max(rn16[:], rn16[:], 1e-12)
            v.drain()
            v.reciprocal(ss16[:], rn16[:])
            v.drain()
            bc16 = ss16[:].rearrange("q (h o) -> q h o", o=1).broadcast_to((128, H, N))
            v.tensor_tensor(kkn[:].rearrange("q (h n) -> q h n", h=H), t1_s[:].rearrange("q (h n) -> q h n", h=H), bc16, ALU.mult)
            v.tensor_mul(bsc[:], kkn[:], asig[:])
            v.tensor_mul(t2_s[:], asig[:], kab[:])
            v.tensor_add(t2_s[:], t2_s[:], omkab[:])
            v.tensor_mul(kt_s[:], k_s[:], t2_s[:])
            i1 = v.tensor_copy(v_b[:], v_s[:])
            p.inc(i1, "sv")
        block.vector(s_sp2)
        SV_PREP = p.val("sv")

        tt1 = p.val("tt")
        sv1 = p.val("sv")
        lwev = [sv1 + k + 1 for k in range(8)]

        def s_ct1(t, tt1=tt1, lwev=lwev):
            p.wait(t, "sv", SV_PREP)
            for k in range(8):
                if k >= 2:
                    p.wait(t, "sv", lwev[k - 2])
                i1 = t.transpose(pb[2 + (k % 2)][:, 0:128], lwp[:, k * 128:(k + 1) * 128], identf[:])
                p.inc(i1, "tt")
        block.tensor(s_ct1)

        for k in range(8):
            def s_lwev(v, k=k, tt1=tt1):
                p.wait(v, "tt", tt1 + k + 1)
                i1 = v.tensor_tensor_scan(cum[:, k, :], pb[2 + (k % 2)][:, 0:128],
                                           zer_f[:], 0.0,
                                           ALU.add, ALU.bypass)
                assert p.inc(i1, "sv") == lwev[k]
            block.vector(s_lwev)

        def s_dexp(sc):
            p.wait(sc, "sv", lwev[7])
            sc.activation(dd[:].rearrange("q a b -> q (a b)"),
                          cum[:].rearrange("q a b -> q (a b)"), AF.Exp, scale=-1.0)
            sc.activation(ddi[:].rearrange("q a b -> q (a b)"),
                          cum[:].rearrange("q a b -> q (a b)"), AF.Exp)
            sc.activation(dLf[:], cum[:, :, 127], AF.Exp, scale=-1.0)
            i1 = None
            for k in range(8):
                i1 = sc.mul(asig[:, k * 128 + 1:k * 128 + 128], dd[:, k, 0:127], -1.0)
            p.inc(i1, "ss")
        block.scalar(s_dexp)

        def s_dsh0(v):
            p.wait(v, "ss")
            v.memset(asig[:].rearrange("q (a b) -> q a b", a=8)[:, :, 0:1], -1.0)
            v.tensor_copy(dLf[:], cumf_dL[:]) if False else None
            i1 = None
            for k in range(8):
                i1 = v.tensor_scalar_mul(dldc[:, k, :], ddi[:, k, :], dLf[:, k:k + 1])
            p.inc(i1, "sv")
        block.vector(s_dsh0)
        SV_DLD = p.val("sv")

        tt2 = p.val("tt")
        sv2 = p.val("sv")
        ctev = [sv2 + j + 1 for j in range(40)]

        def s_ct2(t, tt2=tt2, ctev=ctev):
            p.wait(t, "sv", SV_DLD)
            srcs = [kkn, bsc, kt_s, r_s, None]
            for si in range(5):
                for k in range(8):
                    idx = si * 8 + k
                    if idx >= 2:
                        p.wait(t, "sv", ctev[idx - 2])
                    if si == 4:
                        i1 = t.transpose(pb[2 + (idx % 2)][:, 0:128], dldc[:, k, :], identf[:])
                    else:
                        i1 = t.transpose(pb[2 + (idx % 2)][:, 0:128],
                                         srcs[si][:, k * 128:(k + 1) * 128], identf[:])
                    p.inc(i1, "tt")
        block.tensor(s_ct2)

        specs = ((None, None), (bh_c, ddi), (kh_c, ddi), (rt_c, dd), (None, None))
        for si in range(5):
            for k in range(8):
                idx = si * 8 + k
                def s_ctev(v, idx=idx, k=k, si=si):
                    dst, mulsrc = specs[si]
                    p.wait(v, "tt", tt2 + idx + 1)
                    bank = pb[2 + (idx % 2)]
                    if si == 4:
                        v.tensor_tensor(btl[:, k * 128:(k + 1) * 128], bank[:, 0:128],
                                        bsc[:, k * 128:(k + 1) * 128], ALU.mult)
                        i1 = v.tensor_tensor(ktl[:, k * 128:(k + 1) * 128], bank[:, 0:128],
                                             kt_s[:, k * 128:(k + 1) * 128], ALU.mult)
                    elif si == 0:
                        i1 = v.tensor_tensor(lnat[:, k, :], bank[:, 0:128],
                                             asig[:, k * 128:(k + 1) * 128], ALU.mult)
                    else:
                        i1 = v.tensor_tensor(dst[:, k, :], bank[:, 0:128], mulsrc[:, k, :], ALU.mult)
                    assert p.inc(i1, "sv") == ctev[idx]
                block.vector(s_ctev)
        SV_CT = p.val("sv")

        for hp in range(0, H, 2):
            pairs = (hp, hp + 1)
            ctx_h = {}
            for h in pairs:
                slot = b * H + h
                srow = 64 * (slot % 2)
                scol = (slot // 2) * 64
                hp_r = 64 * (h % 2)
                hch = h // 2
                ctx_h[h] = dict(
                    w=ws[h % 2],
                    BQ=[pb[4 * (h % 2) + j] for j in range(4)],
                    stbS=stb2[srow:srow + 64, scol:scol + 64],
                    stfS=stf2[srow:srow + 64, scol:scol + 64],
                    atS=lnat[hp_r:hp_r + 64, hch, :],
                    bhS=bh_c[hp_r:hp_r + 64, hch, :],
                    khS=kh_c[hp_r:hp_r + 64, hch, :],
                    rtS=rt_c[hp_r:hp_r + 64, hch, :],
                    vS=v_b[:, h * 64:(h + 1) * 64],
                    btlS=btl[:, h * 64:(h + 1) * 64],
                    ktlS=ktl[:, h * 64:(h + 1) * 64],
                    dLS=dLf[hp_r:hp_r + 64, hch:hch + 1],
                )

            for h in pairs:
                def f_ta(t, h=h, i=i, cx=ctx_h[h]):
                    if h < 2:
                        p.wait(t, "sv", SV_CT)
                    else:
                        p.wait(t, "sv", mile[("scanev", i, h - 2)])
                    BQ = cx["BQ"]
                    p.inc(t.matmul(BQ[0][:, 0:128], cx["bhS"], cx["atS"], start=True, stop=True), "tt")
                    p.inc(t.matmul(BQ[0][:, 128:256], cx["bhS"], cx["rtS"], start=True, stop=True), "tt")
                    p.inc(t.matmul(BQ[1][:, 0:128], cx["khS"], cx["atS"], start=True, stop=True), "tt")
                    p.inc(t.matmul(BQ[1][:, 128:256], cx["khS"], cx["rtS"], start=True, stop=True), "tt")
                    p.inc(t.matmul(BQ[2][:, 0:128], cx["atS"], cx["bhS"], start=True, stop=True), "tt")
                    mile[("ttA", i, h)] = p.val("tt")
                block.tensor(f_ta)
            for h in pairs:
                def f_bev(v, h=h, i=i, cx=ctx_h[h]):
                    p.wait(v, "tt", mile[("ttA", i, h)])
                    w, BQ = cx["w"], cx["BQ"]
                    v.select(w["AT"], MSU, BQ[0][:, 0:128], zer_f[:])
                    v.select(w["F1T"], MSU, BQ[0][:, 0:128], identf[:])
                    v.select(w["GbT"], MIU, BQ[0][:, 128:256], zer_f[:])
                    v.select(w["CT"], MSU, BQ[1][:, 0:128], zer_f[:])
                    v.select(w["GkT"], MIU, BQ[1][:, 128:256], zer_f[:])
                    i1 = v.select(w["A"], MSL, BQ[2][:, 0:128], zer_f[:])
                    mile[("bev", i, h)] = p.inc(i1, "sv")
                block.vector(f_bev)
            for lvl in range(6):
                for h in pairs:
                    def f_sqt(t, h=h, i=i, lvl=lvl, cx=ctx_h[h]):
                        w, BQ = cx["w"], cx["BQ"]
                        if lvl == 0:
                            p.wait(t, "sv", mile[("bev", i, h)])
                        else:
                            p.wait(t, "sv", mile[("sqs", i, h, lvl - 1)])
                        if lvl >= 2:
                            p.wait(t, "sv", mile[("sqF", i, h, lvl - 2)])
                        bq = lvl % 2
                        p.inc(t.matmul(BQ[bq][:, 256:384], w["AT"], w["A"], start=True, stop=True), "tt")
                        if lvl < 5:
                            p.inc(t.matmul(BQ[bq][:, 384:512], w["A"], w["AT"], start=True, stop=True), "tt")
                        mile[("ttsq", i, h, lvl)] = p.val("tt")
                    block.tensor(f_sqt)
                for h in pairs:
                    def f_sqf(v, h=h, i=i, lvl=lvl, cx=ctx_h[h]):
                        p.wait(v, "tt", mile[("ttsq", i, h, lvl)])
                        w, BQ = cx["w"], cx["BQ"]
                        fdst = ("F2", "F3", "F4", "F5", "F6", "F7")[lvl]
                        i1 = v.select(w[fdst], MSL, BQ[lvl % 2][:, 256:384], identf[:])
                        mile[("sqF", i, h, lvl)] = p.inc(i1, "sv")
                    block.vector(f_sqf)
                if lvl < 5:
                    for h in pairs:
                        def f_sqs(v, h=h, i=i, lvl=lvl, cx=ctx_h[h]):
                            p.wait(v, "tt", mile[("ttsq", i, h, lvl)])
                            w, BQ = cx["w"], cx["BQ"]
                            v.tensor_copy(w["A"], BQ[lvl % 2][:, 256:384])
                            i1 = v.tensor_copy(w["AT"], BQ[lvl % 2][:, 384:512])
                            mile[("sqs", i, h, lvl)] = p.inc(i1, "sv")
                        block.vector(f_sqs)
            for j in range(6):
                for h in pairs:
                    def f_tq(t, h=h, i=i, j=j, cx=ctx_h[h]):
                        w, BQ = cx["w"], cx["BQ"]
                        if j == 0:
                            p.wait(t, "sv", mile[("sqF", i, h, 0)])
                            m = t.matmul(BQ[2][:, 128:256], w["F2"], w["F1T"], start=True, stop=True)
                        else:
                            p.wait(t, "sv", mile[("sqF", i, h, j)])
                            p.wait(t, "sv", mile[("q", i, h, j - 1)])
                            fn = ("F3", "F4", "F5", "F6", "F7")[j - 1]
                            m = t.matmul(BQ[2][:, 128:256], w[fn], w["Q%d" % ((j - 1) % 2)],
                                         start=True, stop=True)
                        p.inc(m, "tt")
                        mile[("ttq", i, h, j)] = p.val("tt")
                    block.tensor(f_tq)
                for h in pairs:
                    def f_sq(v, h=h, i=i, j=j, cx=ctx_h[h]):
                        p.wait(v, "tt", mile[("ttq", i, h, j)])
                        w = cx["w"]
                        dst = "TT" if j == 5 else ("Q%d" % (j % 2))
                        i1 = v.tensor_copy(w[dst], cx["BQ"][2][:, 128:256])
                        mile[("q", i, h, j)] = p.inc(i1, "sv")
                    block.vector(f_sq)
            for h in pairs:
                def f_tcv(t, h=h, i=i, cx=ctx_h[h]):
                    p.wait(t, "sv", mile[("q", i, h, 5)])
                    p.inc(t.matmul(cx["BQ"][2][:, 0:64], cx["w"]["CT"], cx["vS"],
                                   start=True, stop=True), "tt")
                    mile[("ttcv", i, h)] = p.val("tt")
                block.tensor(f_tcv)
            for h in pairs:
                def f_scv(v, h=h, i=i, cx=ctx_h[h]):
                    p.wait(v, "tt", mile[("ttcv", i, h)])
                    i1 = v.tensor_copy(cx["w"]["CV"], cx["BQ"][2][:, 0:64])
                    mile[("cv", i, h)] = p.inc(i1, "sv")
                block.vector(f_scv)
            if chunk > 0:
                for h in pairs:
                    def f_tm0(t, h=h, i=i, cx=ctx_h[h]):
                        p.wait(t, "sv", mile[("cv", i, h)])
                        p.inc(t.matmul(cx["BQ"][3][:, 0:64], cx["atS"], cx["stbS"],
                                       start=True, stop=True), "tt")
                        mile[("ttm0", i, h)] = p.val("tt")
                    block.tensor(f_tm0)
                for h in pairs:
                    def f_sm0(v, h=h, i=i, cx=ctx_h[h]):
                        p.wait(v, "tt", mile[("ttm0", i, h)])
                        i1 = v.tensor_copy(cx["w"]["M0"], cx["BQ"][3][:, 0:64])
                        mile[("m0", i, h)] = p.inc(i1, "sv")
                    block.vector(f_sm0)
            for h in pairs:
                def f_tu(t, h=h, i=i, chunk=chunk, cx=ctx_h[h]):
                    w, BQ = cx["w"], cx["BQ"]
                    if chunk > 0:
                        p.wait(t, "sv", mile[("m0", i, h)])
                        m = t.matmul(BQ[3][:, 64:128], w["TT"], w["CV"], start=True, stop=False)
                        m = t.matmul(BQ[3][:, 64:128], w["TT"], w["M0"], start=False, stop=True)
                    else:
                        p.wait(t, "sv", mile[("cv", i, h)])
                        m = t.matmul(BQ[3][:, 64:128], w["TT"], w["CV"], start=True, stop=True)
                    p.inc(m, "tt")
                    mile[("ttu", i, h)] = p.val("tt")
                block.tensor(f_tu)
            for h in pairs:
                def f_su(v, h=h, i=i, cx=ctx_h[h]):
                    p.wait(v, "tt", mile[("ttu", i, h)])
                    i1 = v.tensor_copy(cx["w"]["U"], cx["BQ"][3][:, 64:128])
                    mile[("u", i, h)] = p.inc(i1, "sv")
                block.vector(f_su)
            for h in pairs:
                def f_to(t, h=h, i=i, chunk=chunk, cx=ctx_h[h]):
                    w, BQ = cx["w"], cx["BQ"]
                    p.wait(t, "sv", mile[("u", i, h)])
                    if chunk > 0:
                        m = t.matmul(BQ[3][:, 128:192], cx["rtS"], cx["stbS"], start=True, stop=False)
                        m = t.matmul(BQ[3][:, 128:192], w["GbT"], w["U"], start=False, stop=False)
                    else:
                        m = t.matmul(BQ[3][:, 128:192], w["GbT"], w["U"], start=True, stop=False)
                    m = t.matmul(BQ[3][:, 128:192], w["GkT"], cx["vS"], start=False, stop=True)
                    p.inc(m, "tt")
                    m = t.matmul(BQ[3][0:64, 192:256], cx["btlS"], w["U"], start=True, stop=False)
                    m = t.matmul(BQ[3][0:64, 192:256], cx["ktlS"], cx["vS"], start=False, stop=True)
                    p.inc(m, "tt")
                    mile[("ttst", i, h)] = p.val("tt")
                block.tensor(f_to)
            for h in pairs:
                def f_fin(v, h=h, i=i, chunk=chunk, cx=ctx_h[h]):
                    p.wait(v, "tt", mile[("ttst", i, h)])
                    BQ = cx["BQ"]
                    v.tensor_copy(o_s[:, h * 64:(h + 1) * 64], BQ[3][:, 128:192])
                    if chunk > 0:
                        v.scalar_tensor_tensor(cx["stfS"], cx["stfS"], cx["dLS"],
                                               BQ[3][0:64, 192:256], ALU.mult, ALU.add)
                    else:
                        v.tensor_copy(cx["stfS"], BQ[3][0:64, 192:256])
                    i1 = v.tensor_copy(cx["stbS"], cx["stfS"])
                    mile[("scanev", i, h)] = p.inc(i1, "sv")
                block.vector(f_fin)
        SV_SCANEND = p.val("sv")

        if debug and "o" in debug:
            def s_dbgo(sy, i=i):
                p.wait(sy, "sv", SV_SCANEND)
                sy.dma_start(out=dbg_d["o"][i * L:(i + 1) * L, :], in_=o_s[:]).then_inc(p.c["dbgs"][0], 16)
                p.c["dbgs"][1] += 16
            block.sync(s_dbgo)
            DBGO = p.val("dbgs")

            def s_dbgow(v, DBGO=DBGO):
                v.wait_ge(p.c["dbgs"][0], DBGO)
            block.vector(s_dbgow)

        def s_epi_v(v):
            p.wait(v, "sv", SV_SCANEND)
            v.tensor_mul(t1_s[:], o_s[:], o_s[:])
            v.reduce_sum(ss16[:], t1_s[:].rearrange("q (h n) -> q h n", h=H), axis=AX.X)
            v.reduce_sum(rn16[:], o_s[:].rearrange("q (h n) -> q h n", h=H), axis=AX.X)
            v.drain()
            v.tensor_scalar_mul(rn16[:], rn16[:], 1.0 / N)
            v.tensor_scalar_mul(ss16[:], ss16[:], 1.0 / N)
            v.drain()
            v.tensor_mul(t2_s[:, 0:16], rn16[:], rn16[:])
            v.drain()
            v.tensor_sub(ss16[:], ss16[:], t2_s[:, 0:16])
            p.inc(v.drain(), "sv")
        block.vector(s_epi_v)

        def s_epi_s(sc):
            p.wait(sc, "sv")
            sc.activation(t2_s[:, 16:32], ss16[:], AF.Ln, bias=cpool[:, 3:4])
            sc.drain()
            sc.activation(ss16[:], t2_s[:, 16:32], AF.Exp, scale=-0.5)
            p.inc(sc.drain(), "ss")
        block.scalar(s_epi_s)

        def s_epi_v2(v):
            p.wait(v, "ss")
            bc = lambda a: a.rearrange("q (h o) -> q h o", o=1).broadcast_to((128, H, N))
            h3 = lambda a: a.rearrange("q (h n) -> q h n", h=H)
            v.tensor_tensor(h3(y_s[:]), h3(o_s[:]), bc(rn16[:]), ALU.subtract)
            v.tensor_tensor(h3(y_s[:]), h3(y_s[:]), bc(ss16[:]), ALU.mult)
            v.tensor_mul(y_s[:], y_s[:], gab[:])
            v.tensor_add(y_s[:], y_s[:], beb[:])
            v.tensor_mul(t1_s[:], r_s[:], kt_s[:])
            v.tensor_mul(t1_s[:], t1_s[:], rkb[:])
            v.reduce_sum(rn16[:], t1_s[:].rearrange("q (h n) -> q h n", h=H), axis=AX.X)
            v.drain()
            v.tensor_tensor(h3(t2_s[:]), h3(v_s[:]), bc(rn16[:]), ALU.mult)
            v.tensor_add(y_s[:], y_s[:], t2_s[:])
            i1 = v.tensor_mul(y_s[:], y_s[:], g_s[:])
            p.inc(i1, "sv")
        block.vector(s_epi_v2)
        SV_Y = p.val("sv")

        tt3 = p.val("tt")
        ss3 = p.val("ss")
        yev = [ss3 + k + 1 for k in range(8)]

        def s_wo_t(t, tt3=tt3, yev=yev):
            p.wait(t, "sv", SV_Y)
            for k in range(8):
                if k >= 2:
                    p.wait(t, "ss", yev[k - 2])
                i1 = t.transpose(pb[2 + (k % 2)][:, 128:256], y_s[:, k * 128:(k + 1) * 128], identf[:])
                p.inc(i1, "tt")
        block.tensor(s_wo_t)

        for k in range(8):
            def s_yev(sc, k=k, tt3=tt3):
                p.wait(sc, "tt", tt3 + k + 1)
                sc.copy(vyc[:, k * 128:(k + 1) * 128], pb[2 + (k % 2)][:, 128:256])
                sc.drain()
                assert p.inc(sc.nop(), "ss") == yev[k]
            block.scalar(s_yev)

        def s_wo_mm(t):
            p.wait(t, "ss", yev[7])
            m0 = m1 = None
            for k in range(8):
                m0 = t.matmul(pb[0][:], vyc[:, k * 128:(k + 1) * 128], Ws["Wo"][:, k, 0:512],
                              start=(k == 0), stop=(k == 7))
                m1 = t.matmul(pb[1][:], vyc[:, k * 128:(k + 1) * 128], Ws["Wo"][:, k, 512:1024],
                              start=(k == 0), stop=(k == 7))
            p.inc(m0, "tt")
            p.inc(m1, "tt")
        block.tensor(s_wo_mm)
        TT_WO = p.val("tt")

        def s_fin_s(sc):
            p.wait(sc, "tt", TT_WO)
            sc.copy(ao_s[:, 0:512], pb[0][:])
            sc.copy(ao_s[:, 512:1024], pb[1][:])
            sc.drain()
            sc.activation(sqs[:], ao_s[:], AF.Square, accum_out=st[:, 4:5])
            sc.activation(st[:, 5:6], st[:, 4:5], AF.Copy, scale=INV_C)
            p.inc(sc.drain(), "ss")
        block.scalar(s_fin_s)

        def s_fin_v(v):
            p.wait(v, "ss")
            v.reduce_sum(st[:, 0:1], ao_s[:], axis=AX.X)
            v.drain()
            v.tensor_scalar_mul(st[:, 2:3], st[:, 0:1], INV_C)
            v.drain()
            v.tensor_mul(st[:, 1:2], st[:, 2:3], st[:, 2:3])
            v.drain()
            v.tensor_sub(st[:, 3:4], st[:, 5:6], st[:, 1:2])
            p.inc(v.drain(), "sv")
        block.vector(s_fin_v)

        def s_fin_s2(sc):
            p.wait(sc, "sv")
            sc.activation(st[:, 7:8], st[:, 3:4], AF.Ln, bias=cpool[:, 0:1])
            sc.drain()
            sc.activation(st[:, 6:7], st[:, 7:8], AF.Exp, scale=-0.5)
            p.inc(sc.drain(), "ss")
        block.scalar(s_fin_s2)

        def s_fin_v2(v, par=par, i=i):
            p.wait(v, "ss")
            if i >= 2:
                v.wait_ge(p.c[don[par]][0], mile[("outdma", i - 2)])
            v.tensor_scalar(t1_s[:], ao_s[:], st[:, 2:3], st[:, 6:7],
                            ALU.subtract, ALU.mult)
            i1 = v.tensor_add(x1_s[:, par, :], xt[:, par, :], t1_s[:])
            mile[("xfree", i)] = p.inc(i1, "sv")
        block.vector(s_fin_v2)

        def s_fin_s3(sc, par=par):
            p.wait(sc, "sv")
            sc.activation(sqs[:], x1_s[:, par, :], AF.Square, accum_out=st[:, 4:5])
            sc.activation(st[:, 5:6], st[:, 4:5], AF.Copy, scale=INV_C)
            p.inc(sc.drain(), "ss")
        block.scalar(s_fin_s3)

        def s_fin_v3(v, par=par):
            p.wait(v, "ss")
            v.reduce_sum(st[:, 0:1], x1_s[:, par, :], axis=AX.X)
            v.drain()
            v.tensor_scalar_mul(st[:, 2:3], st[:, 0:1], INV_C)
            v.drain()
            v.tensor_mul(st[:, 1:2], st[:, 2:3], st[:, 2:3])
            v.drain()
            v.tensor_sub(st[:, 3:4], st[:, 5:6], st[:, 1:2])
            p.inc(v.drain(), "sv")
        block.vector(s_fin_v3)

        def s_fin_s4(sc):
            p.wait(sc, "sv")
            sc.activation(st[:, 7:8], st[:, 3:4], AF.Ln, bias=cpool[:, 0:1])
            sc.drain()
            sc.activation(st[:, 6:7], st[:, 7:8], AF.Exp, scale=-0.5)
            p.inc(sc.drain(), "ss")
        block.scalar(s_fin_s4)

        def s_fin_v4(v, par=par, i=i):
            p.wait(v, "ss")
            v.tensor_scalar(xm_s[:], x1_s[:, par, :], st[:, 2:3], st[:, 6:7],
                            ALU.subtract, ALU.mult)
            v.reduce_sum(n1acc[:, i:i + 1], xm_s[:], axis=AX.X, apply_absolute_value=True)
            p.inc(v.drain(), "sv")
        block.vector(s_fin_v4)

        def s_fin_s5(sc, i=i):
            p.wait(sc, "sv")
            sc.activation(sqs[:], xm_s[:], AF.Square, accum_out=st[:, 8:9])
            sc.drain()
            sc.activation(impacc[:, i:i + 1], st[:, 8:9], AF.Sqrt)
            mile[("outready", i)] = p.inc(sc.drain(), "ss")
        block.scalar(s_fin_s5)

        def s_out(s, i=i, par=par):
            p.wait(s, "ss", mile[("outready", i)])
            s.dma_start(out=x1_d[i * L:(i + 1) * L, :], in_=x1_s[:, par, :]).then_inc(p.c[don[par]][0], 16)
            p.c[don[par]][1] += 16
            mile[("outdma", i)] = p.c[don[par]][1]
        block.sync(s_out)

        def s_outg(g, i=i):
            p.wait(g, "ss", mile[("outready", i)])
            g.dma_start(out=xm_d[i * L:(i + 1) * L, :], in_=xm_s[:]).then_inc(p.c["dg"][0], 16)
            p.c["dg"][1] += 16
            mile[("xmdma", i)] = p.c["dg"][1]
        block.gpsimd(s_outg)

        if debug:
            dbg_src = {"r": r_s, "kt": kt_s, "v": v_s, "lwpos": lwp, "asig": asig,
                       "g": g_s, "y": y_s, "kkn": kkn, "bscan": bsc,
                       "ln0": ln0, "ao": ao_s, "xm": xm_s}

            def s_dbg(s, i=i):
                p.wait(s, "ss", mile[("outready", i)])
                for nm in [d for d in debug if d != "o"]:
                    s.dma_start(out=dbg_d[nm][i * L:(i + 1) * L, :], in_=dbg_src[nm][:]).then_inc(p.c["dbgs"][0], 16)
                    p.c["dbgs"][1] += 16
            block.sync(s_dbg)
            DBGV = p.val("dbgs")

            def s_dbgwait(v, DBGV=DBGV):
                v.wait_ge(p.c["dbgs"][0], DBGV)
            block.vector(s_dbgwait)

    def s_tail(s):
        p.wait(s, "ss", mile[("outready", TILES - 1)])
        s.dma_start(out=imp_d[:], in_=impacc[:]).then_inc(p.c["dm"][0], 16)
        s.dma_start(out=n1_d[:], in_=n1acc[:]).then_inc(p.c["dm"][0], 16)
        p.c["dm"][1] += 32
        s.wait_ge(p.c["do0"][0], p.c["do0"][1])
        s.wait_ge(p.c["dg"][0], p.c["dg"][1])
        s.wait_ge(p.c["do1"][0], p.c["do1"][1])
        s.wait_ge(p.c["dm"][0], p.c["dm"][1])
        if debug:
            s.wait_ge(p.c["dbgs"][0], p.c["dbgs"][1])
    block.sync(s_tail)

    blk.__exit__(None, None, None)
    for g in reversed(ctx):
        g.__exit__(None, None, None)
    for g in p.ctxs:
        g.__exit__(None, None, None)
    return nc


def l1_inputs(x_core, inp):
    A = lambda k: np.ascontiguousarray(np.asarray(inp["L0_attn_" + k], dtype=np.float32))
    vecs = np.stack([A(k).reshape(-1) for k in
                     ("x_r", "x_k", "x_v", "x_w", "x_a", "x_g", "w0", "a0",
                      "k_k", "k_a", "r_k", "ln_gamma", "ln_beta")], axis=0)
    t = np.arange(128)[:, None]
    s = np.arange(128)[None, :]
    msl = (s < t).astype(np.uint8)
    masks = np.concatenate([msl, msl.T, (s <= t).astype(np.uint8).T], axis=1)
    return {
        "x": np.ascontiguousarray(x_core, dtype=np.float32),
        "Wr": A("Wr"), "Wk": A("Wk"), "Wv": A("Wv"), "Wo": A("Wo"),
        "w1": A("w1"), "w2": A("w2"), "a1": A("a1"), "a2": A("a2"),
        "g1": A("g1"), "g2": A("g2"),
        "vecs": np.ascontiguousarray(vecs),
        "identf": np.eye(128, dtype=np.float32),
        "masks": np.ascontiguousarray(masks),
    }







SLOT_FAM = [(0, 0), (1, 0), (2, 0), (0, 1), (1, 1), (2, 1), (0, 2), (1, 2)]
FAMW = {0: ((C, 4096), (4096, C)), 1: ((C, 2048), (2048, 2048), (2048, C)),
        2: ((C, 3072), (3072, C))}


def build_l2(T, n_all, kcap=1408, nbisect=36):
    TOK = B_CORE * T
    TILES = TOK // 128
    NTA = n_all // 128
    GT = kcap // 128 + 1
    NG = GT - 1
    assert (GT * 128) % 512 == 0
    TCH = GT * 128 // 512
    pos = 0.9 * (n_all - 1)
    k_lo = int(pos) + 1
    frac = float(np.float32(pos - int(pos)))
    BIG = float(2 ** 26)
    BIGI = 2 ** 26
    HB = TILES // 2

    nc = bass.Bass("TRN2", target_bir_lowering=False, detect_race_conditions=False)
    x1_d = nc.dram_tensor("x1", [TOK, C], F32, kind="ExternalInput").ap()
    xm_d = nc.dram_tensor("xm16", [TOK, C], BF16, kind="ExternalInput").ap()
    impo_d = nc.dram_tensor("imp_own", [128, TILES], F32, kind="ExternalInput").ap()
    n1o_d = nc.dram_tensor("n1_own", [128, TILES], F32, kind="ExternalInput").ap()
    impa_d = nc.dram_tensor("imp_all", [128, NTA], F32, kind="ExternalInput").ap()
    noise_d = nc.dram_tensor("noise", [128, TILES * 8], F32, kind="ExternalInput").ap()
    eu_d = nc.dram_tensor("eu", [1, 8], F32, kind="ExternalInput").ap()
    gw1_d = nc.dram_tensor("gw1", [C, C], F32, kind="ExternalInput").ap()
    gw2_d = nc.dram_tensor("gw2", [C, 512], F32, kind="ExternalInput").ap()
    gw3_d = nc.dram_tensor("gw3", [512, 8], F32, kind="ExternalInput").ap()
    aw_d = nc.dram_tensor("aw", [C, 8], F32, kind="ExternalInput").ap()
    gvec_d = nc.dram_tensor("gvec", [4, C], F32, kind="ExternalInput").ap()
    gsml_d = nc.dram_tensor("gsml", [2, 512], F32, kind="ExternalInput").ap()
    suo_d = nc.dram_tensor("suo", [128, 128], F32, kind="ExternalInput").ap()
    identf_d = nc.dram_tensor("identf", [128, 128], F32, kind="ExternalInput").ap()
    ew_d = {}
    eb_d = {}
    for f in (0, 1, 2):
        nexp = 3 if f < 2 else 2
        for li, sh in enumerate(FAMW[f]):
            ew_d[(f, li)] = nc.dram_tensor(f"e{f}w{li + 1}", [nexp] + list(sh), F32,
                                           kind="ExternalInput").ap()
            eb_d[(f, li)] = nc.dram_tensor(f"e{f}b{li + 1}", [nexp, sh[1]], F32,
                                           kind="ExternalInput").ap()

    y_d = nc.dram_tensor("y", [TOK, C], F32, kind="ExternalOutput").ap()
    xs_dram = nc.dram_tensor("xsd", [TOK, C + 8], BF16, kind="Internal").ap()
    comb_dram = nc.dram_tensor("combd", [TOK, C], F32, kind="Internal").ap()

    ctx = []

    def sb(name, shape, dtype=F32):
        g = nc.sbuf_tensor(name, shape, dtype)
        t = g.__enter__()
        ctx.append(g)
        return t

    identf = sb("identf_s", [128, 128], F32)
    onesf = sb("onesf", [128, 128], F32)
    cpool = sb("cpool", [128, 8], F32)
    msku = sb("msku", [128, TILES], U8)
    probs = sb("probs", [128, TILES, 8], F32)
    prg = sb("prg", [128, GT, 8], F32)
    idxg = sb("idxg", [128, NG], I32)
    XgT = sb("XgT", [128, 8, GT * 128], BF16)
    comb = sb("comb", [128, GT, C], F32)
    Cbf = sb("Cbf", [8, C], BF16)
    prT = sb("prT", [8, 128], BF16)
    st = sb("st", [128, 16], F32)
    eurow = sb("eurow", [1, 16], F32)

    s2 = []

    def sb2(name, shape, dtype=F32):
        g = nc.sbuf_tensor(name, shape, dtype)
        t = g.__enter__()
        s2.append(g)
        return t
    zer_f = sb2("zer_f", [128, 128], F32)
    suo = sb2("suo_s", [128, 128], F32)
    impa = sb2("impa", [128, NTA], F32)
    tmpa = sb2("tmpa", [128, NTA], F32)
    mska = sb2("mska", [128, NTA], U8)
    bis = sb2("bis", [128, 24], F32)
    impt = sb2("impt", [128, TILES], F32)
    n1t = sb2("n1t", [128, TILES], F32)
    mskf = sb2("mskf", [128, TILES], F32)
    scl = sb2("scl", [128, TILES], F32)
    rnk = sb2("rnk", [128, TILES], F32)
    tt2b = sb2("tt2b", [128, TILES], F32)
    srbf = sb2("srbf", [128, kcap], F32)
    iotari = sb2("iotari", [128, kcap], I32)
    eqc = sb2("eqc", [128, 2, kcap], BF16)
    mov = sb2("mov", [128, 2 * TILES], BF16)
    movi = sb2("movi", [128, 2 * TILES], I32)
    sloti = sb2("sloti", [128, NG], I32)
    slotf = sb2("slotf", [128, NG], F32)
    idxacc = sb2("idxacc", [128, 2 * NG], F32)
    idxf = sb2("idxf", [128, NG], F32)
    af_acc = sb2("af_acc", [128, 16], F32)
    mskg = sb2("mskg", [128, NG], F32)
    bmf = sb2("bmf", [128, NG], F32)
    affd = sb2("affd", [128, 8], F32)
    afS = sb2("afS", [128, NG, 8], F32)
    nslot = sb2("nslot", [128, NG, 8], F32)
    lgs = sb2("lgs", [128, NG, 8], F32)
    lg2 = sb2("lg2", [128, NG, 8], F32)
    lga = sb2("lga", [128, TILES, 8], F32)
    lgb = sb2("lgb", [128, TILES, 8], F32)
    m1t = sb2("m1t", [128, TILES], F32)
    m2t = sb2("m2t", [128, TILES], F32)
    c1t = sb2("c1t", [128, TILES], F32)
    logc = sb2("logc", [128, 8], F32)
    affs = sb2("affs", [128, 16], F32)
    noi = sb2("noi", [128, TILES * 8], F32)

    s1 = []

    def sb1(name, shape, dtype=F32):
        g = nc.sbuf_tensor(name, shape, dtype)
        t = g.__enter__()
        s1.append(g)
        return t
    gb1r = sb1("gb1r", [1, C], F32)
    lnsr = sb1("lnsr", [1, C], F32)
    lnor = sb1("lnor", [1, C], F32)
    gb2r = sb1("gb2r", [1, 512], F32)
    gsmr = sb1("gsmr", [1, 16], F32)
    gw1s = sb1("gw1s", [128, 8, C], BF16)
    gw2s = sb1("gw2s", [128, 8, 512], BF16)
    gw3s = sb1("gw3s", [128, 4, 8], BF16)
    aws = sb1("aws", [128, 8, 8], BF16)
    gb1b = sb1("gb1b", [128, C], BF16)
    lnsb = sb1("lnsb", [128, C], BF16)
    lnob = sb1("lnob", [128, C], BF16)
    gb2b = sb1("gb2b", [128, 512], BF16)
    gb3b = sb1("gb3b", [128, 8], F32)
    abb = sb1("abb", [128, 8], F32)
    xgbf = sb1("xgbf", [128, C + 8], BF16)
    xtmp = sb1("xtmp", [128, C], F32)
    htmp = sb1("htmp", [128, C], F32)
    h2tmp = sb1("h2tmp", [128, 512], F32)
    sqs = sb1("sqs", [128, C], F32)
    hnT = sb1("hnT", [128, 8, 128], BF16)
    h2T = sb1("h2T", [128, 4, 128], BF16)
    xmtb = sb1("xmtb", [128, 2, C], BF16)
    xstb = sb1("xstb", [128, 2, C + 8], BF16)
    svb = sb1("svb", [128, 2, 2], BF16)
    ameanb = sb1("ameanb", [128, 16], BF16)

    pb = []
    for i in range(8):
        g = nc.psum_tensor(f"pb{i}", [128, 512], F32)
        pb.append(g.__enter__())
        ctx.append(g)

    blk = nc.Block()
    block = blk.__enter__()
    p = P(nc, ["dh", "dg", "do", "sv", "ss", "tt", "gi", "gp"])
    mile = {}

    def i_sync(s):
        s.dma_start(out=identf[:], in_=identf_d[:]).then_inc(p.c["dh"][0], 16)
        s.dma_start(out=suo[:], in_=suo_d[:]).then_inc(p.c["dh"][0], 16)
        s.dma_start(out=impa[:], in_=impa_d[:]).then_inc(p.c["dh"][0], 16)
        s.dma_start(out=impt[:], in_=impo_d[:]).then_inc(p.c["dh"][0], 16)
        s.dma_start(out=n1t[:], in_=n1o_d[:]).then_inc(p.c["dh"][0], 16)
        s.dma_start(out=noi[:], in_=noise_d[:]).then_inc(p.c["dh"][0], 16)
        s.dma_start(out=eurow[0:1, 0:8], in_=eu_d[:]).then_inc(p.c["dh"][0], 16)
        s.dma_start(out=gb1r[:], in_=gvec_d[0:1, :]).then_inc(p.c["dh"][0], 16)
        s.dma_start(out=lnsr[:], in_=gvec_d[1:2, :]).then_inc(p.c["dh"][0], 16)
        s.dma_start(out=lnor[:], in_=gvec_d[2:3, :]).then_inc(p.c["dh"][0], 16)
        s.dma_start(out=gb2r[:], in_=gsml_d[0:1, :]).then_inc(p.c["dh"][0], 16)
        s.dma_start(out=gsmr[:], in_=gsml_d[1:2, 0:16]).then_inc(p.c["dh"][0], 16)
        p.c["dh"][1] += 12 * 16
    block.sync(i_sync)

    def i_gp(g):
        g.dma_start(out=gw1s[:], in_=gw1_d.rearrange("(k q) n -> q k n", q=128)).then_inc(p.c["dg"][0], 16)
        g.dma_start(out=gw2s[:], in_=gw2_d.rearrange("(k q) n -> q k n", q=128)).then_inc(p.c["dg"][0], 16)
        g.dma_start(out=gw3s[:], in_=gw3_d.rearrange("(k q) n -> q k n", q=128)).then_inc(p.c["dg"][0], 16)
        g.dma_start(out=aws[:], in_=aw_d.rearrange("(k q) n -> q k n", q=128)).then_inc(p.c["dg"][0], 16)
        p.c["dg"][1] += 4 * 16
    block.gpsimd(i_gp)
    DG_W = p.val("dg")

    def i_gp2(g):
        g.iota(iotari[:], pattern=[[1, kcap]], base=0, channel_multiplier=0)
        g.iota(movi[:, 0:TILES], pattern=[[0, TILES]], base=0, channel_multiplier=1)
        g.iota(movi[:, TILES:2 * TILES], pattern=[[128, TILES]], base=0, channel_multiplier=0)
        g.iota(sloti[:], pattern=[[128, NG]], base=0, channel_multiplier=1)
        p.inc(g.drain(), "gp")
    block.gpsimd(i_gp2)
    GI_IOTA = p.val("gp")

    def i_v2(v):
        p.wait(v, "gp", GI_IOTA)
        v.tensor_copy(srbf[:], iotari[:])
        v.drain()
        mt = mov[:].rearrange("q (c t) -> q c t", t=2)
        v.tensor_copy(mt[:, :, 0:1], movi[:, 0:TILES].rearrange("q (c o) -> q c o", o=1))
        v.drain()
        v.tensor_copy(mt[:, :, 1:2], movi[:, TILES:2 * TILES].rearrange("q (c o) -> q c o", o=1))
        v.drain()
        v.tensor_copy(slotf[:], sloti[:])
        p.inc(v.drain(), "sv")
    block.vector(i_v2)

    def i_vec(v):
        for col, val in ((0, 1e-5), (1, 1e-8), (2, 1.0), (3, BIG), (4, 1e-12)):
            v.memset(cpool[:, col:col + 1], val)
        v.memset(onesf[:], 1.0)
        v.memset(zer_f[:], 0.0)
        v.memset(XgT[:, :, 0:128], 0.0)
        v.memset(prg[:].rearrange("q a e -> q (a e)"), 0.0)
        v.memset(comb[:, 0, :], 0.0)
        v.memset(bis[:, 0:2], 0.0)
        v.memset(bis[:, 2:4], 1024.0)
        v.memset(idxg[:], BIGI)
        p.inc(v.drain(), "sv")
    block.vector(i_vec)
    SV0 = p.val("sv")

    bc_list = [(gb1r, gb1b, 1024), (lnsr, lnsb, 1024), (lnor, lnob, 1024),
               (gb2r, gb2b, 512)]
    for bi, (srcr, dst, width) in enumerate(bc_list):
        def tsec(t, srcr=srcr, width=width, bi=bi):
            if bi == 0:
                p.wait(t, "dh")
                p.wait(t, "sv", SV0)
            else:
                p.wait(t, "ss")
            m = t.matmul(pb[0][:], onesf[0:1, :], srcr[0:1, 0:512], start=True, stop=True)
            p.inc(m, "tt")
            if width > 512:
                m = t.matmul(pb[1][:], onesf[0:1, :], srcr[0:1, 512:1024], start=True, stop=True)
                p.inc(m, "tt")
        block.tensor(tsec)

        def ssec(sc, dst=dst, width=width):
            p.wait(sc, "tt")
            sc.copy(dst[:, 0:512], pb[0][:])
            if width > 512:
                sc.copy(dst[:, 512:1024], pb[1][:])
            p.inc(sc.drain(), "ss")
        block.scalar(ssec)

    def t_small(t):
        p.wait(t, "ss")
        m = t.matmul(pb[0][:, 0:8], onesf[0:1, :], gsmr[0:1, 0:8], start=True, stop=True)
        p.inc(m, "tt")
        m = t.matmul(pb[1][:, 0:8], onesf[0:1, :], gsmr[0:1, 8:16], start=True, stop=True)
        p.inc(m, "tt")
    block.tensor(t_small)

    def s_small(sc):
        p.wait(sc, "tt")
        sc.copy(gb3b[:], pb[0][:, 0:8])
        sc.copy(abb[:], pb[1][:, 0:8])
        p.inc(sc.drain(), "ss")
    block.scalar(s_small)
    SS_BC = p.val("ss")

    def zf_s(s):
        p.wait(s, "sv", SV0)
        for a in range(TILES):
            s.dma_start(out=comb_dram[a * 128:(a + 1) * 128, :], in_=comb[:, 0, :]).then_inc(p.c["dh"][0], 16)
            p.c["dh"][1] += 16
    block.sync(zf_s)
    DH_CZ = p.val("dh")

    def s_ns(sc):
        p.wait(sc, "dh")
        sc.activation(eurow[0:1, 8:16], eurow[0:1, 0:8], AF.Ln, bias=cpool[0:1, 1:2])
        p.inc(sc.drain(), "ss")
    block.scalar(s_ns)

    def v_ns(v):
        p.wait(v, "ss")
        v.tensor_mul(eurow[0:1, 8:16], eurow[0:1, 8:16], eurow[0:1, 0:8])
        v.drain()
        v.reduce_sum(eurow[0:1, 0:1], eurow[0:1, 8:16], axis=AX.X, negate=True)
        v.drain()
        v.tensor_scalar(eurow[0:1, 1:2], eurow[0:1, 0:1], -0.01, 0.02, ALU.mult, ALU.add)
        v.drain()
        v.tensor_scalar_max(eurow[0:1, 1:2], eurow[0:1, 1:2], 1e-4)
        p.inc(v.drain(), "sv")
    block.vector(v_ns)
    SV_NS = p.val("sv")

    def t_ns(t):
        p.wait(t, "sv", SV_NS)
        m = t.matmul(pb[2][:, 0:1], onesf[0:1, :], eurow[0:1, 1:2], start=True, stop=True)
        p.inc(m, "tt")
    block.tensor(t_ns)
    TT_NS = p.val("tt")

    def v_ns2(v):
        p.wait(v, "tt", TT_NS)
        v.tensor_copy(bis[:, 12:13], pb[2][:, 0:1])
        p.inc(v.drain(), "sv")
    block.vector(v_ns2)

    ranks = (float(k_lo), float(k_lo + 1))
    for it in range(nbisect):
        def b_v(v, it=it):
            v.tensor_add(bis[:, 4:6], bis[:, 0:2], bis[:, 2:4])
            v.drain()
            v.tensor_scalar_mul(bis[:, 4:6], bis[:, 4:6], 0.5)
            v.drain()
            for kk in range(2):
                v.tensor_scalar(tmpa[:], impa[:], bis[:, 4 + kk:5 + kk], None, ALU.is_le)
                v.drain()
                v.reduce_sum(st[:, kk:kk + 1], tmpa[:], axis=AX.X)
                v.drain()
            p.inc(v.nop(), "sv")
        block.vector(b_v)
        SVB = p.val("sv")

        def b_t(t, it=it):
            p.wait(t, "sv", SVB)
            m = t.matmul(pb[4][:, 0:2], onesf[:], st[:, 0:2], start=True, stop=True)
            p.inc(m, "tt")
        block.tensor(b_t)
        TTB = p.val("tt")

        def b_u(v, it=it):
            p.wait(v, "tt", TTB)
            v.tensor_copy(bis[:, 6:8], pb[4][:, 0:2])
            v.drain()
            v.tensor_scalar(bis[:, 8:9], bis[:, 6:7], ranks[0], None, ALU.is_ge)
            v.tensor_scalar(bis[:, 9:10], bis[:, 7:8], ranks[1], None, ALU.is_ge)
            v.drain()
            v.tensor_sub(bis[:, 16:18], bis[:, 4:6], bis[:, 2:4])
            v.drain()
            v.tensor_mul(bis[:, 16:18], bis[:, 16:18], bis[:, 8:10])
            v.drain()
            v.tensor_add(bis[:, 2:4], bis[:, 2:4], bis[:, 16:18])
            v.tensor_sub(bis[:, 18:20], bis[:, 4:6], bis[:, 0:2])
            v.tensor_scalar(bis[:, 8:10], bis[:, 8:10], -1.0, 1.0, ALU.mult, ALU.add)
            v.drain()
            v.tensor_mul(bis[:, 18:20], bis[:, 18:20], bis[:, 8:10])
            v.drain()
            v.tensor_add(bis[:, 0:2], bis[:, 0:2], bis[:, 18:20])
            p.inc(v.drain(), "sv")
        block.vector(b_u)

    for kk in range(2):
        def vm_v(v, kk=kk):
            v.tensor_scalar(mska[:], impa[:], bis[:, kk:kk + 1], None, ALU.is_gt)
            v.drain()
            v.select(tmpa[:], mska[:], impa[:],
                     cpool[:, 3:4].broadcast_to((128, NTA)))
            v.drain()
            v.tensor_reduce(st[:, 4 + kk:5 + kk], tmpa[:], axis=AX.X, op=ALU.min)
            p.inc(v.drain(), "sv")
        block.vector(vm_v)
        SVV = p.val("sv")

        def vm_t(t, kk=kk):
            p.wait(t, "sv", SVV)
            i1 = t.transpose(pb[5][0:1, 0:128], st[:, 4 + kk:5 + kk], identf[:])
            p.inc(i1, "tt")
        block.tensor(vm_t)
        TTV = p.val("tt")

        def vm_v2(v, kk=kk):
            p.wait(v, "tt", TTV)
            v.tensor_reduce(st[0:1, 6 + kk:7 + kk], pb[5][0:1, 0:128], axis=AX.X, op=ALU.min)
            p.inc(v.drain(), "sv")
        block.vector(vm_v2)
        SVV2 = p.val("sv")

        def vm_t2(t, kk=kk):
            p.wait(t, "sv", SVV2)
            m = t.matmul(pb[5][:, 4 + kk:5 + kk], onesf[0:1, :], st[0:1, 6 + kk:7 + kk],
                         start=True, stop=True)
            p.inc(m, "tt")
        block.tensor(vm_t2)
        TTV2 = p.val("tt")

        def vm_v3(v, kk=kk):
            p.wait(v, "tt", TTV2)
            v.tensor_copy(bis[:, 14 + kk:15 + kk], pb[5][:, 4 + kk:5 + kk])
            p.inc(v.drain(), "sv")
        block.vector(vm_v3)

    def std_v(v):
        v.reduce_sum(st[:, 8:9], impa[:], axis=AX.X)
        p.inc(v.drain(), "sv")
    block.vector(std_v)
    SVS = p.val("sv")

    def std_t(t):
        p.wait(t, "sv", SVS)
        m = t.matmul(pb[5][:, 8:9], onesf[:], st[:, 8:9], start=True, stop=True)
        p.inc(m, "tt")
    block.tensor(std_t)
    TTS = p.val("tt")

    def std_v2(v):
        p.wait(v, "tt", TTS)
        v.tensor_scalar_mul(bis[:, 13:14], pb[5][:, 8:9], 1.0 / n_all)
        p.inc(v.drain(), "sv")
    block.vector(std_v2)

    def std_s(sc):
        p.wait(sc, "sv")
        sc.activation(tmpa[:], impa[:], AF.Square, bias=bis[:, 13:14], scale=-1.0,
                      accum_out=st[:, 9:10])
        sc.activation(st[:, 10:11], st[:, 9:10], AF.Copy, scale=1.0 / n_all)
        p.inc(sc.drain(), "ss")
    block.scalar(std_s)
    SSS = p.val("ss")

    def std_t2(t):
        p.wait(t, "ss", SSS)
        m = t.matmul(pb[5][:, 10:11], onesf[:], st[:, 10:11], start=True, stop=True)
        p.inc(m, "tt")
    block.tensor(std_t2)
    TTS2 = p.val("tt")

    def std_s2(sc):
        p.wait(sc, "tt", TTS2)
        sc.activation(bis[:, 11:12], pb[5][:, 10:11], AF.Sqrt)
        p.inc(sc.drain(), "ss")
    block.scalar(std_s2)

    def thr_v(v):
        p.wait(v, "ss")
        v.tensor_scalar(bis[:, 11:12], bis[:, 11:12], 0.1, 1.0, ALU.mult, ALU.add)
        v.tensor_sub(bis[:, 20:21], bis[:, 15:16], bis[:, 14:15])
        v.drain()
        v.tensor_scalar_mul(bis[:, 20:21], bis[:, 20:21], frac)
        v.drain()
        v.tensor_add(bis[:, 10:11], bis[:, 14:15], bis[:, 20:21])
        v.drain()
        v.tensor_scalar(msku[:], impt[:], bis[:, 10:11], None, ALU.is_gt)
        v.drain()
        v.tensor_copy(mskf[:], msku[:])
        v.drain()
        v.tensor_scalar_add(scl[:], n1t[:], 1e-8)
        v.drain()
        v.tensor_scalar_mul(scl[:], scl[:], bis[:, 11:12])
        v.drain()
        v.reciprocal(tt2b[:], scl[:])
        v.drain()
        v.tensor_mul(scl[:], tt2b[:], mskf[:])
        v.reduce_sum(st[:, 12:13], mskf[:], axis=AX.X)
        p.inc(v.drain(), "sv")
    block.vector(thr_v)
    SVT = p.val("sv")

    def pf_t(t):
        p.wait(t, "sv", SVT)
        m = t.matmul(pb[6][:, 0:1], suo[:], st[:, 12:13], start=True, stop=True)
        p.inc(m, "tt")
        m = t.matmul(pb[6][:, 4:5], onesf[:], st[:, 12:13], start=True, stop=True)
        p.inc(m, "tt")
    block.tensor(pf_t)
    TTP = p.val("tt")

    def pf_v(v):
        p.wait(v, "tt", TTP)
        v.tensor_copy(st[:, 13:14], pb[6][:, 0:1])
        v.tensor_copy(st[:, 14:15], pb[6][:, 4:5])
        v.drain()
        v.tensor_tensor_scan(rnk[:], mskf[:], zer_f[:, 0:TILES], 0.0, ALU.add, ALU.bypass)
        v.drain()
        v.tensor_sub(rnk[:], rnk[:], mskf[:])
        v.drain()
        v.tensor_scalar_add(rnk[:], rnk[:], st[:, 13:14])
        v.drain()
        v.tensor_scalar(tt2b[:], mskf[:], -BIG, BIG, ALU.mult, ALU.add)
        v.drain()
        v.tensor_add(tt2b[:], tt2b[:], rnk[:])
        p.inc(v.drain(), "sv")
    block.vector(pf_v)
    SVR = p.val("sv")

    for c in range(TILES):
        def eq_v(v, c=c):
            if c >= 2:
                p.wait(v, "tt", mile[("eqt", c - 2)])
            v.tensor_scalar(eqc[:, c % 2, :], srbf[:], tt2b[:, c:c + 1], None, ALU.is_equal)
            p.inc(v.drain(), "sv")
        block.vector(eq_v)
        SVE = p.val("sv")

        def eq_t(t, c=c, SVE=SVE):
            p.wait(t, "sv", SVE)
            if c >= 2:
                p.wait(t, "sv", mile[("eqa", c - 2)])
            m = None
            for gg in range(NG):
                m = t.matmul(pb[5 + c % 2][:, 2 * gg:2 * gg + 2], eqc[:, c % 2, gg * 128:(gg + 1) * 128],
                             mov[:, 2 * c:2 * c + 2], start=True, stop=True)
                p.inc(m, "tt")
            mile[("eqt", c)] = p.val("tt")
        block.tensor(eq_t)
        TT_EQ1 = p.val("tt")

        def eq_a(v, c=c, TT_EQ1=TT_EQ1):
            p.wait(v, "tt", TT_EQ1)
            if c == 0:
                v.tensor_copy(idxacc[:], pb[5][:, 0:2 * NG])
            else:
                v.tensor_add(idxacc[:], idxacc[:], pb[5 + c % 2][:, 0:2 * NG])
            v.drain()
            mile[("eqa", c)] = p.inc(v.nop(), "sv")
        block.vector(eq_a)
    TT_EQ = p.val("tt")

    def ix_v(v):
        pbv = idxacc[:].rearrange("q (g t) -> q t g", t=2)
        v.tensor_tensor(idxf[:], pbv[:, 0, :], pbv[:, 1, :], ALU.add)
        v.drain()
        v.tensor_scalar(mskg[:], slotf[:], st[:, 14:15], None, ALU.is_ge)
        v.drain()
        v.scalar_tensor_tensor(idxf[:], mskg[:], BIG, idxf[:], ALU.mult, ALU.add)
        v.drain()
        v.tensor_copy(idxg[:], idxf[:])
        v.drain()
        v.tensor_scalar(bmf[:], idxf[:], float(TOK // 2), None, ALU.is_ge)
        p.inc(v.drain(), "sv")
    block.vector(ix_v)
    SV_IDX = p.val("sv")

    for i in range(TILES):
        par = i % 2

        def af_s(s, i=i, par=par):
            if i >= 2:
                p.wait(s, "tt", mile[("afmm", i - 2)])
            s.dma_start(out=xmtb[:, par, :], in_=xm_d[i * 128:(i + 1) * 128, :]).then_inc(p.c["dh"][0], 16)
            p.c["dh"][1] += 16
        block.sync(af_s)
        DHA = p.val("dh")

        def af_v(v, i=i, par=par, DHA=DHA):
            p.wait(v, "dh", DHA)
            ba = 0 if i < HB else 1
            v.memset(svb[:, par, :], 0.0)
            v.drain()
            v.memset(svb[:, par, ba:ba + 1], 1.0)
            v.drain()
            v.tensor_scalar_mul(xstb[:, par, 0:C], xmtb[:, par, :], scl[:, i:i + 1])
            v.drain()
            v.tensor_copy(xstb[:, par, C:C + 8], noi[:, 8 * i:8 * i + 8])
            p.inc(v.drain(), "sv")
        block.vector(af_v)
        SVA = p.val("sv")

        def af_t(t, i=i, par=par, SVA=SVA):
            p.wait(t, "sv", SVA)
            if i >= 2:
                p.wait(t, "sv", mile[("afacc", i - 2)])
            m = None
            for c in range(8):
                m = t.matmul(pb[7][:, 16 * par + 2 * c:16 * par + 2 * c + 2],
                             xstb[:, par, c * 128:(c + 1) * 128],
                             svb[:, par, :], start=True, stop=True)
                p.inc(m, "tt")
            mile[("afmm", i)] = p.val("tt")
        block.tensor(af_t)
        TTAF1 = p.val("tt")

        def af_a(v, i=i, par=par, TTAF1=TTAF1):
            p.wait(v, "tt", TTAF1)
            if i == 0:
                v.tensor_copy(af_acc[:], pb[7][:, 0:16])
            else:
                v.tensor_add(af_acc[:], af_acc[:], pb[7][:, 16 * par:16 * par + 16])
            v.drain()
            mile[("afacc", i)] = p.inc(v.nop(), "sv")
        block.vector(af_a)

        def af_sw(s, i=i, par=par, SVA=SVA):
            p.wait(s, "sv", SVA)
            s.dma_start(out=xs_dram[i * 128:(i + 1) * 128, :], in_=xstb[:, par, :]).then_inc(p.c["dh"][0], 16)
            p.c["dh"][1] += 16
        block.sync(af_sw)
    TT_AF = p.val("tt")
    DH_XS = p.val("dh")

    def af_e(v):
        v.tensor_scalar_mul(ameanb[:], af_acc[:], 1.0 / T)
        p.inc(v.drain(), "sv")
    block.vector(af_e)
    SV_AM = p.val("sv")

    def af_t2(t):
        p.wait(t, "sv", SV_AM)
        m0 = m1 = None
        for c in range(8):
            m0 = t.matmul(pb[6][0:1, 8:16], ameanb[:, 2 * c:2 * c + 1], aws[:, c, :],
                          start=(c == 0), stop=(c == 7))
            m1 = t.matmul(pb[5][0:1, 0:8], ameanb[:, 2 * c + 1:2 * c + 2], aws[:, c, :],
                          start=(c == 0), stop=(c == 7))
        p.inc(m0, "tt")
        p.inc(m1, "tt")
    block.tensor(af_t2)
    TT_AF2 = p.val("tt")

    def af_e2(v):
        p.wait(v, "tt", TT_AF2)
        v.tensor_copy(st[0:1, 0:8], pb[6][0:1, 8:16])
        v.tensor_copy(st[0:1, 8:16], pb[5][0:1, 0:8])
        p.inc(v.drain(), "sv")
    block.vector(af_e2)
    SV_AF2 = p.val("sv")

    def af_t3(t):
        p.wait(t, "sv", SV_AF2)
        m = t.matmul(pb[6][:, 0:16], onesf[0:1, :], st[0:1, 0:16], start=True, stop=True)
        p.inc(m, "tt")
    block.tensor(af_t3)
    TT_AF3 = p.val("tt")

    def af_e3(v):
        p.wait(v, "tt", TT_AF3)
        v.tensor_add(affs[:, 0:8], pb[6][:, 0:8], abb[:])
        v.tensor_add(affs[:, 8:16], pb[6][:, 8:16], abb[:])
        v.drain()
        v.tensor_scalar_mul(affs[:], affs[:], 0.3)
        p.inc(v.drain(), "sv")
    block.vector(af_e3)

    for gti in range(NG):
        def xg_g(g, gti=gti, DH_XS=DH_XS):
            if gti == 0:
                p.wait(g, "sv", SV_IDX)
                p.wait(g, "dh", DH_XS)
            else:
                p.wait(g, "sv", mile[("xgv", gti - 1)])
            g.memset(xgbf[:], 0.0)
            g.indirect_dma_start(
                out=xgbf[:], out_offset=None,
                in_=xs_dram[:], in_offset=bass.IndirectOffsetOnAxis(ap=idxg[:, gti:gti + 1], axis=0),
                bounds_check=TOK - 1, oob_is_err=False,
            ).then_inc(p.c["gi"][0], 16)
            p.c["gi"][1] += 16
            g.wait_ge(p.c["gi"][0], p.c["gi"][1])
        block.gpsimd(xg_g)
        GI_XG = p.val("gi")

        def xg_v(v, gti=gti):
            p.wait(v, "gi", GI_XG)
            v.tensor_copy(xtmp[:], xgbf[:, 0:C])
            v.drain()
            v.tensor_copy(nslot[:, gti, :], xgbf[:, C:C + 8])
            v.drain()
            v.scalar_tensor_tensor(comb[:, gti + 1, :], xtmp[:], 0.1, zer_f[:, 0:1].broadcast_to((128, C)), ALU.mult, ALU.add)
            mile[("xgv", gti)] = p.inc(v.drain(), "sv")
        block.vector(xg_v)
        SVXG = p.val("sv")
        svend = p.val("sv") + 8

        def xg_t(t, gti=gti, SVXG=SVXG, svend=svend):
            p.wait(t, "sv", SVXG)
            for c in range(8):
                if c >= 2:
                    p.wait(t, "sv", svend - 8 + c - 1)
                i1 = t.transpose(pb[2 + (c % 2)][:, 0:128], xtmp[:, c * 128:(c + 1) * 128], identf[:])
                p.inc(i1, "tt")
        block.tensor(xg_t)
        TTXG = p.val("tt")

        for c in range(8):
            def xg_e(v, c=c, gti=gti, TTXG=TTXG):
                p.wait(v, "tt", TTXG - 8 + c + 1)
                i1 = v.tensor_copy(XgT[:, c, (gti + 1) * 128:(gti + 2) * 128], pb[2 + (c % 2)][:, 0:128])
                v.drain()
                p.inc(v.nop(), "sv")
            block.vector(xg_e)
        assert p.val("sv") == svend
    SV_XGT = p.val("sv")

    for gti in range(GT):
        def gp_t1(t, gti=gti):
            if gti == 0:
                p.wait(t, "sv")
                p.wait(t, "dg", DG_W)
                p.wait(t, "ss", SS_BC)
            else:
                p.wait(t, "sv", mile[("gpfree", gti - 1)])
            m0 = m1 = None
            for c in range(8):
                m0 = t.matmul(pb[0][:], XgT[:, c, gti * 128:(gti + 1) * 128], gw1s[:, c, 0:512],
                              start=(c == 0), stop=(c == 7))
                m1 = t.matmul(pb[1][:], XgT[:, c, gti * 128:(gti + 1) * 128], gw1s[:, c, 512:1024],
                              start=(c == 0), stop=(c == 7))
            p.inc(m0, "tt")
            p.inc(m1, "tt")
        block.tensor(gp_t1)
        TTG1 = p.val("tt")

        def gp_v1(v, gti=gti):
            p.wait(v, "tt", TTG1)
            v.tensor_add(htmp[:, 0:512], pb[0][:], gb1b[:, 0:512])
            v.tensor_add(htmp[:, 512:1024], pb[1][:], gb1b[:, 512:1024])
            p.inc(v.drain(), "sv")
        block.vector(gp_v1)
        SVG1 = p.val("sv")

        def gp_s1(sc, gti=gti):
            p.wait(sc, "sv", SVG1)
            sc.activation(xtmp[:], htmp[:], AF.Silu)
            sc.drain()
            sc.activation(sqs[:], xtmp[:], AF.Square, accum_out=st[:, 4:5])
            sc.activation(st[:, 5:6], st[:, 4:5], AF.Copy, scale=INV_C)
            p.inc(sc.drain(), "ss")
        block.scalar(gp_s1)

        def gp_v2(v):
            p.wait(v, "ss")
            v.reduce_sum(st[:, 0:1], xtmp[:], axis=AX.X)
            v.drain()
            v.tensor_scalar_mul(st[:, 2:3], st[:, 0:1], INV_C)
            v.drain()
            v.tensor_mul(st[:, 1:2], st[:, 2:3], st[:, 2:3])
            v.drain()
            v.tensor_sub(st[:, 3:4], st[:, 5:6], st[:, 1:2])
            p.inc(v.drain(), "sv")
        block.vector(gp_v2)

        def gp_s2(sc):
            p.wait(sc, "sv")
            sc.activation(st[:, 7:8], st[:, 3:4], AF.Ln, bias=cpool[:, 0:1])
            sc.drain()
            sc.activation(st[:, 6:7], st[:, 7:8], AF.Exp, scale=-0.5)
            p.inc(sc.drain(), "ss")
        block.scalar(gp_s2)

        def gp_v3(v):
            p.wait(v, "ss")
            v.tensor_scalar(htmp[:], xtmp[:], st[:, 2:3], st[:, 6:7], ALU.subtract, ALU.mult)
            v.drain()
            v.tensor_mul(htmp[:], htmp[:], lnsb[:])
            v.drain()
            v.tensor_add(htmp[:], htmp[:], lnob[:])
            p.inc(v.drain(), "sv")
        block.vector(gp_v3)
        SVG3 = p.val("sv")
        svend = p.val("sv") + 8

        def gp_t2(t, SVG3=SVG3, svend=svend):
            p.wait(t, "sv", SVG3)
            for c in range(8):
                if c >= 2:
                    p.wait(t, "sv", svend - 8 + c - 1)
                i1 = t.transpose(pb[2 + (c % 2)][:, 0:128], htmp[:, c * 128:(c + 1) * 128], identf[:])
                p.inc(i1, "tt")
        block.tensor(gp_t2)
        TTG2 = p.val("tt")

        for c in range(8):
            def gp_e(v, c=c, TTG2=TTG2):
                p.wait(v, "tt", TTG2 - 8 + c + 1)
                v.tensor_copy(hnT[:, c, :], pb[2 + (c % 2)][:, 0:128])
                v.drain()
                p.inc(v.nop(), "sv")
            block.vector(gp_e)
        assert p.val("sv") == svend

        def gp_t3(t):
            p.wait(t, "sv", svend)
            m = None
            for c in range(8):
                m = t.matmul(pb[4][:], hnT[:, c, :], gw2s[:, c, :], start=(c == 0), stop=(c == 7))
            p.inc(m, "tt")
        block.tensor(gp_t3)
        TTG3 = p.val("tt")

        def gp_v4(v, TTG3=TTG3):
            p.wait(v, "tt", TTG3)
            v.tensor_add(h2tmp[:], pb[4][:], gb2b[:])
            p.inc(v.drain(), "sv")
        block.vector(gp_v4)
        SVG4 = p.val("sv")

        def gp_s3(sc, SVG4=SVG4):
            p.wait(sc, "sv", SVG4)
            sc.activation(h2tmp[:], h2tmp[:], AF.Silu)
            p.inc(sc.drain(), "ss")
        block.scalar(gp_s3)
        SSG3 = p.val("ss")
        svend2 = p.val("sv") + 4

        def gp_t4(t, SSG3=SSG3, svend2=svend2):
            p.wait(t, "ss", SSG3)
            for c in range(4):
                if c >= 2:
                    p.wait(t, "sv", svend2 - 4 + c - 1)
                i1 = t.transpose(pb[2 + (c % 2)][:, 0:128], h2tmp[:, c * 128:(c + 1) * 128], identf[:])
                p.inc(i1, "tt")
        block.tensor(gp_t4)
        TTG4 = p.val("tt")

        for c in range(4):
            def gp_e2(v, c=c, TTG4=TTG4):
                p.wait(v, "tt", TTG4 - 4 + c + 1)
                v.tensor_copy(h2T[:, c, :], pb[2 + (c % 2)][:, 0:128])
                v.drain()
                p.inc(v.nop(), "sv")
            block.vector(gp_e2)
        assert p.val("sv") == svend2

        def gp_t5(t, svend2=svend2):
            p.wait(t, "sv", svend2)
            m = None
            for c in range(4):
                m = t.matmul(pb[5][:, 0:8], h2T[:, c, :], gw3s[:, c, :], start=(c == 0), stop=(c == 3))
            p.inc(m, "tt")
        block.tensor(gp_t5)
        TTG5 = p.val("tt")

        def gp_v5(v, gti=gti, TTG5=TTG5):
            p.wait(v, "tt", TTG5)
            if gti == 0:
                v.tensor_add(logc[:], pb[5][:, 0:8], gb3b[:])
            else:
                v.tensor_add(lgs[:, gti - 1, :], pb[5][:, 0:8], gb3b[:])
            v.drain()
            mile[("gpfree", gti)] = p.inc(v.nop(), "sv")
        block.vector(gp_v5)

    def sp_v(v):
        v.tensor_sub(affd[:], affs[:, 8:16], affs[:, 0:8])
        v.drain()
        for gg in range(NG):
            v.tensor_scalar_mul(afS[:, gg, :], affd[:], bmf[:, gg:gg + 1])
            v.drain()
            v.tensor_add(afS[:, gg, :], afS[:, gg, :], affs[:, 0:8])
            v.drain()
        v.tensor_tensor(lgs[:], lgs[:], afS[:], ALU.add)
        v.drain()
        v.scalar_tensor_tensor(lgs[:].rearrange("q g e -> q (g e)"),
                               nslot[:].rearrange("q g e -> q (g e)"), bis[:, 12:13],
                               lgs[:].rearrange("q g e -> q (g e)"), ALU.mult, ALU.add)
        v.drain()
        v.tensor_scalar_mul(lgs[:].rearrange("q g e -> q (g e)"),
                            lgs[:].rearrange("q g e -> q (g e)"), 1.0 / 1.5)
        v.drain()
        v.reduce_max(m1t[:, 0:NG], lgs[:], axis=AX.X)
        v.drain()
        m1b = m1t[:, 0:NG].rearrange("q (g o) -> q g o", o=1).broadcast_to((128, NG, 8))
        v.tensor_tensor(lg2[:], lgs[:], m1b, ALU.is_equal)
        v.drain()
        v.reduce_sum(c1t[:, 0:NG], lg2[:], axis=AX.X)
        v.drain()
        v.scalar_tensor_tensor(lg2[:].rearrange("q g e -> q (g e)"),
                               lg2[:].rearrange("q g e -> q (g e)"), -BIG,
                               lgs[:].rearrange("q g e -> q (g e)"), ALU.mult, ALU.add)
        v.drain()
        v.reduce_max(m2t[:, 0:NG], lg2[:], axis=AX.X)
        v.drain()
        v.tensor_scalar(c1t[:, 0:NG], c1t[:, 0:NG], 2.0, None, ALU.is_ge)
        v.drain()
        v.tensor_sub(tt2b[:, 0:NG], m1t[:, 0:NG], m2t[:, 0:NG])
        v.drain()
        v.tensor_mul(tt2b[:, 0:NG], tt2b[:, 0:NG], c1t[:, 0:NG])
        v.drain()
        v.tensor_add(m2t[:, 0:NG], m2t[:, 0:NG], tt2b[:, 0:NG])
        v.drain()
        v.tensor_tensor(lg2[:], lgs[:], m1b, ALU.subtract)
        p.inc(v.drain(), "sv")
    block.vector(sp_v)
    SVS1 = p.val("sv")

    def sp_s(sc):
        p.wait(sc, "sv", SVS1)
        sc.activation(lg2[:].rearrange("q g e -> q (g e)"),
                      lg2[:].rearrange("q g e -> q (g e)"), AF.Exp)
        p.inc(sc.drain(), "ss")
    block.scalar(sp_s)
    SSS1 = p.val("ss")

    def sp_v2(v):
        p.wait(v, "ss", SSS1)
        m2b = m2t[:, 0:NG].rearrange("q (g o) -> q g o", o=1).broadcast_to((128, NG, 8))
        v.tensor_tensor(lgs[:], lgs[:], m2b, ALU.is_ge)
        v.drain()
        v.tensor_mul(lg2[:].rearrange("q g e -> q (g e)"), lg2[:].rearrange("q g e -> q (g e)"),
                     lgs[:].rearrange("q g e -> q (g e)"))
        v.drain()
        v.reduce_sum(m1t[:, 0:NG], lg2[:], axis=AX.X)
        v.drain()
        v.reciprocal(c1t[:, 0:NG], m1t[:, 0:NG])
        v.drain()
        rb = c1t[:, 0:NG].rearrange("q (g o) -> q g o", o=1).broadcast_to((128, NG, 8))
        v.tensor_tensor(prg[:, 1:GT, :], lg2[:], rb, ALU.mult)
        p.inc(v.drain(), "sv")
    block.vector(sp_v2)

    def dg_v(v):
        cb = logc[:].rearrange("q (o e) -> q o e", o=1).broadcast_to((128, TILES, 8))
        v.tensor_copy(lga[:], cb)
        v.drain()
        ab_ = affs[:, 0:8].rearrange("q (o e) -> q o e", o=1).broadcast_to((128, HB, 8))
        bb_ = affs[:, 8:16].rearrange("q (o e) -> q o e", o=1).broadcast_to((128, TILES - HB, 8))
        v.tensor_tensor(lga[:, 0:HB, :], lga[:, 0:HB, :], ab_, ALU.add)
        v.tensor_tensor(lga[:, HB:TILES, :], lga[:, HB:TILES, :], bb_, ALU.add)
        v.drain()
        v.scalar_tensor_tensor(lga[:].rearrange("q a e -> q (a e)"), noi[:], bis[:, 12:13],
                               lga[:].rearrange("q a e -> q (a e)"), ALU.mult, ALU.add)
        v.drain()
        v.tensor_scalar_mul(lga[:].rearrange("q a e -> q (a e)"),
                            lga[:].rearrange("q a e -> q (a e)"), 1.0 / 1.5)
        v.drain()
        v.reduce_max(m1t[:], lga[:], axis=AX.X)
        v.drain()
        m1b = m1t[:].rearrange("q (a o) -> q a o", o=1).broadcast_to((128, TILES, 8))
        v.tensor_tensor(lgb[:], lga[:], m1b, ALU.is_equal)
        v.drain()
        v.reduce_sum(c1t[:], lgb[:], axis=AX.X)
        v.drain()
        v.scalar_tensor_tensor(lgb[:].rearrange("q a e -> q (a e)"),
                               lgb[:].rearrange("q a e -> q (a e)"), -BIG,
                               lga[:].rearrange("q a e -> q (a e)"), ALU.mult, ALU.add)
        v.drain()
        v.reduce_max(m2t[:], lgb[:], axis=AX.X)
        v.drain()
        v.tensor_scalar(c1t[:], c1t[:], 2.0, None, ALU.is_ge)
        v.drain()
        v.tensor_sub(tt2b[:], m1t[:], m2t[:])
        v.drain()
        v.tensor_mul(tt2b[:], tt2b[:], c1t[:])
        v.drain()
        v.tensor_add(m2t[:], m2t[:], tt2b[:])
        v.drain()
        v.tensor_tensor(lgb[:], lga[:], m1b, ALU.subtract)
        p.inc(v.drain(), "sv")
    block.vector(dg_v)
    SVD1 = p.val("sv")

    def dg_sc(sc):
        p.wait(sc, "sv", SVD1)
        sc.activation(lgb[:].rearrange("q a e -> q (a e)"),
                      lgb[:].rearrange("q a e -> q (a e)"), AF.Exp)
        p.inc(sc.drain(), "ss")
    block.scalar(dg_sc)
    SSD = p.val("ss")

    def dg_v2(v):
        p.wait(v, "ss", SSD)
        m2b = m2t[:].rearrange("q (a o) -> q a o", o=1).broadcast_to((128, TILES, 8))
        v.tensor_tensor(lga[:], lga[:], m2b, ALU.is_ge)
        v.drain()
        v.tensor_mul(lgb[:].rearrange("q a e -> q (a e)"), lgb[:].rearrange("q a e -> q (a e)"),
                     lga[:].rearrange("q a e -> q (a e)"))
        v.drain()
        v.reduce_sum(m1t[:], lgb[:], axis=AX.X)
        v.drain()
        v.reciprocal(c1t[:], m1t[:])
        v.drain()
        rb = c1t[:].rearrange("q (a o) -> q a o", o=1).broadcast_to((128, TILES, 8))
        v.tensor_tensor(probs[:], lgb[:], rb, ALU.mult)
        p.inc(v.drain(), "sv")
    block.vector(dg_v2)
    GI_PRG = p.val("gi")

    return _build_l2_experts(nc, blk, block, p, mile, ctx, s1, locals())


def _build_l2_experts(nc, blk, block, p, mile, ctx, s1, env):
    g_ = env
    TOK, TILES, GT, NG, TCH = g_["TOK"], g_["TILES"], g_["GT"], g_["NG"], g_["TCH"]
    pb, XgT, comb, Cbf, prg, prT = (g_[k] for k in
        ("pb", "XgT", "comb", "Cbf", "prg", "prT"))
    ew_d, eb_d = g_["ew_d"], g_["eb_d"]
    identf, onesf, msku, probs, st, cpool = (g_[k] for k in
        ("identf", "onesf", "msku", "probs", "st", "cpool"))
    x1_d, y_d, comb_dram, idxg = (g_[k] for k in
        ("x1_d", "y_d", "comb_dram", "idxg"))
    sb = g_["sb"]
    s2 = g_["s2"]
    GI_PRG = g_["GI_PRG"]

    for gg in reversed(s1):
        gg.__exit__(None, None, None)
    for gg in reversed(s2):
        gg.__exit__(None, None, None)

    s3 = []

    def sb3(name, shape, dtype=F32):
        g = nc.sbuf_tensor(name, shape, dtype)
        t = g.__enter__()
        s3.append(g)
        return t
    wq1 = sb3("wq1", [128, 8, 512], BF16)
    wq2 = sb3("wq2", [128, 16, 512], BF16)
    wq3f = sb3("wq3f", [128, 4096], BF16)
    bq1 = sb3("bq1", [128, 16], F32)
    bq2 = sb3("bq2", [128, 4], F32)
    bqs = sb3("bqs", [4, 128], F32)
    h1q = sb3("h1q", [128, 16, GT * 128], BF16)
    h2q = sb3("h2q", [128, 4, GT * 128], BF16)
    zf = sb3("zf", [128, 512], F32)
    z2f = sb3("z2f", [128, 512], F32)
    sgf = sb3("sgf", [128, 512], F32)
    Cacc = sb3("Cacc", [128, C], F32)
    bxb = sb3("bxb", [128, C], F32)
    ebrow = sb3("ebrow", [1, C], F32)
    crow = sb3("crow", [1, C], BF16)
    wq3_8 = wq3f[:].rearrange("q (k n) -> q k n", k=8)
    wq3_4 = wq3f[:].rearrange("q (k n) -> q k n", k=4)

    BAR_SV = p.val("sv")
    BAR_SS = p.val("ss")
    BAR_TT = p.val("tt")

    def bar_g(g):
        p.wait(g, "sv", BAR_SV)
        p.wait(g, "ss", BAR_SS)
        p.wait(g, "tt", BAR_TT)
        p.wait(g, "gi", GI_PRG)
    block.gpsimd(bar_g)
    GI_BAR = p.val("gi")

    def mm1_stage(t_, nck, ncols_h, wsrc, hdst, psum_tiles, svbase):
        pass

    for e in range(8):
        fam, idx = SLOT_FAM[e]
        if fam == 0:
            parts, hw, act = 4, 1024, "gelu"
        elif fam == 2:
            parts, hw, act = 3, 1024, "ssig"
        else:
            parts, hw, act = 2, 1024, None

        if fam in (0, 2):
            for q in range(parts):
                for dh_ in range(2):
                    def wl1_g(g, fam=fam, idx=idx, q=q, e=e, dh_=dh_):
                        if e == 0 and q == 0 and dh_ == 0:
                            p.wait(g, "gi", GI_BAR)
                        p.wait(g, "sv")
                        off = q * 1024 + dh_ * 512
                        g.dma_start(out=wq1[:], in_=ew_d[(fam, 0)][idx, :, off:off + 512]
                                    .rearrange("(k q2) n -> q2 k n", q2=128)).then_inc(p.c["gi"][0], 16)
                        g.dma_start(out=bqs[:], in_=eb_d[(fam, 0)][idx, off:off + 512]
                                    .rearrange("(h q2) -> h q2", h=4)).then_inc(p.c["gi"][0], 16)
                        p.c["gi"][1] += 32
                    block.gpsimd(wl1_g)
                    GIW = p.val("gi")

                    def bqt_t(t, GIW=GIW):
                        p.wait(t, "gi", GIW)
                        i1 = t.transpose(pb[3][:, 0:4], bqs[0:4, 0:128], identf[0:4, 0:4])
                        p.inc(i1, "tt")
                    block.tensor(bqt_t)
                    TTBQ = p.val("tt")

                    def bqt_v(v, dh_=dh_, TTBQ=TTBQ):
                        p.wait(v, "tt", TTBQ)
                        v.tensor_copy(bq1[:, dh_ * 4:(dh_ + 1) * 4], pb[3][:, 0:4])
                        v.drain()
                        p.inc(v.nop(), "sv")
                    block.vector(bqt_v)

                    for ht in range(4):
                        hb = dh_ * 4 + ht
                        for tc in range(TCH):
                            bank = pb[(ht * TCH + tc) % 2]
                            svh = p.val("sv")

                            def m1_t(t, ht=ht, tc=tc, bank=bank, GIW=GIW, svh=svh):
                                if ht == 0 and tc == 0:
                                    p.wait(t, "gi", GIW)
                                if ht * TCH + tc >= 2:
                                    p.wait(t, "sv", svh - 1)
                                m = None
                                for c in range(8):
                                    m = t.matmul(bank[:], wq1[:, c, ht * 128:(ht + 1) * 128],
                                                 XgT[:, c, tc * 512:(tc + 1) * 512],
                                                 start=(c == 0), stop=(c == 7))
                                p.inc(m, "tt")
                            block.tensor(m1_t)
                            TTM = p.val("tt")

                            def m1_v(v, hb=hb, bank=bank, TTM=TTM, act=act):
                                p.wait(v, "tt", TTM)
                                v.tensor_scalar_add(zf[:], bank[:], bq1[:, hb:hb + 1])
                                p.inc(v.nop(), "sv")
                                if act == "gelu":
                                    v.tensor_mul(z2f[:], zf[:], zf[:])
                                    v.drain()
                                    v.tensor_scalar(z2f[:], z2f[:], 0.07135481282929373,
                                                    1.5957691216057308, ALU.mult, ALU.add)
                                    v.drain()
                                    v.tensor_mul(z2f[:], z2f[:], zf[:])
                                p.inc(v.drain(), "sv")
                            block.vector(m1_v)
                            SVZ = p.val("sv")

                            def m1_s(sc, act=act, SVZ=SVZ):
                                p.wait(sc, "sv", SVZ)
                                if act == "gelu":
                                    sc.activation(sgf[:], z2f[:], AF.Sigmoid)
                                else:
                                    sc.activation(z2f[:], zf[:], AF.Silu)
                                    sc.drain()
                                    sc.activation(sgf[:], zf[:], AF.Sigmoid)
                                p.inc(sc.drain(), "ss")
                            block.scalar(m1_s)
                            SSZ = p.val("ss")

                            def m1_v2(v, hb=hb, tc=tc, act=act, SSZ=SSZ):
                                p.wait(v, "ss", SSZ)
                                if act == "gelu":
                                    v.tensor_mul(h1q[:, hb, tc * 512:(tc + 1) * 512], zf[:], sgf[:])
                                else:
                                    v.tensor_mul(h1q[:, hb, tc * 512:(tc + 1) * 512], z2f[:], sgf[:])
                                v.drain()
                                p.inc(v.nop(), "sv")
                            block.vector(m1_v2)
                SV_H1 = p.val("sv")

                for dc in range(2):
                    def wl3_g(g, fam=fam, idx=idx, q=q, dc=dc):
                        p.wait(g, "sv")
                        g.dma_start(out=wq3_8, in_=ew_d[(fam, 1)][idx, q * 1024:(q + 1) * 1024,
                                    dc * 512:(dc + 1) * 512]
                                    .rearrange("(k q2) n -> q2 k n", q2=128)).then_inc(p.c["gi"][0], 16)
                        p.c["gi"][1] += 16
                    block.gpsimd(wl3_g)
                    GIW3 = p.val("gi")

                    for tt_ in range(GT):
                        bank = pb[4 + tt_ % 2]
                        svh = p.val("sv")

                        def m2_t(t, tt_=tt_, dc=dc, bank=bank, svh=svh, GIW3=GIW3, SV_H1=SV_H1):
                            if tt_ == 0:
                                p.wait(t, "gi", GIW3)
                                if dc == 0:
                                    p.wait(t, "sv", SV_H1)
                            if tt_ >= 2:
                                p.wait(t, "sv", svh)
                            m = None
                            for hc in range(8):
                                m = t.matmul(bank[:], h1q[:, hc, tt_ * 128:(tt_ + 1) * 128],
                                             wq3_8[:, hc, :],
                                             start=(hc == 0), stop=(hc == 7))
                            p.inc(m, "tt")
                        block.tensor(m2_t)
                        TTM2 = p.val("tt")

                        def m2_v(v, tt_=tt_, dc=dc, q=q, e=e, bank=bank, TTM2=TTM2):
                            p.wait(v, "tt", TTM2)
                            if tt_ == 0:
                                if q == 0:
                                    v.tensor_copy(Cacc[:, dc * 512:(dc + 1) * 512], bank[:])
                                else:
                                    v.tensor_add(Cacc[:, dc * 512:(dc + 1) * 512],
                                                 Cacc[:, dc * 512:(dc + 1) * 512], bank[:])
                            else:
                                v.scalar_tensor_tensor(comb[:, tt_, dc * 512:(dc + 1) * 512],
                                                       bank[:], prg[:, tt_, e:e + 1],
                                                       comb[:, tt_, dc * 512:(dc + 1) * 512],
                                                       ALU.mult, ALU.add)
                            v.drain()
                            p.inc(v.nop(), "sv")
                        block.vector(m2_v)
        else:
            for ch in range(4):
                def wl1_g(g, idx=idx, ch=ch, e=e):
                    if e == 0 and ch == 0:
                        p.wait(g, "gi", GI_BAR)
                    p.wait(g, "sv")
                    off = ch * 512
                    g.dma_start(out=wq1[:], in_=ew_d[(1, 0)][idx, :, off:off + 512]
                                .rearrange("(k q2) n -> q2 k n", q2=128)).then_inc(p.c["gi"][0], 16)
                    g.dma_start(out=bqs[:], in_=eb_d[(1, 0)][idx, off:off + 512]
                                .rearrange("(h q2) -> h q2", h=4)).then_inc(p.c["gi"][0], 16)
                    p.c["gi"][1] += 32
                block.gpsimd(wl1_g)
                GIW = p.val("gi")

                def bqt_t(t, GIW=GIW):
                    p.wait(t, "gi", GIW)
                    i1 = t.transpose(pb[3][:, 0:4], bqs[0:4, 0:128], identf[0:4, 0:4])
                    p.inc(i1, "tt")
                block.tensor(bqt_t)
                TTBQ = p.val("tt")

                def bqt_v(v, ch=ch, TTBQ=TTBQ):
                    p.wait(v, "tt", TTBQ)
                    v.tensor_copy(bq1[:, ch * 4:(ch + 1) * 4], pb[3][:, 0:4])
                    v.drain()
                    p.inc(v.nop(), "sv")
                block.vector(bqt_v)
                SVBQ = p.val("sv")

                for ht in range(4):
                    hb = ch * 4 + ht
                    for tc in range(TCH):
                        bank = pb[(ht * TCH + tc) % 2]
                        svh = p.val("sv")

                        def m1_t(t, ht=ht, tc=tc, bank=bank, GIW=GIW, svh=svh):
                            if ht == 0 and tc == 0:
                                p.wait(t, "gi", GIW)
                            if ht * TCH + tc >= 2:
                                p.wait(t, "sv", svh)
                            m = None
                            for c in range(8):
                                m = t.matmul(bank[:], wq1[:, c, ht * 128:(ht + 1) * 128],
                                             XgT[:, c, tc * 512:(tc + 1) * 512],
                                             start=(c == 0), stop=(c == 7))
                            p.inc(m, "tt")
                        block.tensor(m1_t)
                        TTM = p.val("tt")

                        def m1_s(sc, ht=ht, tc=tc, hb=hb, bank=bank, TTM=TTM, SVBQ=SVBQ):
                            p.wait(sc, "tt", TTM)
                            if ht == 0 and tc == 0:
                                p.wait(sc, "sv", SVBQ)
                            sc.activation(h1q[:, hb, tc * 512:(tc + 1) * 512],
                                          bank[:], AF.Silu, bias=bq1[:, hb:hb + 1])
                            sc.drain()
                            p.inc(sc.nop(), "sv")
                        block.scalar(m1_s)
            SV_H1 = p.val("sv")
            for wq in range(4):
                def wl2_g(g, idx=idx, wq=wq, SV_H1=SV_H1):
                    p.wait(g, "sv")
                    g.dma_start(out=wq2[:], in_=ew_d[(1, 1)][idx, :, wq * 512:(wq + 1) * 512]
                                .rearrange("(k q2) n -> q2 k n", q2=128)).then_inc(p.c["gi"][0], 16)
                    g.dma_start(out=bqs[:], in_=eb_d[(1, 1)][idx, wq * 512:(wq + 1) * 512]
                                .rearrange("(h q2) -> h q2", h=4)).then_inc(p.c["gi"][0], 16)
                    g.dma_start(out=wq3_4, in_=ew_d[(1, 2)][idx, wq * 512:(wq + 1) * 512, :]
                                .rearrange("(k q2) n -> q2 k n", q2=128)).then_inc(p.c["gi"][0], 16)
                    p.c["gi"][1] += 48
                block.gpsimd(wl2_g)
                GIW2 = p.val("gi")

                def bqt2_t(t, GIW2=GIW2):
                    p.wait(t, "gi", GIW2)
                    i1 = t.transpose(pb[3][:, 0:4], bqs[0:4, 0:128], identf[0:4, 0:4])
                    p.inc(i1, "tt")
                block.tensor(bqt2_t)
                TTBQ2 = p.val("tt")

                def bqt2_v(v, TTBQ2=TTBQ2):
                    p.wait(v, "tt", TTBQ2)
                    v.tensor_copy(bq2[:, 0:4], pb[3][:, 0:4])
                    v.drain()
                    p.inc(v.nop(), "sv")
                block.vector(bqt2_v)
                SVBQ2 = p.val("sv")

                for ht in range(4):
                    for tc in range(TCH):
                        bank = pb[(ht * TCH + tc) % 2]
                        svh = p.val("sv")

                        def m2e_t(t, ht=ht, tc=tc, bank=bank, GIW2=GIW2, svh=svh, SV_H1=SV_H1):
                            if ht == 0 and tc == 0:
                                p.wait(t, "gi", GIW2)
                                p.wait(t, "sv", SV_H1)
                            if ht * TCH + tc >= 2:
                                p.wait(t, "sv", svh)
                            m = None
                            for hc in range(16):
                                m = t.matmul(bank[:], wq2[:, hc, ht * 128:(ht + 1) * 128],
                                             h1q[:, hc, tc * 512:(tc + 1) * 512],
                                             start=(hc == 0), stop=(hc == 15))
                            p.inc(m, "tt")
                        block.tensor(m2e_t)
                        TTM = p.val("tt")

                        def m2e_s(sc, ht=ht, tc=tc, bank=bank, TTM=TTM, SVBQ2=SVBQ2):
                            p.wait(sc, "tt", TTM)
                            if ht == 0 and tc == 0:
                                p.wait(sc, "sv", SVBQ2)
                            sc.activation(h2q[:, ht, tc * 512:(tc + 1) * 512],
                                          bank[:], AF.Silu, bias=bq2[:, ht:ht + 1])
                            sc.drain()
                            p.inc(sc.nop(), "sv")
                        block.scalar(m2e_s)
                SV_H2 = p.val("sv")
                for tt_ in range(GT):
                    for dc in range(2):
                        bank = pb[4 + (tt_ * 2 + dc) % 2]
                        svh = p.val("sv")

                        def m3_t(t, tt_=tt_, dc=dc, bank=bank, svh=svh, SV_H2=SV_H2):
                            if tt_ == 0 and dc == 0:
                                p.wait(t, "sv", SV_H2)
                            if tt_ * 2 + dc >= 2:
                                p.wait(t, "sv", svh)
                            m = None
                            for hc in range(4):
                                m = t.matmul(bank[:], h2q[:, hc, tt_ * 128:(tt_ + 1) * 128],
                                             wq3_4[:, hc, dc * 512:(dc + 1) * 512],
                                             start=(hc == 0), stop=(hc == 3))
                            p.inc(m, "tt")
                        block.tensor(m3_t)
                        TTM3 = p.val("tt")

                        def m3_v(v, tt_=tt_, dc=dc, wq=wq, e=e, bank=bank, TTM3=TTM3):
                            p.wait(v, "tt", TTM3)
                            if tt_ == 0:
                                if wq == 0:
                                    v.tensor_copy(Cacc[:, dc * 512:(dc + 1) * 512], bank[:])
                                else:
                                    v.tensor_add(Cacc[:, dc * 512:(dc + 1) * 512],
                                                 Cacc[:, dc * 512:(dc + 1) * 512], bank[:])
                            else:
                                v.scalar_tensor_tensor(comb[:, tt_, dc * 512:(dc + 1) * 512],
                                                       bank[:], prg[:, tt_, e:e + 1],
                                                       comb[:, tt_, dc * 512:(dc + 1) * 512],
                                                       ALU.mult, ALU.add)
                            v.drain()
                            p.inc(v.nop(), "sv")
                        block.vector(m3_v)

        def eb_s(s, fam=fam, idx=idx):
            lastb = eb_d[(fam, len(FAMW[fam]) - 1)]
            p.wait(s, "sv")
            s.dma_start(out=ebrow[0:1, 0:1024], in_=lastb[idx:idx + 1, :]).then_inc(p.c["dh"][0], 16)
            p.c["dh"][1] += 16
        block.sync(eb_s)
        DHB = p.val("dh")

        def eb_t(t, DHB=DHB):
            p.wait(t, "dh", DHB)
            m = t.matmul(pb[6][:], onesf[0:1, :], ebrow[0:1, 0:512], start=True, stop=True)
            p.inc(m, "tt")
            m = t.matmul(pb[7][:], onesf[0:1, :], ebrow[0:1, 512:1024], start=True, stop=True)
            p.inc(m, "tt")
        block.tensor(eb_t)
        TTB2 = p.val("tt")

        def eb_v(v, e=e, TTB2=TTB2):
            p.wait(v, "tt", TTB2)
            v.tensor_copy(bxb[:, 0:512], pb[6][:])
            v.tensor_copy(bxb[:, 512:1024], pb[7][:])
            v.drain()
            for tt_ in range(1, GT):
                v.scalar_tensor_tensor(comb[:, tt_, :], bxb[:], prg[:, tt_, e:e + 1],
                                       comb[:, tt_, :], ALU.mult, ALU.add)
            v.tensor_add(Cacc[:], Cacc[:], bxb[:])
            v.drain()
            v.tensor_copy(crow[:], Cacc[0:1, :])
            v.drain()
            p.inc(v.nop(), "sv")
        block.vector(eb_v)
        SVEB = p.val("sv")

        def eb_g(g, e=e, SVEB=SVEB):
            p.wait(g, "sv", SVEB)
            g.dma_start(out=Cbf[e:e + 1, :], in_=crow[:]).then_inc(p.c["gi"][0], 16)
            p.c["gi"][1] += 16
            g.wait_ge(p.c["gi"][0], p.c["gi"][1])
        block.gpsimd(eb_g)

    SV_EXP = p.val("sv")

    DH_CZ = g_["DH_CZ"]

    def sc_g(g):
        p.wait(g, "sv", SV_EXP)
        g.wait_ge(p.c["dh"][0], DH_CZ)
        for gti in range(NG):
            g.indirect_dma_start(
                out=comb_dram[:], out_offset=bass.IndirectOffsetOnAxis(ap=idxg[:, gti:gti + 1], axis=0),
                in_=comb[:, gti + 1, :], in_offset=None,
                bounds_check=TOK - 1, oob_is_err=False,
            ).then_inc(p.c["gi"][0], 16)
            p.c["gi"][1] += 16
        g.wait_ge(p.c["gi"][0], p.c["gi"][1])
    block.gpsimd(sc_g)
    GI_SC = p.val("gi")

    for gg in reversed(s3):
        gg.__exit__(None, None, None)
    fin = sb("fin", [128, C], F32)
    x1t = sb("x1t", [128, 2, C], F32)
    cmt = sb("cmt", [128, 2, C], F32)
    ysb = sb("ysb", [128, 2, C], F32)
    sq2 = sb("sq2", [128, C], F32)

    for i in range(TILES):
        par = i % 2

        def f_s(s, i=i, par=par):
            if i == 0:
                p.wait(s, "gi", GI_SC)
            if i >= 2:
                p.wait(s, "sv", mile[("finfree", i - 2)])
                s.wait_ge(p.c["do"][0], mile[("ydma", i - 2)])
            s.dma_start(out=x1t[:, par, :], in_=x1_d[i * 128:(i + 1) * 128, :]).then_inc(p.c["dh"][0], 16)
            s.dma_start(out=cmt[:, par, :], in_=comb_dram[i * 128:(i + 1) * 128, :]).then_inc(p.c["dh"][0], 16)
            p.c["dh"][1] += 32
        block.sync(f_s)
        DHF = p.val("dh")

        def f_t0(t, i=i):
            if i == 0:
                p.wait(t, "sv", SV_EXP)
            else:
                p.wait(t, "sv", mile[("prT", i - 1)])
            i1 = t.transpose(pb[2][0:8, 0:128], probs[:, i, :], identf[:])
            p.inc(i1, "tt")
        block.tensor(f_t0)
        TTF0 = p.val("tt")

        def f_v0(v, TTF0=TTF0, i=i):
            p.wait(v, "tt", TTF0)
            v.tensor_copy(prT[:], pb[2][0:8, 0:128])
            v.drain()
            mile[("prT", i)] = p.inc(v.nop(), "sv")
        block.vector(f_v0)
        SVF0 = p.val("sv")

        def f_t1(t, i=i, SVF0=SVF0):
            p.wait(t, "sv", SVF0)
            if i == 0:
                p.wait(t, "gi", GI_SC)
            if i > 0:
                p.wait(t, "sv", mile[("basefree", i - 1)])
            m = t.matmul(pb[0][:], prT[:], Cbf[:, 0:512], start=True, stop=True)
            p.inc(m, "tt")
            m = t.matmul(pb[1][:], prT[:], Cbf[:, 512:1024], start=True, stop=True)
            p.inc(m, "tt")
        block.tensor(f_t1)
        TTF1 = p.val("tt")

        def f_v1(v, i=i, par=par, TTF1=TTF1, DHF=DHF):
            p.wait(v, "tt", TTF1)
            p.wait(v, "dh", DHF)
            mb = msku[:, i:i + 1].broadcast_to((128, 512))
            v.select(fin[:, 0:512], mb, cmt[:, par, 0:512], pb[0][:])
            v.select(fin[:, 512:1024], mb, cmt[:, par, 512:1024], pb[1][:])
            v.drain()
            mile[("basefree", i)] = p.inc(v.nop(), "sv")
        block.vector(f_v1)

        def f_s1(sc):
            p.wait(sc, "sv")
            sc.activation(sq2[:], fin[:], AF.Square, accum_out=st[:, 4:5])
            sc.activation(st[:, 5:6], st[:, 4:5], AF.Copy, scale=INV_C)
            p.inc(sc.drain(), "ss")
        block.scalar(f_s1)

        def f_v2(v):
            p.wait(v, "ss")
            v.reduce_sum(st[:, 0:1], fin[:], axis=AX.X)
            v.drain()
            v.tensor_scalar_mul(st[:, 2:3], st[:, 0:1], INV_C)
            v.drain()
            v.tensor_mul(st[:, 1:2], st[:, 2:3], st[:, 2:3])
            v.drain()
            v.tensor_sub(st[:, 3:4], st[:, 5:6], st[:, 1:2])
            p.inc(v.drain(), "sv")
        block.vector(f_v2)

        def f_s2(sc):
            p.wait(sc, "sv")
            sc.activation(st[:, 7:8], st[:, 3:4], AF.Ln, bias=cpool[:, 0:1])
            sc.drain()
            sc.activation(st[:, 6:7], st[:, 7:8], AF.Exp, scale=-0.5)
            p.inc(sc.drain(), "ss")
        block.scalar(f_s2)

        def f_v3(v, i=i, par=par):
            p.wait(v, "ss")
            v.tensor_scalar(fin[:], fin[:], st[:, 2:3], st[:, 6:7], ALU.subtract, ALU.mult)
            v.drain()
            v.tensor_add(ysb[:, par, :], x1t[:, par, :], fin[:])
            v.drain()
            mile[("finfree", i)] = p.inc(v.nop(), "sv")
        block.vector(f_v3)
        SVF3 = p.val("sv")

        def f_o(s, i=i, par=par, SVF3=SVF3):
            p.wait(s, "sv", SVF3)
            s.dma_start(out=y_d[i * 128:(i + 1) * 128, :], in_=ysb[:, par, :]).then_inc(p.c["do"][0], 16)
            p.c["do"][1] += 16
            mile[("ydma", i)] = p.c["do"][1]
        block.sync(f_o)

    def tail(s):
        s.wait_ge(p.c["do"][0], p.c["do"][1])
    block.sync(tail)

    blk.__exit__(None, None, None)
    for gg in reversed(ctx):
        gg.__exit__(None, None, None)
    for gg in p.ctxs:
        gg.__exit__(None, None, None)
    return nc




def l2_inputs(inp, x1, xm16, imp_own, n1_own, imp_all, noise_perm):
    A = lambda k: np.ascontiguousarray(np.asarray(inp["L0_ffn_" + k], dtype=np.float32))
    gvec = np.zeros((4, C), np.float32)
    gvec[0] = A("gate_b1").reshape(-1)
    gvec[1] = A("ln_scale").reshape(-1)
    gvec[2] = A("ln_offset").reshape(-1)
    gsml = np.zeros((2, 512), np.float32)
    gsml[0] = A("gate_b2").reshape(-1)
    gsml[1, 0:8] = A("gate_b3").reshape(-1)
    gsml[1, 8:16] = A("aff_b").reshape(-1)
    pp = np.arange(128)[:, None]
    ii = np.arange(128)[None, :]
    d = {
        "x1": np.ascontiguousarray(x1), "xm16": np.ascontiguousarray(xm16),
        "imp_own": np.ascontiguousarray(imp_own), "n1_own": np.ascontiguousarray(n1_own),
        "imp_all": np.ascontiguousarray(imp_all), "noise": np.ascontiguousarray(noise_perm),
        "eu": A("expert_usage").reshape(1, 8),
        "gw1": A("gate_w1"), "gw2": A("gate_w2"), "gw3": A("gate_w3"),
        "aw": A("aff_w"), "gvec": gvec, "gsml": gsml,
        "suo": (pp < ii).astype(np.float32),
        "identf": np.eye(128, dtype=np.float32),
    }
    for f in (0, 1, 2):
        for li in range(len(FAMW[f])):
            d[f"e{f}w{li + 1}"] = A(f"e{f}_w{li + 1}")
            d[f"e{f}b{li + 1}"] = A(f"e{f}_b{li + 1}")
    return d


def _noise_perm(noise_tok):
    TOK = noise_tok.shape[0]
    TILES = TOK // 128
    return np.ascontiguousarray(
        noise_tok.reshape(TILES, 128, 8).transpose(1, 0, 2).reshape(128, TILES * 8))


def kernel(**inputs):
    import jax
    T = 4096
    x = np.asarray(inputs["L0_attn_x"], dtype=np.float32)
    BT = x.shape[0]
    with jax.default_device(jax.devices("cpu")[0]):
        noise = np.asarray(jax.random.normal(jax.random.key(1), (BT, T, 8)), np.float32)

    nc1 = build_l1(T)
    maps1 = []
    for c in range(NCORES):
        xc = x[B_CORE * c:B_CORE * (c + 1)].reshape(-1, C)
        maps1.append(l1_inputs(xc, inputs))
    r1 = run_bass_kernel_spmd(nc1, maps1, core_ids=list(range(NCORES))).results

    imp_all = np.concatenate([r1[c]["impo"] for c in range(NCORES)], axis=1)
    n_all = imp_all.size
    nc2 = build_l2(T, n_all)
    maps2 = []
    for c in range(NCORES):
        nz = noise[B_CORE * c:B_CORE * (c + 1)].reshape(-1, 8)
        maps2.append(l2_inputs(inputs, r1[c]["x1o"], r1[c]["xmo"], r1[c]["impo"],
                               r1[c]["n1o"], imp_all, _noise_perm(nz)))
    r2 = run_bass_kernel_spmd(nc2, maps2, core_ids=list(range(NCORES))).results
    y = np.concatenate([r2[c]["y"].reshape(B_CORE, T, C) for c in range(NCORES)], axis=0)
    return y



# version 04
# speedup vs baseline: 1.3698x; 1.0993x over previous
import numpy as np
import concourse.bass as bass
import concourse.mybir as mybir
from concourse.bass_utils import run_bass_kernel_spmd

F32 = mybir.dt.float32
BF16 = mybir.dt.bfloat16
I32 = mybir.dt.int32
U8 = mybir.dt.uint8
AF = mybir.ActivationFunctionType
ALU = mybir.AluOpType
AX = mybir.AxisListType

C = 1024
H = 16
N = 64
NCORES = 8
B_CORE = 2
L = 128
INV_C = 1.0 / C


class P:
    def __init__(self, nc, semnames):
        self.nc = nc
        self.c = {}
        self.ctxs = []
        for nm in semnames:
            g = nc.semaphore(nm)
            self.c[nm] = [g.__enter__(), 0]
            self.ctxs.append(g)

    def inc(self, inst, nm, n=1):
        self.c[nm][1] += n
        inst.then_inc(self.c[nm][0], n)
        return self.c[nm][1]

    def val(self, nm):
        return self.c[nm][1]

    def wait(self, eng, nm, v=None):
        eng.wait_ge(self.c[nm][0], self.c[nm][1] if v is None else v)


def build_l1(T, debug=()):
    NT = T // L
    TILES = B_CORE * NT
    nc = bass.Bass("TRN2", target_bir_lowering=False, detect_race_conditions=False)

    x_d = nc.dram_tensor("x", [TILES * L, C], F32, kind="ExternalInput").ap()
    Wd = {w: nc.dram_tensor(w, [C, C], F32, kind="ExternalInput").ap()
          for w in ("Wr", "Wk", "Wv", "Wo")}
    w1_d = nc.dram_tensor("w1", [C, 64], F32, kind="ExternalInput").ap()
    a1_d = nc.dram_tensor("a1", [C, 64], F32, kind="ExternalInput").ap()
    g1_d = nc.dram_tensor("g1", [C, 128], F32, kind="ExternalInput").ap()
    w2_d = nc.dram_tensor("w2", [64, C], F32, kind="ExternalInput").ap()
    a2_d = nc.dram_tensor("a2", [64, C], F32, kind="ExternalInput").ap()
    g2_d = nc.dram_tensor("g2", [128, C], F32, kind="ExternalInput").ap()
    vecs_d = nc.dram_tensor("vecs", [13, C], F32, kind="ExternalInput").ap()
    identf_d = nc.dram_tensor("identf", [128, 128], F32, kind="ExternalInput").ap()
    masks_d = nc.dram_tensor("masks", [128, 384], U8, kind="ExternalInput").ap()

    x1_d = nc.dram_tensor("x1o", [TILES * L, C], F32, kind="ExternalOutput").ap()
    xm_d = nc.dram_tensor("xmo", [TILES * L, C], BF16, kind="ExternalOutput").ap()
    imp_d = nc.dram_tensor("impo", [128, TILES], F32, kind="ExternalOutput").ap()
    n1_d = nc.dram_tensor("n1o", [128, TILES], F32, kind="ExternalOutput").ap()
    dbg_d = {nm: nc.dram_tensor("dbg_" + nm, [TILES * L, C], F32, kind="ExternalOutput").ap()
             for nm in debug}

    ctx = []

    def sb(name, shape, dtype=F32):
        g = nc.sbuf_tensor(name, shape, dtype)
        t = g.__enter__()
        ctx.append(g)
        return t

    Ws = {w: sb(w + "_s", [128, 8, C], BF16) for w in ("Wr", "Wk", "Wv", "Wo")}
    lora1 = sb("lora1", [128, 16, 256], BF16)
    w2s = sb("w2s", [64, C], BF16)
    a2s = sb("a2s", [64, C], BF16)
    g2s = sb("g2s", [128, C], BF16)
    lam = sb("lam", [128, 8, 8], F32)
    identf = sb("identf_s", [128, 128], F32)
    masks = sb("masks_s", [128, 384], U8)
    zer_f = sb("zer_f", [128, 128], F32)
    onesf = sb("onesf", [128, 128], F32)
    cpool = sb("cpool", [128, 8], F32)
    kkb = sb("kkb", [128, C], BF16)
    kab = sb("kab", [128, C], BF16)
    omkab = sb("omkab", [128, C], BF16)
    rkb = sb("rkb", [128, C], BF16)
    gab = sb("gab", [128, C], BF16)
    beb = sb("beb", [128, C], BF16)
    w0b = sb("w0b", [128, C], BF16)
    a0b = sb("a0b", [128, C], BF16)
    ictx = []

    def sbi(name, shape, dtype=F32):
        g = nc.sbuf_tensor(name, shape, dtype)
        t = g.__enter__()
        ictx.append(g)
        return t
    l1f = sbi("l1f", [128, 8, 256], F32)
    vrow = {nm: sbi("vr_" + nm, [1, C], F32)
            for nm in ("k_k", "k_a", "rk", "gam", "bet", "w0", "a0")}

    pb = []
    for i in range(8):
        g = nc.psum_tensor(f"pb{i}", [128, 512], F32)
        pb.append(g.__enter__())
        ctx.append(g)
    MSL = masks[:, 0:128]
    MSU = masks[:, 128:256]
    MIU = masks[:, 256:384]

    blk = nc.Block()
    block = blk.__enter__()
    p = P(nc, ["dx0", "dx1", "dw", "dg", "do0", "do1", "dm", "dbgs", "sv", "ss", "tt"])
    dxn = ["dx0", "dx1"]
    don = ["do0", "do1"]
    mile = {}

    def init_sync(s):
        s.dma_start(out=identf[:], in_=identf_d[:]).then_inc(p.c["dm"][0], 16)
        s.dma_start(out=masks[:], in_=masks_d[:]).then_inc(p.c["dm"][0], 16)
        for j in range(6):
            s.dma_start(out=lam[:, :, j:j + 1],
                        in_=vecs_d[j:j + 1, :].rearrange("o (k q) -> q k o", q=128),
                        allow_slow_non_contiguous=True).then_inc(p.c["dm"][0], 16)
        p.c["dm"][1] += 5 * 16
        for j, nm in enumerate(("w0", "a0", "k_k", "k_a", "rk", "gam", "bet")):
            s.dma_start(out=vrow[nm][:], in_=vecs_d[6 + j:7 + j, :]).then_inc(p.c["dm"][0], 16)
        p.c["dm"][1] += 10 * 16
    block.sync(init_sync)

    def init_gp(g):
        for w in ("Wr", "Wk", "Wv", "Wo"):
            g.dma_start(out=Ws[w][:], in_=Wd[w].rearrange("(k q) n -> q k n", q=128)).then_inc(p.c["dw"][0], 16)
        g.dma_start(out=w2s[:], in_=w2_d[:]).then_inc(p.c["dw"][0], 16)
        g.dma_start(out=a2s[:], in_=a2_d[:]).then_inc(p.c["dw"][0], 16)
        g.dma_start(out=g2s[:], in_=g2_d[:]).then_inc(p.c["dw"][0], 16)
        g.dma_start(out=l1f[:, :, 0:64], in_=w1_d.rearrange("(k q) n -> q k n", q=128)).then_inc(p.c["dw"][0], 16)
        g.dma_start(out=l1f[:, :, 64:128], in_=a1_d.rearrange("(k q) n -> q k n", q=128)).then_inc(p.c["dw"][0], 16)
        g.dma_start(out=l1f[:, :, 128:256], in_=g1_d.rearrange("(k q) n -> q k n", q=128)).then_inc(p.c["dw"][0], 16)
        p.c["dw"][1] += 10 * 16
    block.gpsimd(init_gp)

    def init_vec(v):
        for col, val in ((0, 1e-5), (1, -0.5), (2, 1.0), (3, 6.4e-4)):
            v.memset(cpool[:, col:col + 1], val)
        v.memset(onesf[:], 1.0)
        v.memset(zer_f[:], 0.0)
        p.wait(v, "dw")
        p.wait(v, "dm")
        for k in range(8):
            v.tensor_copy(lora1[:, k, :], l1f[:, k, :])
            v.tensor_scalar_mul(lora1[:, 8 + k, 0:64], l1f[:, k, 0:64], lam[:, k, 3:4])
            v.tensor_scalar_mul(lora1[:, 8 + k, 64:128], l1f[:, k, 64:128], lam[:, k, 4:5])
            v.tensor_scalar_mul(lora1[:, 8 + k, 128:256], l1f[:, k, 128:256], lam[:, k, 5:6])
        p.inc(v.drain(), "sv")
    block.vector(init_vec)
    SV_ONES = p.val("sv")

    bnames = (("k_k", kkb), ("k_a", kab), ("rk", rkb), ("gam", gab),
              ("bet", beb), ("w0", w0b), ("a0", a0b))
    for bi, (nm, dst) in enumerate(bnames):
        def tsec(t, nm=nm, bi=bi):
            if bi == 0:
                p.wait(t, "dm")
                p.wait(t, "sv", SV_ONES)
            else:
                p.wait(t, "ss")
            m = t.matmul(pb[0][:], onesf[0:1, :], vrow[nm][0:1, 0:512], start=True, stop=True)
            p.inc(m, "tt")
            m = t.matmul(pb[1][:], onesf[0:1, :], vrow[nm][0:1, 512:1024], start=True, stop=True)
            p.inc(m, "tt")
        block.tensor(tsec)

        def ssec(sc, dst=dst):
            p.wait(sc, "tt")
            sc.copy(dst[:, 0:512], pb[0][:])
            sc.copy(dst[:, 512:1024], pb[1][:])
            p.inc(sc.drain(), "ss")
        block.scalar(ssec)
    SS_BC = p.val("ss")

    def init_vec2(v):
        p.wait(v, "ss", SS_BC)
        v.tensor_scalar(omkab[:], kab[:], -1.0, 1.0, ALU.mult, ALU.add)
        p.inc(v.drain(), "sv")
    block.vector(init_vec2)

    for g in reversed(ictx):
        g.__exit__(None, None, None)
    xt = sb("xt", [128, 2, C])
    st = sb("st", [128, 16])
    ln0 = sb("ln0", [128, C])
    xreg = sb("xreg", [128, C])
    sqs = xreg
    lwp = xreg
    lnat = sb("lnat", [128, 16, 128], BF16)
    lnxx = lnat
    prevc = sb("prevc", [128, 8])
    mixr = sb("mixr", [128, 8, 128], BF16)
    mixk = sb("mixk", [128, 8, 128], BF16)
    mixv = sb("mixv", [128, 8, 128], BF16)
    r_s = sb("r_s", [128, C])
    k_s = sb("k_s", [128, C])
    v_s = sb("v_s", [128, C])
    t1_s = sb("t1_s", [128, C])
    t2_s = sb("t2_s", [128, C])
    asig = sb("asig", [128, C])
    g_s = sb("g_s", [128, C])
    win_b = sb("win_b", [128, 64])
    ain_b = sb("ain_b", [128, 64])
    gin_b = sb("gin_b", [128, 128])
    winT = sb("winT", [64, 128], BF16)
    ainT = sb("ainT", [64, 128], BF16)
    ginT = sb("ginT", [128, 128], BF16)
    kkn = sb("kkn", [128, C])
    kt_s = sb("kt_s", [128, C])
    bsc = sb("bsc", [128, C])
    ss16 = sb("ss16", [128, 16])
    rn16 = sb("rn16", [128, 16])
    cum = sb("cum", [128, 8, 128])
    lncm = cum
    dd = sb("dd", [128, 8, 128], BF16)
    dLf = sb("dLf", [128, 8])
    ddi = sb("ddi", [128, 8, 128], BF16)
    dldc = cum
    bh_c = mixr
    kh_c = mixk
    rt_c = mixv
    btl = sb("btl", [128, C], BF16)
    ktl = sb("ktl", [128, C], BF16)
    vyc = sb("vyc", [128, C], BF16)
    v_b = vyc
    o_s = sb("o_s", [128, C])
    y_s = o_s
    ao_s = ln0
    x1_s = sb("x1_s", [128, 2, C])
    xm_s = t1_s
    impacc = sb("impacc", [128, TILES])
    n1acc = sb("n1acc", [128, TILES])
    stf2 = sb("stf2", [128, H * 64])
    stb2 = sb("stb2", [128, H * 64], BF16)
    wsn128 = ("A", "AT", "F1T", "CT", "GbT", "GkT", "F2", "F3", "F4", "F5", "F6",
              "F7", "Q0", "Q1", "TT")
    wsn64 = ("CV", "M0", "U")
    ws = [dict([(n, sb(f"ws{i}{n}", [128, 128], BF16)[:]) for n in wsn128]
               + [(n, sb(f"ws{i}{n}", [128, 64], BF16)[:]) for n in wsn64])
          for i in range(2)]

    for i in range(TILES):
        b = i // NT
        chunk = i % NT
        par = i % 2

        def s_load(s, i=i, par=par):
            if i == 0:
                p.wait(s, "ss", SS_BC)
                p.wait(s, "sv", SV_ONES)
            if i >= 2:
                p.wait(s, "sv", mile[("xfree", i - 2)])
            s.dma_start(out=xt[:, par, :], in_=x_d[i * L:(i + 1) * L, :]).then_inc(p.c[dxn[par]][0], 16)
            p.c[dxn[par]][1] += 16
        block.sync(s_load)
        xv = p.val(dxn[par])

        def s_sumsq(sc, par=par, xv=xv, i=i):
            p.wait(sc, dxn[par], xv)
            if i > 0:
                p.wait(sc, "sv", mile[("mix", i - 1)])
            sc.activation(sqs[:], xt[:, par, :], AF.Square, accum_out=st[:, 4:5])
            sc.activation(st[:, 5:6], st[:, 4:5], AF.Copy, scale=INV_C)
            p.inc(sc.drain(), "ss")
        block.scalar(s_sumsq)

        def s_var(v, par=par, xv=xv, i=i):
            p.wait(v, dxn[par], xv)
            if i > 0:
                p.wait(v, "tt", mile[("trdone", i - 1)])
            v.reduce_sum(st[:, 0:1], xt[:, par, :], axis=AX.X)
            p.wait(v, "ss")
            v.tensor_scalar_mul(st[:, 2:3], st[:, 0:1], INV_C)
            v.drain()
            v.tensor_mul(st[:, 1:2], st[:, 2:3], st[:, 2:3])
            v.drain()
            v.tensor_sub(st[:, 3:4], st[:, 5:6], st[:, 1:2])
            p.inc(v.drain(), "sv")
        block.vector(s_var)

        def s_rstd(sc):
            p.wait(sc, "sv")
            sc.activation(st[:, 7:8], st[:, 3:4], AF.Ln, bias=cpool[:, 0:1])
            sc.drain()
            sc.activation(st[:, 6:7], st[:, 7:8], AF.Exp, scale=-0.5)
            p.inc(sc.drain(), "ss")
        block.scalar(s_rstd)

        def s_ln0(v, par=par):
            p.wait(v, "ss")
            i1 = v.tensor_scalar(ln0[:], xt[:, par, :], st[:, 2:3], st[:, 6:7],
                                 ALU.subtract, ALU.mult)
            p.inc(i1, "sv")
        block.vector(s_ln0)
        SV_LN0 = p.val("sv")

        tt0 = p.val("tt")
        sv0 = p.val("sv")
        ss0 = p.val("ss")
        trev = [sv0 + k + 1 for k in range(8)]
        trevs = [ss0 + k + 1 for k in range(8)]

        def s_tr(t, tt0=tt0, trev=trev, trevs=trevs, i=i):
            p.wait(t, "sv", SV_LN0)
            for k in range(8):
                if k >= 2:
                    p.wait(t, "sv", trev[k - 2])
                    p.wait(t, "ss", trevs[k - 2])
                i1 = t.transpose(pb[2 + (k % 2)][:, 0:128], ln0[:, k * 128:(k + 1) * 128], identf[:])
                p.inc(i1, "tt")
            mile[("trdone", i)] = p.val("tt")
        block.tensor(s_tr)

        for k in range(8):
            def s_trev_v(v, k=k, tt0=tt0):
                p.wait(v, "tt", tt0 + k + 1)
                i1 = v.tensor_copy(lncm[:, k, :], pb[2 + (k % 2)][:, 0:128])
                assert p.inc(i1, "sv") == trev[k]
            block.vector(s_trev_v)

            def s_trev_s(sc, k=k, tt0=tt0):
                p.wait(sc, "tt", tt0 + k + 1)
                sc.copy(lnxx[:, k, :], pb[2 + (k % 2)][:, 0:128])
                sc.drain()
                assert p.inc(sc.nop(), "ss") == trevs[k]
            block.scalar(s_trev_s)

        def s_mix(v, chunk=chunk, i=i):
            p.wait(v, "sv", trev[7])
            if i > 0:
                p.wait(v, "tt", mile[("projdone", i - 1)])
            for k in range(8):
                v.tensor_sub(xreg[:, k * 128 + 1:k * 128 + 128], lncm[:, k, 0:127], lncm[:, k, 1:128])
                if chunk == 0:
                    v.tensor_scalar_mul(xreg[:, k * 128:k * 128 + 1], lncm[:, k, 0:1], -1.0)
                else:
                    v.tensor_sub(xreg[:, k * 128:k * 128 + 1], prevc[:, k:k + 1], lncm[:, k, 0:1])
            v.drain()
            v.tensor_copy(prevc[:], lncm[:, :, 127])
            for k in range(8):
                v.scalar_tensor_tensor(mixr[:, k, :], xreg[:, k * 128:(k + 1) * 128], lam[:, k, 0:1],
                                       lncm[:, k, :], ALU.mult, ALU.add)
                v.scalar_tensor_tensor(mixk[:, k, :], xreg[:, k * 128:(k + 1) * 128], lam[:, k, 1:2],
                                       lncm[:, k, :], ALU.mult, ALU.add)
                v.scalar_tensor_tensor(mixv[:, k, :], xreg[:, k * 128:(k + 1) * 128], lam[:, k, 2:3],
                                       lncm[:, k, :], ALU.mult, ALU.add)
                i1 = v.tensor_copy(lnxx[:, 8 + k, :], xreg[:, k * 128:(k + 1) * 128])
            p.inc(i1, "sv")
        block.vector(s_mix)
        MIX_SV = p.val("sv")
        mile[("mix", i)] = MIX_SV

        def s_proj(t, i=i):
            p.wait(t, "sv", MIX_SV)
            p.wait(t, "ss", trevs[7])
            if i == 0:
                p.wait(t, "dw")
                p.wait(t, "ss", SS_BC)
            else:
                p.wait(t, "ss", mile[("projfree", i - 1)])
                p.wait(t, "sv", mile[("projfreev", i - 1)])
                p.wait(t, "ss", mile[("lorafree", i - 1)])
            for pi, (mix, wname) in enumerate(((mixr, "Wr"), (mixk, "Wk"), (mixv, "Wv"))):
                b0, b1 = pb[2 * pi], pb[2 * pi + 1]
                m0 = m1 = None
                for k in range(8):
                    m0 = t.matmul(b0[:], mix[:, k, :], Ws[wname][:, k, 0:512],
                                  start=(k == 0), stop=(k == 7))
                    m1 = t.matmul(b1[:], mix[:, k, :], Ws[wname][:, k, 512:1024],
                                  start=(k == 0), stop=(k == 7))
                p.inc(m0, "tt")
                p.inc(m1, "tt")
            mlast = None
            for k in range(16):
                mlast = t.matmul(pb[6][:, 0:256], lnxx[:, k, :], lora1[:, k, :],
                                 start=(k == 0), stop=(k == 15))
            p.inc(mlast, "tt")
            mile[("projdone", i)] = p.val("tt")
        block.tensor(s_proj)
        TT_PROJ = p.val("tt")

        def s_pev(sc, i=i):
            p.wait(sc, "tt", TT_PROJ - 5)
            sc.copy(r_s[:, 0:512], pb[0][:])
            sc.copy(r_s[:, 512:1024], pb[1][:])
            p.wait(sc, "tt", TT_PROJ - 3)
            sc.copy(k_s[:, 0:512], pb[2][:])
            i1 = sc.copy(k_s[:, 512:1024], pb[3][:])
            mile[("projfree", i)] = p.inc(i1, "ss")
            p.wait(sc, "tt", TT_PROJ)
            sc.activation(win_b[:], pb[6][:, 0:64], AF.Tanh)
            sc.copy(ain_b[:], pb[6][:, 64:128])
            i1 = sc.activation(gin_b[:], pb[6][:, 128:256], AF.Sigmoid)
            mile[("lorafree", i)] = p.inc(i1, "ss")
        block.scalar(s_pev)
        SS_LORIN = p.val("ss")

        def s_pev_v(v, i=i):
            p.wait(v, "tt", TT_PROJ - 1)
            v.tensor_copy(v_s[:, 0:512], pb[4][:])
            i1 = v.tensor_copy(v_s[:, 512:1024], pb[5][:])
            mile[("projfreev", i)] = p.inc(i1, "sv")
        block.vector(s_pev_v)

        def s_l2(t):
            p.wait(t, "ss", SS_LORIN)
            i1 = t.transpose(pb[6][0:64, 0:128], win_b[:], identf[:])
            p.inc(i1, "tt")
            i2 = t.transpose(pb[7][0:64, 0:128], ain_b[:], identf[:])
            p.inc(i2, "tt")
            i3 = t.transpose(pb[6][:, 128:256], gin_b[:], identf[:])
            p.inc(i3, "tt")
        block.tensor(s_l2)
        TT_LT = p.val("tt")

        def s_l2ev(v):
            p.wait(v, "tt", TT_LT)
            v.tensor_copy(winT[:], pb[6][0:64, 0:128])
            v.tensor_copy(ainT[:], pb[7][0:64, 0:128])
            i1 = v.tensor_copy(ginT[:], pb[6][:, 128:256])
            p.inc(i1, "sv")
        block.vector(s_l2ev)
        SV_LT = p.val("sv")

        def s_l3(t, i=i):
            p.wait(t, "sv", SV_LT)
            for (lhsT, rhs, bk) in ((winT, w2s, 0), (ainT, a2s, 2), (ginT, g2s, 4)):
                m = t.matmul(pb[bk][:], lhsT[:], rhs[:, 0:512], start=True, stop=True)
                p.inc(m, "tt")
                m = t.matmul(pb[bk + 1][:], lhsT[:], rhs[:, 512:1024], start=True, stop=True)
                p.inc(m, "tt")
        block.tensor(s_l3)
        TT_L3 = p.val("tt")

        def s_wag_v(v, i=i):
            p.wait(v, "tt", TT_L3)
            if i > 0:
                v.wait_ge(p.c["dg"][0], mile[("xmdma", i - 1)])
            v.tensor_add(t1_s[:, 0:512], pb[0][:], w0b[:, 0:512])
            v.tensor_add(t1_s[:, 512:1024], pb[1][:], w0b[:, 512:1024])
            v.tensor_add(t2_s[:, 0:512], pb[2][:], a0b[:, 0:512])
            v.tensor_add(t2_s[:, 512:1024], pb[3][:], a0b[:, 512:1024])
            v.tensor_copy(g_s[:, 0:512], pb[4][:])
            i1 = v.tensor_copy(g_s[:, 512:1024], pb[5][:])
            p.inc(i1, "sv")
        block.vector(s_wag_v)
        SV_WAG = p.val("sv")

        def s_wag_s(sc):
            p.wait(sc, "sv", SV_WAG)
            sc.activation(lwp[:], t1_s[:], AF.Exp, scale=-1.0)
            sc.activation(t1_s[:], lwp[:], AF.Ln, bias=cpool[:, 2:3])
            sc.activation(lwp[:], t1_s[:], AF.Exp, scale=-1.0, bias=cpool[:, 1:2])
            i1 = sc.activation(asig[:], t2_s[:], AF.Sigmoid)
            p.inc(i1, "ss")
        block.scalar(s_wag_s)
        SS_WAG = p.val("ss")

        def s_sp1(v):
            p.wait(v, "ss", SS_WAG)
            v.tensor_mul(t1_s[:], k_s[:], kkb[:])
            v.tensor_mul(t2_s[:], t1_s[:], t1_s[:])
            v.reduce_sum(ss16[:], t2_s[:].rearrange("q (h n) -> q h n", h=H), axis=AX.X)
            p.inc(v.drain(), "sv")
        block.vector(s_sp1)

        def s_sp1s(sc):
            p.wait(sc, "sv")
            sc.activation(rn16[:], ss16[:], AF.Sqrt)
            p.inc(sc.drain(), "ss")
        block.scalar(s_sp1s)

        def s_sp2(v):
            p.wait(v, "ss")
            v.tensor_scalar_max(rn16[:], rn16[:], 1e-12)
            v.drain()
            v.reciprocal(ss16[:], rn16[:])
            v.drain()
            bc16 = ss16[:].rearrange("q (h o) -> q h o", o=1).broadcast_to((128, H, N))
            v.tensor_tensor(kkn[:].rearrange("q (h n) -> q h n", h=H), t1_s[:].rearrange("q (h n) -> q h n", h=H), bc16, ALU.mult)
            v.tensor_mul(bsc[:], kkn[:], asig[:])
            v.tensor_mul(t2_s[:], asig[:], kab[:])
            v.tensor_add(t2_s[:], t2_s[:], omkab[:])
            v.tensor_mul(kt_s[:], k_s[:], t2_s[:])
            i1 = v.tensor_copy(v_b[:], v_s[:])
            p.inc(i1, "sv")
        block.vector(s_sp2)
        SV_PREP = p.val("sv")

        tt1 = p.val("tt")
        sv1 = p.val("sv")
        lwev = [sv1 + k + 1 for k in range(8)]

        def s_ct1(t, tt1=tt1, lwev=lwev):
            p.wait(t, "sv", SV_PREP)
            for k in range(8):
                if k >= 2:
                    p.wait(t, "sv", lwev[k - 2])
                i1 = t.transpose(pb[2 + (k % 2)][:, 0:128], lwp[:, k * 128:(k + 1) * 128], identf[:])
                p.inc(i1, "tt")
        block.tensor(s_ct1)

        for k in range(8):
            def s_lwev(v, k=k, tt1=tt1):
                p.wait(v, "tt", tt1 + k + 1)
                i1 = v.tensor_tensor_scan(cum[:, k, :], pb[2 + (k % 2)][:, 0:128],
                                           zer_f[:], 0.0,
                                           ALU.add, ALU.bypass)
                assert p.inc(i1, "sv") == lwev[k]
            block.vector(s_lwev)

        def s_dexp(sc):
            p.wait(sc, "sv", lwev[7])
            sc.activation(dd[:].rearrange("q a b -> q (a b)"),
                          cum[:].rearrange("q a b -> q (a b)"), AF.Exp, scale=-1.0)
            sc.activation(ddi[:].rearrange("q a b -> q (a b)"),
                          cum[:].rearrange("q a b -> q (a b)"), AF.Exp)
            sc.activation(dLf[:], cum[:, :, 127], AF.Exp, scale=-1.0)
            i1 = None
            for k in range(8):
                i1 = sc.mul(asig[:, k * 128 + 1:k * 128 + 128], dd[:, k, 0:127], -1.0)
            p.inc(i1, "ss")
        block.scalar(s_dexp)

        def s_dsh0(v):
            p.wait(v, "ss")
            v.memset(asig[:].rearrange("q (a b) -> q a b", a=8)[:, :, 0:1], -1.0)
            v.tensor_copy(dLf[:], cumf_dL[:]) if False else None
            i1 = None
            for k in range(8):
                i1 = v.tensor_scalar_mul(dldc[:, k, :], ddi[:, k, :], dLf[:, k:k + 1])
            p.inc(i1, "sv")
        block.vector(s_dsh0)
        SV_DLD = p.val("sv")

        tt2 = p.val("tt")
        sv2 = p.val("sv")
        ctev = [sv2 + j + 1 for j in range(40)]

        def s_ct2(t, tt2=tt2, ctev=ctev):
            p.wait(t, "sv", SV_DLD)
            srcs = [kkn, bsc, kt_s, r_s, None]
            for si in range(5):
                for k in range(8):
                    idx = si * 8 + k
                    if idx >= 2:
                        p.wait(t, "sv", ctev[idx - 2])
                    if si == 4:
                        i1 = t.transpose(pb[2 + (idx % 2)][:, 0:128], dldc[:, k, :], identf[:])
                    else:
                        i1 = t.transpose(pb[2 + (idx % 2)][:, 0:128],
                                         srcs[si][:, k * 128:(k + 1) * 128], identf[:])
                    p.inc(i1, "tt")
        block.tensor(s_ct2)

        specs = ((None, None), (bh_c, ddi), (kh_c, ddi), (rt_c, dd), (None, None))
        for si in range(5):
            for k in range(8):
                idx = si * 8 + k
                def s_ctev(v, idx=idx, k=k, si=si):
                    dst, mulsrc = specs[si]
                    p.wait(v, "tt", tt2 + idx + 1)
                    bank = pb[2 + (idx % 2)]
                    if si == 4:
                        v.tensor_tensor(btl[:, k * 128:(k + 1) * 128], bank[:, 0:128],
                                        bsc[:, k * 128:(k + 1) * 128], ALU.mult)
                        i1 = v.tensor_tensor(ktl[:, k * 128:(k + 1) * 128], bank[:, 0:128],
                                             kt_s[:, k * 128:(k + 1) * 128], ALU.mult)
                    elif si == 0:
                        i1 = v.tensor_tensor(lnat[:, k, :], bank[:, 0:128],
                                             asig[:, k * 128:(k + 1) * 128], ALU.mult)
                    else:
                        i1 = v.tensor_tensor(dst[:, k, :], bank[:, 0:128], mulsrc[:, k, :], ALU.mult)
                    assert p.inc(i1, "sv") == ctev[idx]
                block.vector(s_ctev)
        SV_CT = p.val("sv")

        for hp in range(0, H, 2):
            pairs = (hp, hp + 1)
            ctx_h = {}
            for h in pairs:
                slot = b * H + h
                srow = 64 * (slot % 2)
                scol = (slot // 2) * 64
                hp_r = 64 * (h % 2)
                hch = h // 2
                ctx_h[h] = dict(
                    w=ws[h % 2],
                    BQ=[pb[4 * (h % 2) + j] for j in range(4)],
                    stbS=stb2[srow:srow + 64, scol:scol + 64],
                    stfS=stf2[srow:srow + 64, scol:scol + 64],
                    atS=lnat[hp_r:hp_r + 64, hch, :],
                    bhS=bh_c[hp_r:hp_r + 64, hch, :],
                    khS=kh_c[hp_r:hp_r + 64, hch, :],
                    rtS=rt_c[hp_r:hp_r + 64, hch, :],
                    vS=v_b[:, h * 64:(h + 1) * 64],
                    btlS=btl[:, h * 64:(h + 1) * 64],
                    ktlS=ktl[:, h * 64:(h + 1) * 64],
                    dLS=dLf[hp_r:hp_r + 64, hch:hch + 1],
                )

            for h in pairs:
                def f_ta(t, h=h, i=i, cx=ctx_h[h]):
                    if h < 2:
                        p.wait(t, "sv", SV_CT)
                    else:
                        p.wait(t, "sv", mile[("scanev", i, h - 2)])
                    BQ = cx["BQ"]
                    p.inc(t.matmul(BQ[0][:, 0:128], cx["bhS"], cx["atS"], start=True, stop=True), "tt")
                    p.inc(t.matmul(BQ[0][:, 128:256], cx["bhS"], cx["rtS"], start=True, stop=True), "tt")
                    p.inc(t.matmul(BQ[1][:, 0:128], cx["khS"], cx["atS"], start=True, stop=True), "tt")
                    p.inc(t.matmul(BQ[1][:, 128:256], cx["khS"], cx["rtS"], start=True, stop=True), "tt")
                    p.inc(t.matmul(BQ[2][:, 0:128], cx["atS"], cx["bhS"], start=True, stop=True), "tt")
                    mile[("ttA", i, h)] = p.val("tt")
                block.tensor(f_ta)
            for h in pairs:
                def f_bev(v, h=h, i=i, cx=ctx_h[h]):
                    p.wait(v, "tt", mile[("ttA", i, h)])
                    w, BQ = cx["w"], cx["BQ"]
                    v.select(w["AT"], MSU, BQ[0][:, 0:128], zer_f[:])
                    v.select(w["F1T"], MSU, BQ[0][:, 0:128], identf[:])
                    v.select(w["GbT"], MIU, BQ[0][:, 128:256], zer_f[:])
                    v.select(w["CT"], MSU, BQ[1][:, 0:128], zer_f[:])
                    v.select(w["GkT"], MIU, BQ[1][:, 128:256], zer_f[:])
                    i1 = v.select(w["A"], MSL, BQ[2][:, 0:128], zer_f[:])
                    mile[("bev", i, h)] = p.inc(i1, "sv")
                block.vector(f_bev)
            for lvl in range(6):
                for h in pairs:
                    def f_sqt(t, h=h, i=i, lvl=lvl, cx=ctx_h[h]):
                        w, BQ = cx["w"], cx["BQ"]
                        if lvl == 0:
                            p.wait(t, "sv", mile[("bev", i, h)])
                        else:
                            p.wait(t, "sv", mile[("sqs", i, h, lvl - 1)])
                        if lvl >= 2:
                            p.wait(t, "sv", mile[("sqF", i, h, lvl - 2)])
                        bq = lvl % 2
                        p.inc(t.matmul(BQ[bq][:, 256:384], w["AT"], w["A"], start=True, stop=True), "tt")
                        if lvl < 5:
                            p.inc(t.matmul(BQ[bq][:, 384:512], w["A"], w["AT"], start=True, stop=True), "tt")
                        mile[("ttsq", i, h, lvl)] = p.val("tt")
                    block.tensor(f_sqt)
                for h in pairs:
                    def f_sqf(v, h=h, i=i, lvl=lvl, cx=ctx_h[h]):
                        p.wait(v, "tt", mile[("ttsq", i, h, lvl)])
                        w, BQ = cx["w"], cx["BQ"]
                        fdst = ("F2", "F3", "F4", "F5", "F6", "F7")[lvl]
                        i1 = v.select(w[fdst], MSL, BQ[lvl % 2][:, 256:384], identf[:])
                        mile[("sqF", i, h, lvl)] = p.inc(i1, "sv")
                    block.vector(f_sqf)
                if lvl < 5:
                    for h in pairs:
                        def f_sqs(v, h=h, i=i, lvl=lvl, cx=ctx_h[h]):
                            p.wait(v, "tt", mile[("ttsq", i, h, lvl)])
                            w, BQ = cx["w"], cx["BQ"]
                            v.tensor_copy(w["A"], BQ[lvl % 2][:, 256:384])
                            i1 = v.tensor_copy(w["AT"], BQ[lvl % 2][:, 384:512])
                            mile[("sqs", i, h, lvl)] = p.inc(i1, "sv")
                        block.vector(f_sqs)
            for j in range(6):
                for h in pairs:
                    def f_tq(t, h=h, i=i, j=j, cx=ctx_h[h]):
                        w, BQ = cx["w"], cx["BQ"]
                        if j == 0:
                            p.wait(t, "sv", mile[("sqF", i, h, 0)])
                            m = t.matmul(BQ[2][:, 128:256], w["F2"], w["F1T"], start=True, stop=True)
                        else:
                            p.wait(t, "sv", mile[("sqF", i, h, j)])
                            p.wait(t, "sv", mile[("q", i, h, j - 1)])
                            fn = ("F3", "F4", "F5", "F6", "F7")[j - 1]
                            m = t.matmul(BQ[2][:, 128:256], w[fn], w["Q%d" % ((j - 1) % 2)],
                                         start=True, stop=True)
                        p.inc(m, "tt")
                        mile[("ttq", i, h, j)] = p.val("tt")
                    block.tensor(f_tq)
                for h in pairs:
                    def f_sq(v, h=h, i=i, j=j, cx=ctx_h[h]):
                        p.wait(v, "tt", mile[("ttq", i, h, j)])
                        w = cx["w"]
                        dst = "TT" if j == 5 else ("Q%d" % (j % 2))
                        i1 = v.tensor_copy(w[dst], cx["BQ"][2][:, 128:256])
                        mile[("q", i, h, j)] = p.inc(i1, "sv")
                    block.vector(f_sq)
            for h in pairs:
                def f_tcv(t, h=h, i=i, cx=ctx_h[h]):
                    p.wait(t, "sv", mile[("q", i, h, 5)])
                    p.inc(t.matmul(cx["BQ"][2][:, 0:64], cx["w"]["CT"], cx["vS"],
                                   start=True, stop=True), "tt")
                    mile[("ttcv", i, h)] = p.val("tt")
                block.tensor(f_tcv)
            for h in pairs:
                def f_scv(v, h=h, i=i, cx=ctx_h[h]):
                    p.wait(v, "tt", mile[("ttcv", i, h)])
                    i1 = v.tensor_copy(cx["w"]["CV"], cx["BQ"][2][:, 0:64])
                    mile[("cv", i, h)] = p.inc(i1, "sv")
                block.vector(f_scv)
            if chunk > 0:
                for h in pairs:
                    def f_tm0(t, h=h, i=i, cx=ctx_h[h]):
                        p.wait(t, "sv", mile[("cv", i, h)])
                        p.inc(t.matmul(cx["BQ"][3][:, 0:64], cx["atS"], cx["stbS"],
                                       start=True, stop=True), "tt")
                        mile[("ttm0", i, h)] = p.val("tt")
                    block.tensor(f_tm0)
                for h in pairs:
                    def f_sm0(v, h=h, i=i, cx=ctx_h[h]):
                        p.wait(v, "tt", mile[("ttm0", i, h)])
                        i1 = v.tensor_copy(cx["w"]["M0"], cx["BQ"][3][:, 0:64])
                        mile[("m0", i, h)] = p.inc(i1, "sv")
                    block.vector(f_sm0)
            for h in pairs:
                def f_tu(t, h=h, i=i, chunk=chunk, cx=ctx_h[h]):
                    w, BQ = cx["w"], cx["BQ"]
                    if chunk > 0:
                        p.wait(t, "sv", mile[("m0", i, h)])
                        m = t.matmul(BQ[3][:, 64:128], w["TT"], w["CV"], start=True, stop=False)
                        m = t.matmul(BQ[3][:, 64:128], w["TT"], w["M0"], start=False, stop=True)
                    else:
                        p.wait(t, "sv", mile[("cv", i, h)])
                        m = t.matmul(BQ[3][:, 64:128], w["TT"], w["CV"], start=True, stop=True)
                    p.inc(m, "tt")
                    mile[("ttu", i, h)] = p.val("tt")
                block.tensor(f_tu)
            for h in pairs:
                def f_su(v, h=h, i=i, cx=ctx_h[h]):
                    p.wait(v, "tt", mile[("ttu", i, h)])
                    i1 = v.tensor_copy(cx["w"]["U"], cx["BQ"][3][:, 64:128])
                    mile[("u", i, h)] = p.inc(i1, "sv")
                block.vector(f_su)
            for h in pairs:
                def f_to(t, h=h, i=i, chunk=chunk, cx=ctx_h[h]):
                    w, BQ = cx["w"], cx["BQ"]
                    p.wait(t, "sv", mile[("u", i, h)])
                    if chunk > 0:
                        m = t.matmul(BQ[3][:, 128:192], cx["rtS"], cx["stbS"], start=True, stop=False)
                        m = t.matmul(BQ[3][:, 128:192], w["GbT"], w["U"], start=False, stop=False)
                    else:
                        m = t.matmul(BQ[3][:, 128:192], w["GbT"], w["U"], start=True, stop=False)
                    m = t.matmul(BQ[3][:, 128:192], w["GkT"], cx["vS"], start=False, stop=True)
                    p.inc(m, "tt")
                    m = t.matmul(BQ[3][0:64, 192:256], cx["btlS"], w["U"], start=True, stop=False)
                    m = t.matmul(BQ[3][0:64, 192:256], cx["ktlS"], cx["vS"], start=False, stop=True)
                    p.inc(m, "tt")
                    mile[("ttst", i, h)] = p.val("tt")
                block.tensor(f_to)
            for h in pairs:
                def f_fin(v, h=h, i=i, chunk=chunk, cx=ctx_h[h]):
                    p.wait(v, "tt", mile[("ttst", i, h)])
                    BQ = cx["BQ"]
                    v.tensor_copy(o_s[:, h * 64:(h + 1) * 64], BQ[3][:, 128:192])
                    if chunk > 0:
                        v.scalar_tensor_tensor(cx["stfS"], cx["stfS"], cx["dLS"],
                                               BQ[3][0:64, 192:256], ALU.mult, ALU.add)
                    else:
                        v.tensor_copy(cx["stfS"], BQ[3][0:64, 192:256])
                    i1 = v.tensor_copy(cx["stbS"], cx["stfS"])
                    mile[("scanev", i, h)] = p.inc(i1, "sv")
                block.vector(f_fin)
        SV_SCANEND = p.val("sv")

        if debug and "o" in debug:
            def s_dbgo(sy, i=i):
                p.wait(sy, "sv", SV_SCANEND)
                sy.dma_start(out=dbg_d["o"][i * L:(i + 1) * L, :], in_=o_s[:]).then_inc(p.c["dbgs"][0], 16)
                p.c["dbgs"][1] += 16
            block.sync(s_dbgo)
            DBGO = p.val("dbgs")

            def s_dbgow(v, DBGO=DBGO):
                v.wait_ge(p.c["dbgs"][0], DBGO)
            block.vector(s_dbgow)

        def s_epi_v(v):
            p.wait(v, "sv", SV_SCANEND)
            v.tensor_mul(t1_s[:], o_s[:], o_s[:])
            v.reduce_sum(ss16[:], t1_s[:].rearrange("q (h n) -> q h n", h=H), axis=AX.X)
            v.reduce_sum(rn16[:], o_s[:].rearrange("q (h n) -> q h n", h=H), axis=AX.X)
            v.drain()
            v.tensor_scalar_mul(rn16[:], rn16[:], 1.0 / N)
            v.tensor_scalar_mul(ss16[:], ss16[:], 1.0 / N)
            v.drain()
            v.tensor_mul(t2_s[:, 0:16], rn16[:], rn16[:])
            v.drain()
            v.tensor_sub(ss16[:], ss16[:], t2_s[:, 0:16])
            p.inc(v.drain(), "sv")
        block.vector(s_epi_v)

        def s_epi_s(sc):
            p.wait(sc, "sv")
            sc.activation(t2_s[:, 16:32], ss16[:], AF.Ln, bias=cpool[:, 3:4])
            sc.drain()
            sc.activation(ss16[:], t2_s[:, 16:32], AF.Exp, scale=-0.5)
            p.inc(sc.drain(), "ss")
        block.scalar(s_epi_s)

        def s_epi_v2(v):
            p.wait(v, "ss")
            bc = lambda a: a.rearrange("q (h o) -> q h o", o=1).broadcast_to((128, H, N))
            h3 = lambda a: a.rearrange("q (h n) -> q h n", h=H)
            v.tensor_tensor(h3(y_s[:]), h3(o_s[:]), bc(rn16[:]), ALU.subtract)
            v.tensor_tensor(h3(y_s[:]), h3(y_s[:]), bc(ss16[:]), ALU.mult)
            v.tensor_mul(y_s[:], y_s[:], gab[:])
            v.tensor_add(y_s[:], y_s[:], beb[:])
            v.tensor_mul(t1_s[:], r_s[:], kt_s[:])
            v.tensor_mul(t1_s[:], t1_s[:], rkb[:])
            v.reduce_sum(rn16[:], t1_s[:].rearrange("q (h n) -> q h n", h=H), axis=AX.X)
            v.drain()
            v.tensor_tensor(h3(t2_s[:]), h3(v_s[:]), bc(rn16[:]), ALU.mult)
            v.tensor_add(y_s[:], y_s[:], t2_s[:])
            i1 = v.tensor_mul(y_s[:], y_s[:], g_s[:])
            p.inc(i1, "sv")
        block.vector(s_epi_v2)
        SV_Y = p.val("sv")

        tt3 = p.val("tt")
        ss3 = p.val("ss")
        yev = [ss3 + k + 1 for k in range(8)]

        def s_wo_t(t, tt3=tt3, yev=yev):
            p.wait(t, "sv", SV_Y)
            for k in range(8):
                if k >= 2:
                    p.wait(t, "ss", yev[k - 2])
                i1 = t.transpose(pb[2 + (k % 2)][:, 128:256], y_s[:, k * 128:(k + 1) * 128], identf[:])
                p.inc(i1, "tt")
        block.tensor(s_wo_t)

        for k in range(8):
            def s_yev(sc, k=k, tt3=tt3):
                p.wait(sc, "tt", tt3 + k + 1)
                sc.copy(vyc[:, k * 128:(k + 1) * 128], pb[2 + (k % 2)][:, 128:256])
                sc.drain()
                assert p.inc(sc.nop(), "ss") == yev[k]
            block.scalar(s_yev)

        def s_wo_mm(t):
            p.wait(t, "ss", yev[7])
            m0 = m1 = None
            for k in range(8):
                m0 = t.matmul(pb[0][:], vyc[:, k * 128:(k + 1) * 128], Ws["Wo"][:, k, 0:512],
                              start=(k == 0), stop=(k == 7))
                m1 = t.matmul(pb[1][:], vyc[:, k * 128:(k + 1) * 128], Ws["Wo"][:, k, 512:1024],
                              start=(k == 0), stop=(k == 7))
            p.inc(m0, "tt")
            p.inc(m1, "tt")
        block.tensor(s_wo_mm)
        TT_WO = p.val("tt")

        def s_fin_s(sc):
            p.wait(sc, "tt", TT_WO)
            sc.copy(ao_s[:, 0:512], pb[0][:])
            sc.copy(ao_s[:, 512:1024], pb[1][:])
            sc.drain()
            sc.activation(sqs[:], ao_s[:], AF.Square, accum_out=st[:, 4:5])
            sc.activation(st[:, 5:6], st[:, 4:5], AF.Copy, scale=INV_C)
            p.inc(sc.drain(), "ss")
        block.scalar(s_fin_s)

        def s_fin_v(v):
            p.wait(v, "ss")
            v.reduce_sum(st[:, 0:1], ao_s[:], axis=AX.X)
            v.drain()
            v.tensor_scalar_mul(st[:, 2:3], st[:, 0:1], INV_C)
            v.drain()
            v.tensor_mul(st[:, 1:2], st[:, 2:3], st[:, 2:3])
            v.drain()
            v.tensor_sub(st[:, 3:4], st[:, 5:6], st[:, 1:2])
            p.inc(v.drain(), "sv")
        block.vector(s_fin_v)

        def s_fin_s2(sc):
            p.wait(sc, "sv")
            sc.activation(st[:, 7:8], st[:, 3:4], AF.Ln, bias=cpool[:, 0:1])
            sc.drain()
            sc.activation(st[:, 6:7], st[:, 7:8], AF.Exp, scale=-0.5)
            p.inc(sc.drain(), "ss")
        block.scalar(s_fin_s2)

        def s_fin_v2(v, par=par, i=i):
            p.wait(v, "ss")
            if i >= 2:
                v.wait_ge(p.c[don[par]][0], mile[("outdma", i - 2)])
            v.tensor_scalar(t1_s[:], ao_s[:], st[:, 2:3], st[:, 6:7],
                            ALU.subtract, ALU.mult)
            i1 = v.tensor_add(x1_s[:, par, :], xt[:, par, :], t1_s[:])
            mile[("xfree", i)] = p.inc(i1, "sv")
        block.vector(s_fin_v2)

        def s_fin_s3(sc, par=par):
            p.wait(sc, "sv")
            sc.activation(sqs[:], x1_s[:, par, :], AF.Square, accum_out=st[:, 4:5])
            sc.activation(st[:, 5:6], st[:, 4:5], AF.Copy, scale=INV_C)
            p.inc(sc.drain(), "ss")
        block.scalar(s_fin_s3)

        def s_fin_v3(v, par=par):
            p.wait(v, "ss")
            v.reduce_sum(st[:, 0:1], x1_s[:, par, :], axis=AX.X)
            v.drain()
            v.tensor_scalar_mul(st[:, 2:3], st[:, 0:1], INV_C)
            v.drain()
            v.tensor_mul(st[:, 1:2], st[:, 2:3], st[:, 2:3])
            v.drain()
            v.tensor_sub(st[:, 3:4], st[:, 5:6], st[:, 1:2])
            p.inc(v.drain(), "sv")
        block.vector(s_fin_v3)

        def s_fin_s4(sc):
            p.wait(sc, "sv")
            sc.activation(st[:, 7:8], st[:, 3:4], AF.Ln, bias=cpool[:, 0:1])
            sc.drain()
            sc.activation(st[:, 6:7], st[:, 7:8], AF.Exp, scale=-0.5)
            p.inc(sc.drain(), "ss")
        block.scalar(s_fin_s4)

        def s_fin_v4(v, par=par, i=i):
            p.wait(v, "ss")
            v.tensor_scalar(xm_s[:], x1_s[:, par, :], st[:, 2:3], st[:, 6:7],
                            ALU.subtract, ALU.mult)
            v.reduce_sum(n1acc[:, i:i + 1], xm_s[:], axis=AX.X, apply_absolute_value=True)
            p.inc(v.drain(), "sv")
        block.vector(s_fin_v4)

        def s_fin_s5(sc, i=i):
            p.wait(sc, "sv")
            sc.activation(sqs[:], xm_s[:], AF.Square, accum_out=st[:, 8:9])
            sc.drain()
            sc.activation(impacc[:, i:i + 1], st[:, 8:9], AF.Sqrt)
            mile[("outready", i)] = p.inc(sc.drain(), "ss")
        block.scalar(s_fin_s5)

        def s_out(s, i=i, par=par):
            p.wait(s, "ss", mile[("outready", i)])
            s.dma_start(out=x1_d[i * L:(i + 1) * L, :], in_=x1_s[:, par, :]).then_inc(p.c[don[par]][0], 16)
            p.c[don[par]][1] += 16
            mile[("outdma", i)] = p.c[don[par]][1]
        block.sync(s_out)

        def s_outg(g, i=i):
            p.wait(g, "ss", mile[("outready", i)])
            g.dma_start(out=xm_d[i * L:(i + 1) * L, :], in_=xm_s[:]).then_inc(p.c["dg"][0], 16)
            p.c["dg"][1] += 16
            mile[("xmdma", i)] = p.c["dg"][1]
        block.gpsimd(s_outg)

        if debug:
            dbg_src = {"r": r_s, "kt": kt_s, "v": v_s, "lwpos": lwp, "asig": asig,
                       "g": g_s, "y": y_s, "kkn": kkn, "bscan": bsc,
                       "ln0": ln0, "ao": ao_s, "xm": xm_s}

            def s_dbg(s, i=i):
                p.wait(s, "ss", mile[("outready", i)])
                for nm in [d for d in debug if d != "o"]:
                    s.dma_start(out=dbg_d[nm][i * L:(i + 1) * L, :], in_=dbg_src[nm][:]).then_inc(p.c["dbgs"][0], 16)
                    p.c["dbgs"][1] += 16
            block.sync(s_dbg)
            DBGV = p.val("dbgs")

            def s_dbgwait(v, DBGV=DBGV):
                v.wait_ge(p.c["dbgs"][0], DBGV)
            block.vector(s_dbgwait)

    def s_tail(s):
        p.wait(s, "ss", mile[("outready", TILES - 1)])
        s.dma_start(out=imp_d[:], in_=impacc[:]).then_inc(p.c["dm"][0], 16)
        s.dma_start(out=n1_d[:], in_=n1acc[:]).then_inc(p.c["dm"][0], 16)
        p.c["dm"][1] += 32
        s.wait_ge(p.c["do0"][0], p.c["do0"][1])
        s.wait_ge(p.c["dg"][0], p.c["dg"][1])
        s.wait_ge(p.c["do1"][0], p.c["do1"][1])
        s.wait_ge(p.c["dm"][0], p.c["dm"][1])
        if debug:
            s.wait_ge(p.c["dbgs"][0], p.c["dbgs"][1])
    block.sync(s_tail)

    blk.__exit__(None, None, None)
    for g in reversed(ctx):
        g.__exit__(None, None, None)
    for g in p.ctxs:
        g.__exit__(None, None, None)
    return nc


def l1_inputs(x_core, inp):
    A = lambda k: np.ascontiguousarray(np.asarray(inp["L0_attn_" + k], dtype=np.float32))
    vecs = np.stack([A(k).reshape(-1) for k in
                     ("x_r", "x_k", "x_v", "x_w", "x_a", "x_g", "w0", "a0",
                      "k_k", "k_a", "r_k", "ln_gamma", "ln_beta")], axis=0)
    t = np.arange(128)[:, None]
    s = np.arange(128)[None, :]
    msl = (s < t).astype(np.uint8)
    masks = np.concatenate([msl, msl.T, (s <= t).astype(np.uint8).T], axis=1)
    return {
        "x": np.ascontiguousarray(x_core, dtype=np.float32),
        "Wr": A("Wr"), "Wk": A("Wk"), "Wv": A("Wv"), "Wo": A("Wo"),
        "w1": A("w1"), "w2": A("w2"), "a1": A("a1"), "a2": A("a2"),
        "g1": A("g1"), "g2": A("g2"),
        "vecs": np.ascontiguousarray(vecs),
        "identf": np.eye(128, dtype=np.float32),
        "masks": np.ascontiguousarray(masks),
    }







SLOT_FAM = [(0, 0), (1, 0), (2, 0), (0, 1), (1, 1), (2, 1), (0, 2), (1, 2)]
FAMW = {0: ((C, 4096), (4096, C)), 1: ((C, 2048), (2048, 2048), (2048, C)),
        2: ((C, 3072), (3072, C))}


def build_l2(T, n_all, kcap=1408, nbisect=36):
    TOK = B_CORE * T
    TILES = TOK // 128
    NTA = n_all // 128
    GT = kcap // 128 + 1
    NG = GT - 1
    assert (GT * 128) % 512 == 0
    TCH = GT * 128 // 512
    pos = 0.9 * (n_all - 1)
    k_lo = int(pos) + 1
    frac = float(np.float32(pos - int(pos)))
    BIG = float(2 ** 26)
    BIGI = 2 ** 26
    HB = TILES // 2

    nc = bass.Bass("TRN2", target_bir_lowering=False, detect_race_conditions=False)
    x1_d = nc.dram_tensor("x1", [TOK, C], F32, kind="ExternalInput").ap()
    xm_d = nc.dram_tensor("xm16", [TOK, C], BF16, kind="ExternalInput").ap()
    impo_d = nc.dram_tensor("imp_own", [128, TILES], F32, kind="ExternalInput").ap()
    n1o_d = nc.dram_tensor("n1_own", [128, TILES], F32, kind="ExternalInput").ap()
    impa_d = nc.dram_tensor("imp_all", [128, NTA], F32, kind="ExternalInput").ap()
    noise_d = nc.dram_tensor("noise", [128, TILES * 8], F32, kind="ExternalInput").ap()
    eu_d = nc.dram_tensor("eu", [1, 8], F32, kind="ExternalInput").ap()
    gw1_d = nc.dram_tensor("gw1", [C, C], F32, kind="ExternalInput").ap()
    gw2_d = nc.dram_tensor("gw2", [C, 512], F32, kind="ExternalInput").ap()
    gw3_d = nc.dram_tensor("gw3", [512, 8], F32, kind="ExternalInput").ap()
    aw_d = nc.dram_tensor("aw", [C, 8], F32, kind="ExternalInput").ap()
    gvec_d = nc.dram_tensor("gvec", [4, C], F32, kind="ExternalInput").ap()
    gsml_d = nc.dram_tensor("gsml", [2, 512], F32, kind="ExternalInput").ap()
    suo_d = nc.dram_tensor("suo", [128, 128], F32, kind="ExternalInput").ap()
    identf_d = nc.dram_tensor("identf", [128, 128], F32, kind="ExternalInput").ap()
    ew_d = {}
    eb_d = {}
    for f in (0, 1, 2):
        nexp = 3 if f < 2 else 2
        for li, sh in enumerate(FAMW[f]):
            ew_d[(f, li)] = nc.dram_tensor(f"e{f}w{li + 1}", [nexp] + list(sh), F32,
                                           kind="ExternalInput").ap()
            eb_d[(f, li)] = nc.dram_tensor(f"e{f}b{li + 1}", [nexp, sh[1]], F32,
                                           kind="ExternalInput").ap()

    y_d = nc.dram_tensor("y", [TOK, C], F32, kind="ExternalOutput").ap()
    xs_dram = nc.dram_tensor("xsd", [TOK, C + 8], BF16, kind="Internal").ap()
    comb_dram = nc.dram_tensor("combd", [TOK, C], F32, kind="Internal").ap()

    ctx = []

    def sb(name, shape, dtype=F32):
        g = nc.sbuf_tensor(name, shape, dtype)
        t = g.__enter__()
        ctx.append(g)
        return t

    identf = sb("identf_s", [128, 128], F32)
    onesf = sb("onesf", [128, 128], F32)
    cpool = sb("cpool", [128, 8], F32)
    msku = sb("msku", [128, TILES], U8)
    probs = sb("probs", [128, TILES, 8], F32)
    prg = sb("prg", [128, GT, 8], F32)
    idxg = sb("idxg", [128, NG], I32)
    XgT = sb("XgT", [128, 8, GT * 128], BF16)
    comb = sb("comb", [128, GT, C], F32)
    Cbf = sb("Cbf", [8, C], BF16)
    prT = sb("prT", [8, 128], BF16)
    st = sb("st", [128, 16], F32)
    eurow = sb("eurow", [1, 16], F32)

    s2 = []

    def sb2(name, shape, dtype=F32):
        g = nc.sbuf_tensor(name, shape, dtype)
        t = g.__enter__()
        s2.append(g)
        return t
    zer_f = sb2("zer_f", [128, 128], F32)
    suo = sb2("suo_s", [128, 128], F32)
    impa = sb2("impa", [128, NTA], F32)
    tmpa = sb2("tmpa", [128, NTA], F32)
    mska = sb2("mska", [128, NTA], U8)
    bis = sb2("bis", [128, 24], F32)
    impt = sb2("impt", [128, TILES], F32)
    n1t = sb2("n1t", [128, TILES], F32)
    mskf = sb2("mskf", [128, TILES], F32)
    scl = sb2("scl", [128, TILES], F32)
    rnk = sb2("rnk", [128, TILES], F32)
    tt2b = sb2("tt2b", [128, TILES], F32)
    srbf = sb2("srbf", [128, kcap], F32)
    iotari = sb2("iotari", [128, kcap], I32)
    eqc = sb2("eqc", [128, 2, kcap], BF16)
    mov = sb2("mov", [128, 2 * TILES], BF16)
    movi = sb2("movi", [128, 2 * TILES], I32)
    sloti = sb2("sloti", [128, NG], I32)
    slotf = sb2("slotf", [128, NG], F32)
    idxacc = sb2("idxacc", [128, 2 * NG], F32)
    idxf = sb2("idxf", [128, NG], F32)
    af_acc = sb2("af_acc", [128, 16], F32)
    mskg = sb2("mskg", [128, NG], F32)
    bmf = sb2("bmf", [128, NG], F32)
    affd = sb2("affd", [128, 8], F32)
    afS = sb2("afS", [128, NG, 8], F32)
    nslot = sb2("nslot", [128, NG, 8], F32)
    lgs = sb2("lgs", [128, NG, 8], F32)
    lg2 = sb2("lg2", [128, NG, 8], F32)
    lga = sb2("lga", [128, TILES, 8], F32)
    lgb = sb2("lgb", [128, TILES, 8], F32)
    m1t = sb2("m1t", [128, TILES], F32)
    m2t = sb2("m2t", [128, TILES], F32)
    c1t = sb2("c1t", [128, TILES], F32)
    logc = sb2("logc", [128, 8], F32)
    affs = sb2("affs", [128, 16], F32)
    noi = sb2("noi", [128, TILES * 8], F32)

    s1 = []

    def sb1(name, shape, dtype=F32):
        g = nc.sbuf_tensor(name, shape, dtype)
        t = g.__enter__()
        s1.append(g)
        return t
    gb1r = sb1("gb1r", [1, C], F32)
    lnsr = sb1("lnsr", [1, C], F32)
    lnor = sb1("lnor", [1, C], F32)
    gb2r = sb1("gb2r", [1, 512], F32)
    gsmr = sb1("gsmr", [1, 16], F32)
    gw1s = sb1("gw1s", [128, 8, C], BF16)
    gw2s = sb1("gw2s", [128, 8, 512], BF16)
    gw3s = sb1("gw3s", [128, 4, 8], BF16)
    aws = sb1("aws", [128, 8, 8], BF16)
    gb1b = sb1("gb1b", [128, C], BF16)
    lnsb = sb1("lnsb", [128, C], BF16)
    lnob = sb1("lnob", [128, C], BF16)
    gb2b = sb1("gb2b", [128, 512], BF16)
    gb3b = sb1("gb3b", [128, 8], F32)
    abb = sb1("abb", [128, 8], F32)
    xgbf = sb1("xgbf", [128, C + 8], BF16)
    xtmp = sb1("xtmp", [128, C], F32)
    htmp = sb1("htmp", [128, 2, C], F32)
    h2tmp = sb1("h2tmp", [128, 512], F32)
    sqs = sb1("sqs", [128, C], F32)
    hnT = sb1("hnT", [128, 8, 128], BF16)
    h2T = sb1("h2T", [128, 4, 128], BF16)
    xmtb = sb1("xmtb", [128, 2, C], BF16)
    xstb = sb1("xstb", [128, 2, C + 8], BF16)
    svb = sb1("svb", [128, 2, 2], BF16)
    ameanb = sb1("ameanb", [128, 16], BF16)

    pb = []
    for i in range(8):
        g = nc.psum_tensor(f"pb{i}", [128, 512], F32)
        pb.append(g.__enter__())
        ctx.append(g)

    blk = nc.Block()
    block = blk.__enter__()
    p = P(nc, ["dh", "dg", "do", "sv", "ss", "tt", "gi", "gp"])
    mile = {}

    def i_sync(s):
        s.dma_start(out=identf[:], in_=identf_d[:]).then_inc(p.c["dh"][0], 16)
        s.dma_start(out=suo[:], in_=suo_d[:]).then_inc(p.c["dh"][0], 16)
        s.dma_start(out=impa[:], in_=impa_d[:]).then_inc(p.c["dh"][0], 16)
        s.dma_start(out=impt[:], in_=impo_d[:]).then_inc(p.c["dh"][0], 16)
        s.dma_start(out=n1t[:], in_=n1o_d[:]).then_inc(p.c["dh"][0], 16)
        s.dma_start(out=noi[:], in_=noise_d[:]).then_inc(p.c["dh"][0], 16)
        s.dma_start(out=eurow[0:1, 0:8], in_=eu_d[:]).then_inc(p.c["dh"][0], 16)
        s.dma_start(out=gb1r[:], in_=gvec_d[0:1, :]).then_inc(p.c["dh"][0], 16)
        s.dma_start(out=lnsr[:], in_=gvec_d[1:2, :]).then_inc(p.c["dh"][0], 16)
        s.dma_start(out=lnor[:], in_=gvec_d[2:3, :]).then_inc(p.c["dh"][0], 16)
        s.dma_start(out=gb2r[:], in_=gsml_d[0:1, :]).then_inc(p.c["dh"][0], 16)
        s.dma_start(out=gsmr[:], in_=gsml_d[1:2, 0:16]).then_inc(p.c["dh"][0], 16)
        p.c["dh"][1] += 12 * 16
    block.sync(i_sync)

    def i_gp(g):
        g.dma_start(out=gw1s[:], in_=gw1_d.rearrange("(k q) n -> q k n", q=128)).then_inc(p.c["dg"][0], 16)
        g.dma_start(out=gw2s[:], in_=gw2_d.rearrange("(k q) n -> q k n", q=128)).then_inc(p.c["dg"][0], 16)
        g.dma_start(out=gw3s[:], in_=gw3_d.rearrange("(k q) n -> q k n", q=128)).then_inc(p.c["dg"][0], 16)
        g.dma_start(out=aws[:], in_=aw_d.rearrange("(k q) n -> q k n", q=128)).then_inc(p.c["dg"][0], 16)
        p.c["dg"][1] += 4 * 16
    block.gpsimd(i_gp)
    DG_W = p.val("dg")

    def i_gp2(g):
        g.iota(iotari[:], pattern=[[1, kcap]], base=0, channel_multiplier=0)
        g.iota(movi[:, 0:TILES], pattern=[[0, TILES]], base=0, channel_multiplier=1)
        g.iota(movi[:, TILES:2 * TILES], pattern=[[128, TILES]], base=0, channel_multiplier=0)
        g.iota(sloti[:], pattern=[[128, NG]], base=0, channel_multiplier=1)
        p.inc(g.drain(), "gp")
    block.gpsimd(i_gp2)
    GI_IOTA = p.val("gp")

    def i_v2(v):
        p.wait(v, "gp", GI_IOTA)
        v.tensor_copy(srbf[:], iotari[:])
        v.drain()
        mt = mov[:].rearrange("q (c t) -> q c t", t=2)
        v.tensor_copy(mt[:, :, 0:1], movi[:, 0:TILES].rearrange("q (c o) -> q c o", o=1))
        v.drain()
        v.tensor_copy(mt[:, :, 1:2], movi[:, TILES:2 * TILES].rearrange("q (c o) -> q c o", o=1))
        v.drain()
        v.tensor_copy(slotf[:], sloti[:])
        p.inc(v.drain(), "sv")
    block.vector(i_v2)

    def i_vec(v):
        for col, val in ((0, 1e-5), (1, 1e-8), (2, 1.0), (3, BIG), (4, 1e-12)):
            v.memset(cpool[:, col:col + 1], val)
        v.memset(onesf[:], 1.0)
        v.memset(zer_f[:], 0.0)
        v.memset(XgT[:, :, 0:128], 0.0)
        v.memset(prg[:].rearrange("q a e -> q (a e)"), 0.0)
        v.memset(comb[:, 0, :], 0.0)
        v.memset(bis[:, 0:2], 0.0)
        v.memset(bis[:, 2:4], 1024.0)
        v.memset(idxg[:], BIGI)
        p.inc(v.drain(), "sv")
    block.vector(i_vec)
    SV0 = p.val("sv")

    bc_list = [(gb1r, gb1b, 1024), (lnsr, lnsb, 1024), (lnor, lnob, 1024),
               (gb2r, gb2b, 512)]
    for bi, (srcr, dst, width) in enumerate(bc_list):
        def tsec(t, srcr=srcr, width=width, bi=bi):
            if bi == 0:
                p.wait(t, "dh")
                p.wait(t, "sv", SV0)
            else:
                p.wait(t, "ss")
            m = t.matmul(pb[0][:], onesf[0:1, :], srcr[0:1, 0:512], start=True, stop=True)
            p.inc(m, "tt")
            if width > 512:
                m = t.matmul(pb[1][:], onesf[0:1, :], srcr[0:1, 512:1024], start=True, stop=True)
                p.inc(m, "tt")
        block.tensor(tsec)

        def ssec(sc, dst=dst, width=width):
            p.wait(sc, "tt")
            sc.copy(dst[:, 0:512], pb[0][:])
            if width > 512:
                sc.copy(dst[:, 512:1024], pb[1][:])
            p.inc(sc.drain(), "ss")
        block.scalar(ssec)

    def t_small(t):
        p.wait(t, "ss")
        m = t.matmul(pb[0][:, 0:8], onesf[0:1, :], gsmr[0:1, 0:8], start=True, stop=True)
        p.inc(m, "tt")
        m = t.matmul(pb[1][:, 0:8], onesf[0:1, :], gsmr[0:1, 8:16], start=True, stop=True)
        p.inc(m, "tt")
    block.tensor(t_small)

    def s_small(sc):
        p.wait(sc, "tt")
        sc.copy(gb3b[:], pb[0][:, 0:8])
        sc.copy(abb[:], pb[1][:, 0:8])
        p.inc(sc.drain(), "ss")
    block.scalar(s_small)
    SS_BC = p.val("ss")

    def zf_s(s):
        p.wait(s, "sv", SV0)
        for a in range(TILES):
            s.dma_start(out=comb_dram[a * 128:(a + 1) * 128, :], in_=comb[:, 0, :]).then_inc(p.c["dh"][0], 16)
            p.c["dh"][1] += 16
    block.sync(zf_s)
    DH_CZ = p.val("dh")

    def s_ns(sc):
        p.wait(sc, "dh")
        sc.activation(eurow[0:1, 8:16], eurow[0:1, 0:8], AF.Ln, bias=cpool[0:1, 1:2])
        p.inc(sc.drain(), "ss")
    block.scalar(s_ns)

    def v_ns(v):
        p.wait(v, "ss")
        v.tensor_mul(eurow[0:1, 8:16], eurow[0:1, 8:16], eurow[0:1, 0:8])
        v.drain()
        v.reduce_sum(eurow[0:1, 0:1], eurow[0:1, 8:16], axis=AX.X, negate=True)
        v.drain()
        v.tensor_scalar(eurow[0:1, 1:2], eurow[0:1, 0:1], -0.01, 0.02, ALU.mult, ALU.add)
        v.drain()
        v.tensor_scalar_max(eurow[0:1, 1:2], eurow[0:1, 1:2], 1e-4)
        p.inc(v.drain(), "sv")
    block.vector(v_ns)
    SV_NS = p.val("sv")

    def t_ns(t):
        p.wait(t, "sv", SV_NS)
        m = t.matmul(pb[2][:, 0:1], onesf[0:1, :], eurow[0:1, 1:2], start=True, stop=True)
        p.inc(m, "tt")
    block.tensor(t_ns)
    TT_NS = p.val("tt")

    def v_ns2(v):
        p.wait(v, "tt", TT_NS)
        v.tensor_copy(bis[:, 12:13], pb[2][:, 0:1])
        p.inc(v.drain(), "sv")
    block.vector(v_ns2)

    ranks = (float(k_lo), float(k_lo + 1))
    for it in range(nbisect):
        def b_v(v, it=it):
            v.tensor_add(bis[:, 4:6], bis[:, 0:2], bis[:, 2:4])
            v.drain()
            v.tensor_scalar_mul(bis[:, 4:6], bis[:, 4:6], 0.5)
            v.drain()
            for kk in range(2):
                v.tensor_scalar(tmpa[:], impa[:], bis[:, 4 + kk:5 + kk], None, ALU.is_le)
                v.drain()
                v.reduce_sum(st[:, kk:kk + 1], tmpa[:], axis=AX.X)
                v.drain()
            p.inc(v.nop(), "sv")
        block.vector(b_v)
        SVB = p.val("sv")

        def b_t(t, it=it):
            p.wait(t, "sv", SVB)
            m = t.matmul(pb[4][:, 0:2], onesf[:], st[:, 0:2], start=True, stop=True)
            p.inc(m, "tt")
        block.tensor(b_t)
        TTB = p.val("tt")

        def b_u(v, it=it):
            p.wait(v, "tt", TTB)
            v.tensor_copy(bis[:, 6:8], pb[4][:, 0:2])
            v.drain()
            v.tensor_scalar(bis[:, 8:9], bis[:, 6:7], ranks[0], None, ALU.is_ge)
            v.tensor_scalar(bis[:, 9:10], bis[:, 7:8], ranks[1], None, ALU.is_ge)
            v.drain()
            v.tensor_sub(bis[:, 16:18], bis[:, 4:6], bis[:, 2:4])
            v.drain()
            v.tensor_mul(bis[:, 16:18], bis[:, 16:18], bis[:, 8:10])
            v.drain()
            v.tensor_add(bis[:, 2:4], bis[:, 2:4], bis[:, 16:18])
            v.tensor_sub(bis[:, 18:20], bis[:, 4:6], bis[:, 0:2])
            v.tensor_scalar(bis[:, 8:10], bis[:, 8:10], -1.0, 1.0, ALU.mult, ALU.add)
            v.drain()
            v.tensor_mul(bis[:, 18:20], bis[:, 18:20], bis[:, 8:10])
            v.drain()
            v.tensor_add(bis[:, 0:2], bis[:, 0:2], bis[:, 18:20])
            p.inc(v.drain(), "sv")
        block.vector(b_u)

    for kk in range(2):
        def vm_v(v, kk=kk):
            v.tensor_scalar(mska[:], impa[:], bis[:, kk:kk + 1], None, ALU.is_gt)
            v.drain()
            v.select(tmpa[:], mska[:], impa[:],
                     cpool[:, 3:4].broadcast_to((128, NTA)))
            v.drain()
            v.tensor_reduce(st[:, 4 + kk:5 + kk], tmpa[:], axis=AX.X, op=ALU.min)
            p.inc(v.drain(), "sv")
        block.vector(vm_v)
        SVV = p.val("sv")

        def vm_t(t, kk=kk):
            p.wait(t, "sv", SVV)
            i1 = t.transpose(pb[5][0:1, 0:128], st[:, 4 + kk:5 + kk], identf[:])
            p.inc(i1, "tt")
        block.tensor(vm_t)
        TTV = p.val("tt")

        def vm_v2(v, kk=kk):
            p.wait(v, "tt", TTV)
            v.tensor_reduce(st[0:1, 6 + kk:7 + kk], pb[5][0:1, 0:128], axis=AX.X, op=ALU.min)
            p.inc(v.drain(), "sv")
        block.vector(vm_v2)
        SVV2 = p.val("sv")

        def vm_t2(t, kk=kk):
            p.wait(t, "sv", SVV2)
            m = t.matmul(pb[5][:, 4 + kk:5 + kk], onesf[0:1, :], st[0:1, 6 + kk:7 + kk],
                         start=True, stop=True)
            p.inc(m, "tt")
        block.tensor(vm_t2)
        TTV2 = p.val("tt")

        def vm_v3(v, kk=kk):
            p.wait(v, "tt", TTV2)
            v.tensor_copy(bis[:, 14 + kk:15 + kk], pb[5][:, 4 + kk:5 + kk])
            p.inc(v.drain(), "sv")
        block.vector(vm_v3)

    def std_v(v):
        v.reduce_sum(st[:, 8:9], impa[:], axis=AX.X)
        p.inc(v.drain(), "sv")
    block.vector(std_v)
    SVS = p.val("sv")

    def std_t(t):
        p.wait(t, "sv", SVS)
        m = t.matmul(pb[5][:, 8:9], onesf[:], st[:, 8:9], start=True, stop=True)
        p.inc(m, "tt")
    block.tensor(std_t)
    TTS = p.val("tt")

    def std_v2(v):
        p.wait(v, "tt", TTS)
        v.tensor_scalar_mul(bis[:, 13:14], pb[5][:, 8:9], 1.0 / n_all)
        p.inc(v.drain(), "sv")
    block.vector(std_v2)

    def std_s(sc):
        p.wait(sc, "sv")
        sc.activation(tmpa[:], impa[:], AF.Square, bias=bis[:, 13:14], scale=-1.0,
                      accum_out=st[:, 9:10])
        sc.activation(st[:, 10:11], st[:, 9:10], AF.Copy, scale=1.0 / n_all)
        p.inc(sc.drain(), "ss")
    block.scalar(std_s)
    SSS = p.val("ss")

    def std_t2(t):
        p.wait(t, "ss", SSS)
        m = t.matmul(pb[5][:, 10:11], onesf[:], st[:, 10:11], start=True, stop=True)
        p.inc(m, "tt")
    block.tensor(std_t2)
    TTS2 = p.val("tt")

    def std_s2(sc):
        p.wait(sc, "tt", TTS2)
        sc.activation(bis[:, 11:12], pb[5][:, 10:11], AF.Sqrt)
        p.inc(sc.drain(), "ss")
    block.scalar(std_s2)

    def thr_v(v):
        p.wait(v, "ss")
        v.tensor_scalar(bis[:, 11:12], bis[:, 11:12], 0.1, 1.0, ALU.mult, ALU.add)
        v.tensor_sub(bis[:, 20:21], bis[:, 15:16], bis[:, 14:15])
        v.drain()
        v.tensor_scalar_mul(bis[:, 20:21], bis[:, 20:21], frac)
        v.drain()
        v.tensor_add(bis[:, 10:11], bis[:, 14:15], bis[:, 20:21])
        v.drain()
        v.tensor_scalar(msku[:], impt[:], bis[:, 10:11], None, ALU.is_gt)
        v.drain()
        v.tensor_copy(mskf[:], msku[:])
        v.drain()
        v.tensor_scalar_add(scl[:], n1t[:], 1e-8)
        v.drain()
        v.tensor_scalar_mul(scl[:], scl[:], bis[:, 11:12])
        v.drain()
        v.reciprocal(tt2b[:], scl[:])
        v.drain()
        v.tensor_mul(scl[:], tt2b[:], mskf[:])
        v.reduce_sum(st[:, 12:13], mskf[:], axis=AX.X)
        p.inc(v.drain(), "sv")
    block.vector(thr_v)
    SVT = p.val("sv")

    def pf_t(t):
        p.wait(t, "sv", SVT)
        m = t.matmul(pb[6][:, 0:1], suo[:], st[:, 12:13], start=True, stop=True)
        p.inc(m, "tt")
        m = t.matmul(pb[6][:, 4:5], onesf[:], st[:, 12:13], start=True, stop=True)
        p.inc(m, "tt")
    block.tensor(pf_t)
    TTP = p.val("tt")

    def pf_v(v):
        p.wait(v, "tt", TTP)
        v.tensor_copy(st[:, 13:14], pb[6][:, 0:1])
        v.tensor_copy(st[:, 14:15], pb[6][:, 4:5])
        v.drain()
        v.tensor_tensor_scan(rnk[:], mskf[:], zer_f[:, 0:TILES], 0.0, ALU.add, ALU.bypass)
        v.drain()
        v.tensor_sub(rnk[:], rnk[:], mskf[:])
        v.drain()
        v.tensor_scalar_add(rnk[:], rnk[:], st[:, 13:14])
        v.drain()
        v.tensor_scalar(tt2b[:], mskf[:], -BIG, BIG, ALU.mult, ALU.add)
        v.drain()
        v.tensor_add(tt2b[:], tt2b[:], rnk[:])
        p.inc(v.drain(), "sv")
    block.vector(pf_v)
    SVR = p.val("sv")

    for c in range(TILES):
        def eq_v(v, c=c):
            if c >= 2:
                p.wait(v, "tt", mile[("eqt", c - 2)])
            v.tensor_scalar(eqc[:, c % 2, :], srbf[:], tt2b[:, c:c + 1], None, ALU.is_equal)
            p.inc(v.drain(), "sv")
        block.vector(eq_v)
        SVE = p.val("sv")

        def eq_t(t, c=c, SVE=SVE):
            p.wait(t, "sv", SVE)
            if c >= 2:
                p.wait(t, "sv", mile[("eqa", c - 2)])
            m = None
            for gg in range(NG):
                m = t.matmul(pb[5 + c % 2][:, 2 * gg:2 * gg + 2], eqc[:, c % 2, gg * 128:(gg + 1) * 128],
                             mov[:, 2 * c:2 * c + 2], start=True, stop=True)
                p.inc(m, "tt")
            mile[("eqt", c)] = p.val("tt")
        block.tensor(eq_t)
        TT_EQ1 = p.val("tt")

        def eq_a(v, c=c, TT_EQ1=TT_EQ1):
            p.wait(v, "tt", TT_EQ1)
            if c == 0:
                v.tensor_copy(idxacc[:], pb[5][:, 0:2 * NG])
            else:
                v.tensor_add(idxacc[:], idxacc[:], pb[5 + c % 2][:, 0:2 * NG])
            v.drain()
            mile[("eqa", c)] = p.inc(v.nop(), "sv")
        block.vector(eq_a)
    TT_EQ = p.val("tt")

    def ix_v(v):
        pbv = idxacc[:].rearrange("q (g t) -> q t g", t=2)
        v.tensor_tensor(idxf[:], pbv[:, 0, :], pbv[:, 1, :], ALU.add)
        v.drain()
        v.tensor_scalar(mskg[:], slotf[:], st[:, 14:15], None, ALU.is_ge)
        v.drain()
        v.scalar_tensor_tensor(idxf[:], mskg[:], BIG, idxf[:], ALU.mult, ALU.add)
        v.drain()
        v.tensor_copy(idxg[:], idxf[:])
        v.drain()
        v.tensor_scalar(bmf[:], idxf[:], float(TOK // 2), None, ALU.is_ge)
        p.inc(v.drain(), "sv")
    block.vector(ix_v)
    SV_IDX = p.val("sv")

    for i in range(TILES):
        par = i % 2

        def af_s(s, i=i, par=par):
            if i >= 2:
                p.wait(s, "tt", mile[("afmm", i - 2)])
            s.dma_start(out=xmtb[:, par, :], in_=xm_d[i * 128:(i + 1) * 128, :]).then_inc(p.c["dh"][0], 16)
            p.c["dh"][1] += 16
        block.sync(af_s)
        DHA = p.val("dh")

        def af_v(v, i=i, par=par, DHA=DHA):
            p.wait(v, "dh", DHA)
            ba = 0 if i < HB else 1
            v.memset(svb[:, par, :], 0.0)
            v.drain()
            v.memset(svb[:, par, ba:ba + 1], 1.0)
            v.drain()
            v.tensor_scalar_mul(xstb[:, par, 0:C], xmtb[:, par, :], scl[:, i:i + 1])
            v.drain()
            v.tensor_copy(xstb[:, par, C:C + 8], noi[:, 8 * i:8 * i + 8])
            p.inc(v.drain(), "sv")
        block.vector(af_v)
        SVA = p.val("sv")

        def af_t(t, i=i, par=par, SVA=SVA):
            p.wait(t, "sv", SVA)
            if i >= 2:
                p.wait(t, "sv", mile[("afacc", i - 2)])
            m = None
            for c in range(8):
                m = t.matmul(pb[7][:, 16 * par + 2 * c:16 * par + 2 * c + 2],
                             xstb[:, par, c * 128:(c + 1) * 128],
                             svb[:, par, :], start=True, stop=True)
                p.inc(m, "tt")
            mile[("afmm", i)] = p.val("tt")
        block.tensor(af_t)
        TTAF1 = p.val("tt")

        def af_a(v, i=i, par=par, TTAF1=TTAF1):
            p.wait(v, "tt", TTAF1)
            if i == 0:
                v.tensor_copy(af_acc[:], pb[7][:, 0:16])
            else:
                v.tensor_add(af_acc[:], af_acc[:], pb[7][:, 16 * par:16 * par + 16])
            v.drain()
            mile[("afacc", i)] = p.inc(v.nop(), "sv")
        block.vector(af_a)

        def af_sw(s, i=i, par=par, SVA=SVA):
            p.wait(s, "sv", SVA)
            s.dma_start(out=xs_dram[i * 128:(i + 1) * 128, :], in_=xstb[:, par, :]).then_inc(p.c["dh"][0], 16)
            p.c["dh"][1] += 16
        block.sync(af_sw)
    TT_AF = p.val("tt")
    DH_XS = p.val("dh")

    def af_e(v):
        v.tensor_scalar_mul(ameanb[:], af_acc[:], 1.0 / T)
        p.inc(v.drain(), "sv")
    block.vector(af_e)
    SV_AM = p.val("sv")

    def af_t2(t):
        p.wait(t, "sv", SV_AM)
        m0 = m1 = None
        for c in range(8):
            m0 = t.matmul(pb[6][0:1, 8:16], ameanb[:, 2 * c:2 * c + 1], aws[:, c, :],
                          start=(c == 0), stop=(c == 7))
            m1 = t.matmul(pb[5][0:1, 0:8], ameanb[:, 2 * c + 1:2 * c + 2], aws[:, c, :],
                          start=(c == 0), stop=(c == 7))
        p.inc(m0, "tt")
        p.inc(m1, "tt")
    block.tensor(af_t2)
    TT_AF2 = p.val("tt")

    def af_e2(v):
        p.wait(v, "tt", TT_AF2)
        v.tensor_copy(st[0:1, 0:8], pb[6][0:1, 8:16])
        v.tensor_copy(st[0:1, 8:16], pb[5][0:1, 0:8])
        p.inc(v.drain(), "sv")
    block.vector(af_e2)
    SV_AF2 = p.val("sv")

    def af_t3(t):
        p.wait(t, "sv", SV_AF2)
        m = t.matmul(pb[6][:, 0:16], onesf[0:1, :], st[0:1, 0:16], start=True, stop=True)
        p.inc(m, "tt")
    block.tensor(af_t3)
    TT_AF3 = p.val("tt")

    def af_e3(v):
        p.wait(v, "tt", TT_AF3)
        v.tensor_add(affs[:, 0:8], pb[6][:, 0:8], abb[:])
        v.tensor_add(affs[:, 8:16], pb[6][:, 8:16], abb[:])
        v.drain()
        v.tensor_scalar_mul(affs[:], affs[:], 0.3)
        p.inc(v.drain(), "sv")
    block.vector(af_e3)

    for gti in range(NG):
        def xg_g(g, gti=gti, DH_XS=DH_XS):
            if gti == 0:
                p.wait(g, "sv", SV_IDX)
                p.wait(g, "dh", DH_XS)
            else:
                p.wait(g, "sv", mile[("xgv", gti - 1)])
            g.memset(xgbf[:], 0.0)
            g.indirect_dma_start(
                out=xgbf[:], out_offset=None,
                in_=xs_dram[:], in_offset=bass.IndirectOffsetOnAxis(ap=idxg[:, gti:gti + 1], axis=0),
                bounds_check=TOK - 1, oob_is_err=False,
            ).then_inc(p.c["gi"][0], 16)
            p.c["gi"][1] += 16
            g.wait_ge(p.c["gi"][0], p.c["gi"][1])
        block.gpsimd(xg_g)
        GI_XG = p.val("gi")

        def xg_v(v, gti=gti):
            p.wait(v, "gi", GI_XG)
            v.tensor_copy(xtmp[:], xgbf[:, 0:C])
            v.drain()
            v.tensor_copy(nslot[:, gti, :], xgbf[:, C:C + 8])
            v.drain()
            v.scalar_tensor_tensor(comb[:, gti + 1, :], xtmp[:], 0.1, zer_f[:, 0:1].broadcast_to((128, C)), ALU.mult, ALU.add)
            mile[("xgv", gti)] = p.inc(v.drain(), "sv")
        block.vector(xg_v)
        SVXG = p.val("sv")
        svend = p.val("sv") + 8

        def xg_t(t, gti=gti, SVXG=SVXG, svend=svend):
            p.wait(t, "sv", SVXG)
            for c in range(8):
                if c >= 2:
                    p.wait(t, "sv", svend - 8 + c - 1)
                i1 = t.transpose(pb[2 + (c % 2)][:, 0:128], xtmp[:, c * 128:(c + 1) * 128], identf[:])
                p.inc(i1, "tt")
        block.tensor(xg_t)
        TTXG = p.val("tt")

        for c in range(8):
            def xg_e(v, c=c, gti=gti, TTXG=TTXG):
                p.wait(v, "tt", TTXG - 8 + c + 1)
                i1 = v.tensor_copy(XgT[:, c, (gti + 1) * 128:(gti + 2) * 128], pb[2 + (c % 2)][:, 0:128])
                v.drain()
                p.inc(v.nop(), "sv")
            block.vector(xg_e)
        assert p.val("sv") == svend
    SV_XGT = p.val("sv")

    for gti in range(GT):
        def gp_t1(t, gti=gti):
            if gti == 0:
                p.wait(t, "sv")
                p.wait(t, "dg", DG_W)
                p.wait(t, "ss", SS_BC)
            else:
                p.wait(t, "sv", mile[("gpv1", gti - 1)])
            m0 = m1 = None
            for c in range(8):
                m0 = t.matmul(pb[0][:], XgT[:, c, gti * 128:(gti + 1) * 128], gw1s[:, c, 0:512],
                              start=(c == 0), stop=(c == 7))
                m1 = t.matmul(pb[1][:], XgT[:, c, gti * 128:(gti + 1) * 128], gw1s[:, c, 512:1024],
                              start=(c == 0), stop=(c == 7))
            p.inc(m0, "tt")
            p.inc(m1, "tt")
        block.tensor(gp_t1)
        TTG1 = p.val("tt")

        def gp_v1(v, gti=gti):
            p.wait(v, "tt", TTG1)
            v.tensor_add(htmp[:, gti % 2, 0:512], pb[0][:], gb1b[:, 0:512])
            i1 = v.tensor_add(htmp[:, gti % 2, 512:1024], pb[1][:], gb1b[:, 512:1024])
            mile[("gpv1", gti)] = p.inc(i1, "sv")
        block.vector(gp_v1)
        SVG1 = p.val("sv")

        def gp_s1(sc, gti=gti):
            p.wait(sc, "sv", SVG1)
            sc.activation(xtmp[:], htmp[:, gti % 2, :], AF.Silu)
            sc.drain()
            sc.activation(sqs[:], xtmp[:], AF.Square, accum_out=st[:, 4:5])
            sc.activation(st[:, 5:6], st[:, 4:5], AF.Copy, scale=INV_C)
            p.inc(sc.drain(), "ss")
        block.scalar(gp_s1)

        def gp_v2(v):
            p.wait(v, "ss")
            v.reduce_sum(st[:, 0:1], xtmp[:], axis=AX.X)
            v.drain()
            v.tensor_scalar_mul(st[:, 2:3], st[:, 0:1], INV_C)
            v.drain()
            v.tensor_mul(st[:, 1:2], st[:, 2:3], st[:, 2:3])
            v.drain()
            v.tensor_sub(st[:, 3:4], st[:, 5:6], st[:, 1:2])
            p.inc(v.drain(), "sv")
        block.vector(gp_v2)

        def gp_s2(sc):
            p.wait(sc, "sv")
            sc.activation(st[:, 7:8], st[:, 3:4], AF.Ln, bias=cpool[:, 0:1])
            sc.drain()
            sc.activation(st[:, 6:7], st[:, 7:8], AF.Exp, scale=-0.5)
            p.inc(sc.drain(), "ss")
        block.scalar(gp_s2)

        def gp_v3(v, gti=gti):
            p.wait(v, "ss")
            hB = htmp[:, gti % 2, :]
            v.tensor_scalar(hB, xtmp[:], st[:, 2:3], st[:, 6:7], ALU.subtract, ALU.mult)
            v.tensor_mul(hB, hB, lnsb[:])
            i1 = v.tensor_add(hB, hB, lnob[:])
            p.inc(i1, "sv")
        block.vector(gp_v3)
        SVG3 = p.val("sv")
        svend = p.val("sv") + 8

        def gp_t2(t, gti=gti, SVG3=SVG3, svend=svend):
            p.wait(t, "sv", SVG3)
            for c in range(8):
                if c >= 2:
                    p.wait(t, "sv", svend - 8 + c - 1)
                i1 = t.transpose(pb[2 + (c % 2)][:, 0:128], htmp[:, gti % 2, c * 128:(c + 1) * 128], identf[:])
                p.inc(i1, "tt")
        block.tensor(gp_t2)
        TTG2 = p.val("tt")

        for c in range(8):
            def gp_e(v, c=c, TTG2=TTG2):
                p.wait(v, "tt", TTG2 - 8 + c + 1)
                v.tensor_copy(hnT[:, c, :], pb[2 + (c % 2)][:, 0:128])
                v.drain()
                p.inc(v.nop(), "sv")
            block.vector(gp_e)
        assert p.val("sv") == svend

        def gp_t3(t):
            p.wait(t, "sv", svend)
            m = None
            for c in range(8):
                m = t.matmul(pb[4][:], hnT[:, c, :], gw2s[:, c, :], start=(c == 0), stop=(c == 7))
            p.inc(m, "tt")
        block.tensor(gp_t3)
        TTG3 = p.val("tt")

        def gp_v4(v, TTG3=TTG3):
            p.wait(v, "tt", TTG3)
            v.tensor_add(h2tmp[:], pb[4][:], gb2b[:])
            p.inc(v.drain(), "sv")
        block.vector(gp_v4)
        SVG4 = p.val("sv")

        def gp_s3(sc, SVG4=SVG4):
            p.wait(sc, "sv", SVG4)
            sc.activation(h2tmp[:], h2tmp[:], AF.Silu)
            p.inc(sc.drain(), "ss")
        block.scalar(gp_s3)
        SSG3 = p.val("ss")
        svend2 = p.val("sv") + 4

        def gp_t4(t, SSG3=SSG3, svend2=svend2):
            p.wait(t, "ss", SSG3)
            for c in range(4):
                if c >= 2:
                    p.wait(t, "sv", svend2 - 4 + c - 1)
                i1 = t.transpose(pb[2 + (c % 2)][:, 0:128], h2tmp[:, c * 128:(c + 1) * 128], identf[:])
                p.inc(i1, "tt")
        block.tensor(gp_t4)
        TTG4 = p.val("tt")

        for c in range(4):
            def gp_e2(v, c=c, TTG4=TTG4):
                p.wait(v, "tt", TTG4 - 4 + c + 1)
                v.tensor_copy(h2T[:, c, :], pb[2 + (c % 2)][:, 0:128])
                v.drain()
                p.inc(v.nop(), "sv")
            block.vector(gp_e2)
        assert p.val("sv") == svend2

        def gp_t5(t, svend2=svend2):
            p.wait(t, "sv", svend2)
            m = None
            for c in range(4):
                m = t.matmul(pb[5][:, 0:8], h2T[:, c, :], gw3s[:, c, :], start=(c == 0), stop=(c == 3))
            p.inc(m, "tt")
        block.tensor(gp_t5)
        TTG5 = p.val("tt")

        def gp_v5(v, gti=gti, TTG5=TTG5):
            p.wait(v, "tt", TTG5)
            if gti == 0:
                v.tensor_add(logc[:], pb[5][:, 0:8], gb3b[:])
            else:
                v.tensor_add(lgs[:, gti - 1, :], pb[5][:, 0:8], gb3b[:])
            v.drain()
            mile[("gpfree", gti)] = p.inc(v.nop(), "sv")
        block.vector(gp_v5)

    def sp_v(v):
        v.tensor_sub(affd[:], affs[:, 8:16], affs[:, 0:8])
        v.drain()
        for gg in range(NG):
            v.tensor_scalar_mul(afS[:, gg, :], affd[:], bmf[:, gg:gg + 1])
            v.drain()
            v.tensor_add(afS[:, gg, :], afS[:, gg, :], affs[:, 0:8])
            v.drain()
        v.tensor_tensor(lgs[:], lgs[:], afS[:], ALU.add)
        v.drain()
        v.scalar_tensor_tensor(lgs[:].rearrange("q g e -> q (g e)"),
                               nslot[:].rearrange("q g e -> q (g e)"), bis[:, 12:13],
                               lgs[:].rearrange("q g e -> q (g e)"), ALU.mult, ALU.add)
        v.drain()
        v.tensor_scalar_mul(lgs[:].rearrange("q g e -> q (g e)"),
                            lgs[:].rearrange("q g e -> q (g e)"), 1.0 / 1.5)
        v.drain()
        v.reduce_max(m1t[:, 0:NG], lgs[:], axis=AX.X)
        v.drain()
        m1b = m1t[:, 0:NG].rearrange("q (g o) -> q g o", o=1).broadcast_to((128, NG, 8))
        v.tensor_tensor(lg2[:], lgs[:], m1b, ALU.is_equal)
        v.drain()
        v.reduce_sum(c1t[:, 0:NG], lg2[:], axis=AX.X)
        v.drain()
        v.scalar_tensor_tensor(lg2[:].rearrange("q g e -> q (g e)"),
                               lg2[:].rearrange("q g e -> q (g e)"), -BIG,
                               lgs[:].rearrange("q g e -> q (g e)"), ALU.mult, ALU.add)
        v.drain()
        v.reduce_max(m2t[:, 0:NG], lg2[:], axis=AX.X)
        v.drain()
        v.tensor_scalar(c1t[:, 0:NG], c1t[:, 0:NG], 2.0, None, ALU.is_ge)
        v.drain()
        v.tensor_sub(tt2b[:, 0:NG], m1t[:, 0:NG], m2t[:, 0:NG])
        v.drain()
        v.tensor_mul(tt2b[:, 0:NG], tt2b[:, 0:NG], c1t[:, 0:NG])
        v.drain()
        v.tensor_add(m2t[:, 0:NG], m2t[:, 0:NG], tt2b[:, 0:NG])
        v.drain()
        v.tensor_tensor(lg2[:], lgs[:], m1b, ALU.subtract)
        p.inc(v.drain(), "sv")
    block.vector(sp_v)
    SVS1 = p.val("sv")

    def sp_s(sc):
        p.wait(sc, "sv", SVS1)
        sc.activation(lg2[:].rearrange("q g e -> q (g e)"),
                      lg2[:].rearrange("q g e -> q (g e)"), AF.Exp)
        p.inc(sc.drain(), "ss")
    block.scalar(sp_s)
    SSS1 = p.val("ss")

    def sp_v2(v):
        p.wait(v, "ss", SSS1)
        m2b = m2t[:, 0:NG].rearrange("q (g o) -> q g o", o=1).broadcast_to((128, NG, 8))
        v.tensor_tensor(lgs[:], lgs[:], m2b, ALU.is_ge)
        v.drain()
        v.tensor_mul(lg2[:].rearrange("q g e -> q (g e)"), lg2[:].rearrange("q g e -> q (g e)"),
                     lgs[:].rearrange("q g e -> q (g e)"))
        v.drain()
        v.reduce_sum(m1t[:, 0:NG], lg2[:], axis=AX.X)
        v.drain()
        v.reciprocal(c1t[:, 0:NG], m1t[:, 0:NG])
        v.drain()
        rb = c1t[:, 0:NG].rearrange("q (g o) -> q g o", o=1).broadcast_to((128, NG, 8))
        v.tensor_tensor(prg[:, 1:GT, :], lg2[:], rb, ALU.mult)
        p.inc(v.drain(), "sv")
    block.vector(sp_v2)

    def dg_v(v):
        cb = logc[:].rearrange("q (o e) -> q o e", o=1).broadcast_to((128, TILES, 8))
        v.tensor_copy(lga[:], cb)
        v.drain()
        ab_ = affs[:, 0:8].rearrange("q (o e) -> q o e", o=1).broadcast_to((128, HB, 8))
        bb_ = affs[:, 8:16].rearrange("q (o e) -> q o e", o=1).broadcast_to((128, TILES - HB, 8))
        v.tensor_tensor(lga[:, 0:HB, :], lga[:, 0:HB, :], ab_, ALU.add)
        v.tensor_tensor(lga[:, HB:TILES, :], lga[:, HB:TILES, :], bb_, ALU.add)
        v.drain()
        v.scalar_tensor_tensor(lga[:].rearrange("q a e -> q (a e)"), noi[:], bis[:, 12:13],
                               lga[:].rearrange("q a e -> q (a e)"), ALU.mult, ALU.add)
        v.drain()
        v.tensor_scalar_mul(lga[:].rearrange("q a e -> q (a e)"),
                            lga[:].rearrange("q a e -> q (a e)"), 1.0 / 1.5)
        v.drain()
        v.reduce_max(m1t[:], lga[:], axis=AX.X)
        v.drain()
        m1b = m1t[:].rearrange("q (a o) -> q a o", o=1).broadcast_to((128, TILES, 8))
        v.tensor_tensor(lgb[:], lga[:], m1b, ALU.is_equal)
        v.drain()
        v.reduce_sum(c1t[:], lgb[:], axis=AX.X)
        v.drain()
        v.scalar_tensor_tensor(lgb[:].rearrange("q a e -> q (a e)"),
                               lgb[:].rearrange("q a e -> q (a e)"), -BIG,
                               lga[:].rearrange("q a e -> q (a e)"), ALU.mult, ALU.add)
        v.drain()
        v.reduce_max(m2t[:], lgb[:], axis=AX.X)
        v.drain()
        v.tensor_scalar(c1t[:], c1t[:], 2.0, None, ALU.is_ge)
        v.drain()
        v.tensor_sub(tt2b[:], m1t[:], m2t[:])
        v.drain()
        v.tensor_mul(tt2b[:], tt2b[:], c1t[:])
        v.drain()
        v.tensor_add(m2t[:], m2t[:], tt2b[:])
        v.drain()
        v.tensor_tensor(lgb[:], lga[:], m1b, ALU.subtract)
        p.inc(v.drain(), "sv")
    block.vector(dg_v)
    SVD1 = p.val("sv")

    def dg_sc(sc):
        p.wait(sc, "sv", SVD1)
        sc.activation(lgb[:].rearrange("q a e -> q (a e)"),
                      lgb[:].rearrange("q a e -> q (a e)"), AF.Exp)
        p.inc(sc.drain(), "ss")
    block.scalar(dg_sc)
    SSD = p.val("ss")

    def dg_v2(v):
        p.wait(v, "ss", SSD)
        m2b = m2t[:].rearrange("q (a o) -> q a o", o=1).broadcast_to((128, TILES, 8))
        v.tensor_tensor(lga[:], lga[:], m2b, ALU.is_ge)
        v.drain()
        v.tensor_mul(lgb[:].rearrange("q a e -> q (a e)"), lgb[:].rearrange("q a e -> q (a e)"),
                     lga[:].rearrange("q a e -> q (a e)"))
        v.drain()
        v.reduce_sum(m1t[:], lgb[:], axis=AX.X)
        v.drain()
        v.reciprocal(c1t[:], m1t[:])
        v.drain()
        rb = c1t[:].rearrange("q (a o) -> q a o", o=1).broadcast_to((128, TILES, 8))
        v.tensor_tensor(probs[:], lgb[:], rb, ALU.mult)
        p.inc(v.drain(), "sv")
    block.vector(dg_v2)
    GI_PRG = p.val("gi")

    return _build_l2_experts(nc, blk, block, p, mile, ctx, s1, locals())


def _build_l2_experts(nc, blk, block, p, mile, ctx, s1, env):
    g_ = env
    TOK, TILES, GT, NG, TCH = g_["TOK"], g_["TILES"], g_["GT"], g_["NG"], g_["TCH"]
    pb, XgT, comb, Cbf, prg, prT = (g_[k] for k in
        ("pb", "XgT", "comb", "Cbf", "prg", "prT"))
    ew_d, eb_d = g_["ew_d"], g_["eb_d"]
    identf, onesf, msku, probs, st, cpool = (g_[k] for k in
        ("identf", "onesf", "msku", "probs", "st", "cpool"))
    x1_d, y_d, comb_dram, idxg = (g_[k] for k in
        ("x1_d", "y_d", "comb_dram", "idxg"))
    sb = g_["sb"]
    s2 = g_["s2"]
    GI_PRG = g_["GI_PRG"]

    for gg in reversed(s1):
        gg.__exit__(None, None, None)
    for gg in reversed(s2):
        gg.__exit__(None, None, None)

    s3 = []

    def sb3(name, shape, dtype=F32):
        g = nc.sbuf_tensor(name, shape, dtype)
        t = g.__enter__()
        s3.append(g)
        return t
    wq1 = sb3("wq1", [128, 8, 512], BF16)
    wq2 = sb3("wq2", [128, 16, 512], BF16)
    wq3f = sb3("wq3f", [128, 4096], BF16)
    bq1 = sb3("bq1", [128, 16], F32)
    bq2 = sb3("bq2", [128, 4], F32)
    bqs = sb3("bqs", [4, 128], F32)
    h1q = sb3("h1q", [128, 16, GT * 128], BF16)
    h2q = sb3("h2q", [128, 4, GT * 128], BF16)
    zf = sb3("zf", [128, 512], F32)
    z2f = sb3("z2f", [128, 512], F32)
    sgf = sb3("sgf", [128, 512], F32)
    sg2 = sb3("sg2", [128, 512], F32)
    Cacc = sb3("Cacc", [128, C], F32)
    bxb = sb3("bxb", [128, C], F32)
    ebrow = sb3("ebrow", [1, C], F32)
    crow = sb3("crow", [1, C], BF16)
    wq3_8 = wq3f[:].rearrange("q (k n) -> q k n", k=8)
    wq3_4 = wq3f[:].rearrange("q (k n) -> q k n", k=4)

    BAR_SV = p.val("sv")
    BAR_SS = p.val("ss")
    BAR_TT = p.val("tt")

    def bar_g(g):
        p.wait(g, "sv", BAR_SV)
        p.wait(g, "ss", BAR_SS)
        p.wait(g, "tt", BAR_TT)
        p.wait(g, "gi", GI_PRG)
    block.gpsimd(bar_g)
    GI_BAR = p.val("gi")

    def mm1_stage(t_, nck, ncols_h, wsrc, hdst, psum_tiles, svbase):
        pass

    for e in range(8):
        fam, idx = SLOT_FAM[e]
        if fam == 0:
            parts, hw, act = 4, 1024, "gelu"
        elif fam == 2:
            parts, hw, act = 3, 1024, "ssig"
        else:
            parts, hw, act = 2, 1024, None

        if fam in (0, 2):
            for q in range(parts):
                for dh_ in range(2):
                    def wl1_g(g, fam=fam, idx=idx, q=q, e=e, dh_=dh_):
                        if e == 0 and q == 0 and dh_ == 0:
                            p.wait(g, "gi", GI_BAR)
                        p.wait(g, "sv")
                        p.wait(g, "ss")
                        off = q * 1024 + dh_ * 512
                        g.dma_start(out=wq1[:], in_=ew_d[(fam, 0)][idx, :, off:off + 512]
                                    .rearrange("(k q2) n -> q2 k n", q2=128)).then_inc(p.c["gi"][0], 16)
                        g.dma_start(out=bqs[:], in_=eb_d[(fam, 0)][idx, off:off + 512]
                                    .rearrange("(h q2) -> h q2", h=4)).then_inc(p.c["gi"][0], 16)
                        p.c["gi"][1] += 32
                    block.gpsimd(wl1_g)
                    GIW = p.val("gi")

                    def bqt_t(t, GIW=GIW):
                        p.wait(t, "gi", GIW)
                        i1 = t.transpose(pb[3][:, 0:4], bqs[0:4, 0:128], identf[0:4, 0:4])
                        p.inc(i1, "tt")
                    block.tensor(bqt_t)
                    TTBQ = p.val("tt")

                    def bqt_v(v, dh_=dh_, TTBQ=TTBQ):
                        p.wait(v, "tt", TTBQ)
                        v.tensor_copy(bq1[:, dh_ * 4:(dh_ + 1) * 4], pb[3][:, 0:4])
                        v.drain()
                        p.inc(v.nop(), "sv")
                    block.vector(bqt_v)
                    SVBQ = p.val("sv")

                    for ht in range(4):
                        hb = dh_ * 4 + ht
                        for tc in range(TCH):
                            it_n = ht * TCH + tc
                            bank = pb[it_n % 2]
                            ssh = p.val("ss")
                            svh2 = p.val("sv")

                            def m1_t(t, it_n=it_n, ht=ht, tc=tc, bank=bank, GIW=GIW, ssh=ssh):
                                if it_n == 0:
                                    p.wait(t, "gi", GIW)
                                if it_n >= 2:
                                    p.wait(t, "ss", ssh - 1)
                                m = None
                                for c in range(8):
                                    m = t.matmul(bank[:], wq1[:, c, ht * 128:(ht + 1) * 128],
                                                 XgT[:, c, tc * 512:(tc + 1) * 512],
                                                 start=(c == 0), stop=(c == 7))
                                p.inc(m, "tt")
                            block.tensor(m1_t)
                            TTM = p.val("tt")

                            def m1_s(sc, it_n=it_n, hb=hb, tc=tc, bank=bank, TTM=TTM,
                                     act=act, SVBQ=SVBQ, svh2=svh2):
                                p.wait(sc, "tt", TTM)
                                if it_n == 0:
                                    p.wait(sc, "sv", SVBQ)
                                if act == "gelu":
                                    i1 = sc.activation(h1q[:, hb, tc * 512:(tc + 1) * 512], bank[:],
                                                       AF.Gelu_apprx_tanh, bias=bq1[:, hb:hb + 1])
                                else:
                                    if it_n >= 2:
                                        p.wait(sc, "sv", svh2 - 1)
                                    za = (z2f, zf)[it_n % 2]
                                    sa = (sgf, sg2)[it_n % 2]
                                    sc.activation(za[:], bank[:], AF.Silu, bias=bq1[:, hb:hb + 1])
                                    i1 = sc.activation(sa[:], bank[:], AF.Sigmoid, bias=bq1[:, hb:hb + 1])
                                p.inc(i1, "ss")
                            block.scalar(m1_s)
                            SSZ = p.val("ss")

                            if act != "gelu":
                                def m1_v2(v, it_n=it_n, hb=hb, tc=tc, SSZ=SSZ):
                                    p.wait(v, "ss", SSZ)
                                    za = (z2f, zf)[it_n % 2]
                                    sa = (sgf, sg2)[it_n % 2]
                                    i1 = v.tensor_mul(h1q[:, hb, tc * 512:(tc + 1) * 512], za[:], sa[:])
                                    p.inc(i1, "sv")
                                block.vector(m1_v2)
                SV_H1 = p.val("sv")
                SS_H1 = p.val("ss")

                for dc in range(2):
                    def wl3_g(g, fam=fam, idx=idx, q=q, dc=dc):
                        p.wait(g, "sv")
                        p.wait(g, "ss")
                        g.dma_start(out=wq3_8, in_=ew_d[(fam, 1)][idx, q * 1024:(q + 1) * 1024,
                                    dc * 512:(dc + 1) * 512]
                                    .rearrange("(k q2) n -> q2 k n", q2=128)).then_inc(p.c["gi"][0], 16)
                        p.c["gi"][1] += 16
                    block.gpsimd(wl3_g)
                    GIW3 = p.val("gi")

                    for tt_ in range(GT):
                        bank = pb[4 + tt_ % 2]
                        svh = p.val("sv")

                        def m2_t(t, tt_=tt_, dc=dc, bank=bank, svh=svh, GIW3=GIW3,
                                 SV_H1=SV_H1, SS_H1=SS_H1):
                            if tt_ == 0:
                                p.wait(t, "gi", GIW3)
                                if dc == 0:
                                    p.wait(t, "sv", SV_H1)
                                    p.wait(t, "ss", SS_H1)
                            if tt_ >= 2:
                                p.wait(t, "sv", svh - 1)
                            m = None
                            for hc in range(8):
                                m = t.matmul(bank[:], h1q[:, hc, tt_ * 128:(tt_ + 1) * 128],
                                             wq3_8[:, hc, :],
                                             start=(hc == 0), stop=(hc == 7))
                            p.inc(m, "tt")
                        block.tensor(m2_t)
                        TTM2 = p.val("tt")

                        def m2_v(v, tt_=tt_, dc=dc, q=q, e=e, bank=bank, TTM2=TTM2):
                            p.wait(v, "tt", TTM2)
                            if tt_ == 0:
                                if q == 0:
                                    v.tensor_copy(Cacc[:, dc * 512:(dc + 1) * 512], bank[:])
                                else:
                                    v.tensor_add(Cacc[:, dc * 512:(dc + 1) * 512],
                                                 Cacc[:, dc * 512:(dc + 1) * 512], bank[:])
                            else:
                                v.scalar_tensor_tensor(comb[:, tt_, dc * 512:(dc + 1) * 512],
                                                       bank[:], prg[:, tt_, e:e + 1],
                                                       comb[:, tt_, dc * 512:(dc + 1) * 512],
                                                       ALU.mult, ALU.add)
                            v.drain()
                            p.inc(v.nop(), "sv")
                        block.vector(m2_v)
        else:
            for ch in range(4):
                def wl1_g(g, idx=idx, ch=ch, e=e):
                    if e == 0 and ch == 0:
                        p.wait(g, "gi", GI_BAR)
                    p.wait(g, "sv")
                    off = ch * 512
                    g.dma_start(out=wq1[:], in_=ew_d[(1, 0)][idx, :, off:off + 512]
                                .rearrange("(k q2) n -> q2 k n", q2=128)).then_inc(p.c["gi"][0], 16)
                    g.dma_start(out=bqs[:], in_=eb_d[(1, 0)][idx, off:off + 512]
                                .rearrange("(h q2) -> h q2", h=4)).then_inc(p.c["gi"][0], 16)
                    p.c["gi"][1] += 32
                block.gpsimd(wl1_g)
                GIW = p.val("gi")

                def bqt_t(t, GIW=GIW):
                    p.wait(t, "gi", GIW)
                    i1 = t.transpose(pb[3][:, 0:4], bqs[0:4, 0:128], identf[0:4, 0:4])
                    p.inc(i1, "tt")
                block.tensor(bqt_t)
                TTBQ = p.val("tt")

                def bqt_v(v, ch=ch, TTBQ=TTBQ):
                    p.wait(v, "tt", TTBQ)
                    v.tensor_copy(bq1[:, ch * 4:(ch + 1) * 4], pb[3][:, 0:4])
                    v.drain()
                    p.inc(v.nop(), "sv")
                block.vector(bqt_v)
                SVBQ = p.val("sv")

                for ht in range(4):
                    hb = ch * 4 + ht
                    for tc in range(TCH):
                        bank = pb[(ht * TCH + tc) % 2]
                        svh = p.val("sv")

                        def m1_t(t, ht=ht, tc=tc, bank=bank, GIW=GIW, svh=svh):
                            if ht == 0 and tc == 0:
                                p.wait(t, "gi", GIW)
                            if ht * TCH + tc >= 2:
                                p.wait(t, "sv", svh - 1)
                            m = None
                            for c in range(8):
                                m = t.matmul(bank[:], wq1[:, c, ht * 128:(ht + 1) * 128],
                                             XgT[:, c, tc * 512:(tc + 1) * 512],
                                             start=(c == 0), stop=(c == 7))
                            p.inc(m, "tt")
                        block.tensor(m1_t)
                        TTM = p.val("tt")

                        def m1_s(sc, ht=ht, tc=tc, hb=hb, bank=bank, TTM=TTM, SVBQ=SVBQ):
                            p.wait(sc, "tt", TTM)
                            if ht == 0 and tc == 0:
                                p.wait(sc, "sv", SVBQ)
                            sc.activation(h1q[:, hb, tc * 512:(tc + 1) * 512],
                                          bank[:], AF.Silu, bias=bq1[:, hb:hb + 1])
                            sc.drain()
                            p.inc(sc.nop(), "sv")
                        block.scalar(m1_s)
            SV_H1 = p.val("sv")
            for wq in range(4):
                def wl2_g(g, idx=idx, wq=wq, SV_H1=SV_H1):
                    p.wait(g, "sv")
                    g.dma_start(out=wq2[:], in_=ew_d[(1, 1)][idx, :, wq * 512:(wq + 1) * 512]
                                .rearrange("(k q2) n -> q2 k n", q2=128)).then_inc(p.c["gi"][0], 16)
                    g.dma_start(out=bqs[:], in_=eb_d[(1, 1)][idx, wq * 512:(wq + 1) * 512]
                                .rearrange("(h q2) -> h q2", h=4)).then_inc(p.c["gi"][0], 16)
                    g.dma_start(out=wq3_4, in_=ew_d[(1, 2)][idx, wq * 512:(wq + 1) * 512, :]
                                .rearrange("(k q2) n -> q2 k n", q2=128)).then_inc(p.c["gi"][0], 16)
                    p.c["gi"][1] += 48
                block.gpsimd(wl2_g)
                GIW2 = p.val("gi")

                def bqt2_t(t, GIW2=GIW2):
                    p.wait(t, "gi", GIW2)
                    i1 = t.transpose(pb[3][:, 0:4], bqs[0:4, 0:128], identf[0:4, 0:4])
                    p.inc(i1, "tt")
                block.tensor(bqt2_t)
                TTBQ2 = p.val("tt")

                def bqt2_v(v, TTBQ2=TTBQ2):
                    p.wait(v, "tt", TTBQ2)
                    v.tensor_copy(bq2[:, 0:4], pb[3][:, 0:4])
                    v.drain()
                    p.inc(v.nop(), "sv")
                block.vector(bqt2_v)
                SVBQ2 = p.val("sv")

                for ht in range(4):
                    for tc in range(TCH):
                        bank = pb[(ht * TCH + tc) % 2]
                        svh = p.val("sv")

                        def m2e_t(t, ht=ht, tc=tc, bank=bank, GIW2=GIW2, svh=svh, SV_H1=SV_H1):
                            if ht == 0 and tc == 0:
                                p.wait(t, "gi", GIW2)
                                p.wait(t, "sv", SV_H1)
                            if ht * TCH + tc >= 2:
                                p.wait(t, "sv", svh - 1)
                            m = None
                            for hc in range(16):
                                m = t.matmul(bank[:], wq2[:, hc, ht * 128:(ht + 1) * 128],
                                             h1q[:, hc, tc * 512:(tc + 1) * 512],
                                             start=(hc == 0), stop=(hc == 15))
                            p.inc(m, "tt")
                        block.tensor(m2e_t)
                        TTM = p.val("tt")

                        def m2e_s(sc, ht=ht, tc=tc, bank=bank, TTM=TTM, SVBQ2=SVBQ2):
                            p.wait(sc, "tt", TTM)
                            if ht == 0 and tc == 0:
                                p.wait(sc, "sv", SVBQ2)
                            sc.activation(h2q[:, ht, tc * 512:(tc + 1) * 512],
                                          bank[:], AF.Silu, bias=bq2[:, ht:ht + 1])
                            sc.drain()
                            p.inc(sc.nop(), "sv")
                        block.scalar(m2e_s)
                SV_H2 = p.val("sv")
                for tt_ in range(GT):
                    for dc in range(2):
                        bank = pb[4 + (tt_ * 2 + dc) % 2]
                        svh = p.val("sv")

                        def m3_t(t, tt_=tt_, dc=dc, bank=bank, svh=svh, SV_H2=SV_H2):
                            if tt_ == 0 and dc == 0:
                                p.wait(t, "sv", SV_H2)
                            if tt_ * 2 + dc >= 2:
                                p.wait(t, "sv", svh - 1)
                            m = None
                            for hc in range(4):
                                m = t.matmul(bank[:], h2q[:, hc, tt_ * 128:(tt_ + 1) * 128],
                                             wq3_4[:, hc, dc * 512:(dc + 1) * 512],
                                             start=(hc == 0), stop=(hc == 3))
                            p.inc(m, "tt")
                        block.tensor(m3_t)
                        TTM3 = p.val("tt")

                        def m3_v(v, tt_=tt_, dc=dc, wq=wq, e=e, bank=bank, TTM3=TTM3):
                            p.wait(v, "tt", TTM3)
                            if tt_ == 0:
                                if wq == 0:
                                    v.tensor_copy(Cacc[:, dc * 512:(dc + 1) * 512], bank[:])
                                else:
                                    v.tensor_add(Cacc[:, dc * 512:(dc + 1) * 512],
                                                 Cacc[:, dc * 512:(dc + 1) * 512], bank[:])
                            else:
                                v.scalar_tensor_tensor(comb[:, tt_, dc * 512:(dc + 1) * 512],
                                                       bank[:], prg[:, tt_, e:e + 1],
                                                       comb[:, tt_, dc * 512:(dc + 1) * 512],
                                                       ALU.mult, ALU.add)
                            v.drain()
                            p.inc(v.nop(), "sv")
                        block.vector(m3_v)

        def eb_s(s, fam=fam, idx=idx):
            lastb = eb_d[(fam, len(FAMW[fam]) - 1)]
            p.wait(s, "sv")
            p.wait(s, "ss")
            s.dma_start(out=ebrow[0:1, 0:1024], in_=lastb[idx:idx + 1, :]).then_inc(p.c["dh"][0], 16)
            p.c["dh"][1] += 16
        block.sync(eb_s)
        DHB = p.val("dh")

        def eb_t(t, DHB=DHB):
            p.wait(t, "dh", DHB)
            m = t.matmul(pb[6][:], onesf[0:1, :], ebrow[0:1, 0:512], start=True, stop=True)
            p.inc(m, "tt")
            m = t.matmul(pb[7][:], onesf[0:1, :], ebrow[0:1, 512:1024], start=True, stop=True)
            p.inc(m, "tt")
        block.tensor(eb_t)
        TTB2 = p.val("tt")

        def eb_v(v, e=e, TTB2=TTB2):
            p.wait(v, "tt", TTB2)
            v.tensor_copy(bxb[:, 0:512], pb[6][:])
            v.tensor_copy(bxb[:, 512:1024], pb[7][:])
            v.drain()
            for tt_ in range(1, GT):
                v.scalar_tensor_tensor(comb[:, tt_, :], bxb[:], prg[:, tt_, e:e + 1],
                                       comb[:, tt_, :], ALU.mult, ALU.add)
            v.tensor_add(Cacc[:], Cacc[:], bxb[:])
            v.drain()
            v.tensor_copy(crow[:], Cacc[0:1, :])
            v.drain()
            p.inc(v.nop(), "sv")
        block.vector(eb_v)
        SVEB = p.val("sv")

        def eb_g(g, e=e, SVEB=SVEB):
            p.wait(g, "sv", SVEB)
            g.dma_start(out=Cbf[e:e + 1, :], in_=crow[:]).then_inc(p.c["gi"][0], 16)
            p.c["gi"][1] += 16
            g.wait_ge(p.c["gi"][0], p.c["gi"][1])
        block.gpsimd(eb_g)

    SV_EXP = p.val("sv")

    DH_CZ = g_["DH_CZ"]

    def sc_g(g):
        p.wait(g, "sv", SV_EXP)
        g.wait_ge(p.c["dh"][0], DH_CZ)
        for gti in range(NG):
            g.indirect_dma_start(
                out=comb_dram[:], out_offset=bass.IndirectOffsetOnAxis(ap=idxg[:, gti:gti + 1], axis=0),
                in_=comb[:, gti + 1, :], in_offset=None,
                bounds_check=TOK - 1, oob_is_err=False,
            ).then_inc(p.c["gi"][0], 16)
            p.c["gi"][1] += 16
        g.wait_ge(p.c["gi"][0], p.c["gi"][1])
    block.gpsimd(sc_g)
    GI_SC = p.val("gi")

    for gg in reversed(s3):
        gg.__exit__(None, None, None)
    fin = sb("fin", [128, 2, C], F32)
    x1t = sb("x1t", [128, 2, C], F32)
    cmt = sb("cmt", [128, 2, C], F32)
    ysb = sb("ysb", [128, 2, C], F32)
    sq2 = sb("sq2", [128, C], F32)
    prT2 = sb("prT2", [8, 2, 128], BF16)

    for i in range(TILES):
        par = i % 2

        def f_s(s, i=i, par=par):
            if i == 0:
                p.wait(s, "gi", GI_SC)
            if i >= 2:
                p.wait(s, "sv", mile[("finfree", i - 2)])
                s.wait_ge(p.c["do"][0], mile[("ydma", i - 2)])
            s.dma_start(out=x1t[:, par, :], in_=x1_d[i * 128:(i + 1) * 128, :]).then_inc(p.c["dh"][0], 16)
            s.dma_start(out=cmt[:, par, :], in_=comb_dram[i * 128:(i + 1) * 128, :]).then_inc(p.c["dh"][0], 16)
            p.c["dh"][1] += 32
        block.sync(f_s)
        DHF = p.val("dh")

        def f_t0(t, i=i):
            if i == 0:
                p.wait(t, "sv", SV_EXP)
            else:
                p.wait(t, "sv", mile[("prT", i - 1)])
            i1 = t.transpose(pb[2][0:8, 0:128], probs[:, i, :], identf[:])
            p.inc(i1, "tt")
        block.tensor(f_t0)
        TTF0 = p.val("tt")

        def f_v0(v, TTF0=TTF0, i=i, par=par):
            p.wait(v, "tt", TTF0)
            i1 = v.tensor_copy(prT2[:, par, :], pb[2][0:8, 0:128])
            mile[("prT", i)] = p.inc(i1, "sv")
        block.vector(f_v0)
        SVF0 = p.val("sv")

        def f_t1(t, i=i, par=par, SVF0=SVF0):
            p.wait(t, "sv", SVF0)
            if i == 0:
                p.wait(t, "gi", GI_SC)
            if i > 0:
                p.wait(t, "sv", mile[("basefree", i - 1)])
            m = t.matmul(pb[0][:], prT2[:, par, :], Cbf[:, 0:512], start=True, stop=True)
            p.inc(m, "tt")
            m = t.matmul(pb[1][:], prT2[:, par, :], Cbf[:, 512:1024], start=True, stop=True)
            p.inc(m, "tt")
        block.tensor(f_t1)
        TTF1 = p.val("tt")

        def f_v1(v, i=i, par=par, TTF1=TTF1, DHF=DHF):
            p.wait(v, "tt", TTF1)
            p.wait(v, "dh", DHF)
            mb = msku[:, i:i + 1].broadcast_to((128, 512))
            v.select(fin[:, par, 0:512], mb, cmt[:, par, 0:512], pb[0][:])
            i1 = v.select(fin[:, par, 512:1024], mb, cmt[:, par, 512:1024], pb[1][:])
            mile[("basefree", i)] = p.inc(i1, "sv")
        block.vector(f_v1)

        def f_s1(sc, i=i, par=par):
            p.wait(sc, "sv", mile[("basefree", i)])
            sb_ = 8 * par
            sc.activation(sq2[:], fin[:, par, :], AF.Square, accum_out=st[:, sb_ + 4:sb_ + 5])
            sc.activation(st[:, sb_ + 5:sb_ + 6], st[:, sb_ + 4:sb_ + 5], AF.Copy, scale=INV_C)
            mile[("p9s1", i)] = p.inc(sc.drain(), "ss")
        block.scalar(f_s1)

        def f_v2(v, i=i, par=par):
            p.wait(v, "ss", mile[("p9s1", i)])
            sb_ = 8 * par
            v.reduce_sum(st[:, sb_ + 0:sb_ + 1], fin[:, par, :], axis=AX.X)
            v.drain()
            v.tensor_scalar_mul(st[:, sb_ + 2:sb_ + 3], st[:, sb_ + 0:sb_ + 1], INV_C)
            v.drain()
            v.tensor_mul(st[:, sb_ + 1:sb_ + 2], st[:, sb_ + 2:sb_ + 3], st[:, sb_ + 2:sb_ + 3])
            v.drain()
            v.tensor_sub(st[:, sb_ + 3:sb_ + 4], st[:, sb_ + 5:sb_ + 6], st[:, sb_ + 1:sb_ + 2])
            mile[("p9v2", i)] = p.inc(v.drain(), "sv")
        block.vector(f_v2)

        def f_s2(sc, i=i, par=par):
            p.wait(sc, "sv", mile[("p9v2", i)])
            sb_ = 8 * par
            sc.activation(st[:, sb_ + 7:sb_ + 8], st[:, sb_ + 3:sb_ + 4], AF.Ln, bias=cpool[:, 0:1])
            sc.drain()
            sc.activation(st[:, sb_ + 6:sb_ + 7], st[:, sb_ + 7:sb_ + 8], AF.Exp, scale=-0.5)
            mile[("p9s2", i)] = p.inc(sc.drain(), "ss")
        block.scalar(f_s2)

        def f_v3(v, i=i, par=par):
            p.wait(v, "ss", mile[("p9s2", i)])
            sb_ = 8 * par
            v.tensor_scalar(fin[:, par, :], fin[:, par, :], st[:, sb_ + 2:sb_ + 3],
                            st[:, sb_ + 6:sb_ + 7], ALU.subtract, ALU.mult)
            i1 = v.tensor_add(ysb[:, par, :], x1t[:, par, :], fin[:, par, :])
            mile[("finfree", i)] = p.inc(i1, "sv")
        block.vector(f_v3)
        SVF3 = p.val("sv")

        def f_o(s, i=i, par=par, SVF3=SVF3):
            p.wait(s, "sv", SVF3)
            s.dma_start(out=y_d[i * 128:(i + 1) * 128, :], in_=ysb[:, par, :]).then_inc(p.c["do"][0], 16)
            p.c["do"][1] += 16
            mile[("ydma", i)] = p.c["do"][1]
        block.sync(f_o)

    def tail(s):
        s.wait_ge(p.c["do"][0], p.c["do"][1])
    block.sync(tail)

    blk.__exit__(None, None, None)
    for gg in reversed(ctx):
        gg.__exit__(None, None, None)
    for gg in p.ctxs:
        gg.__exit__(None, None, None)
    return nc




def l2_inputs(inp, x1, xm16, imp_own, n1_own, imp_all, noise_perm):
    A = lambda k: np.ascontiguousarray(np.asarray(inp["L0_ffn_" + k], dtype=np.float32))
    gvec = np.zeros((4, C), np.float32)
    gvec[0] = A("gate_b1").reshape(-1)
    gvec[1] = A("ln_scale").reshape(-1)
    gvec[2] = A("ln_offset").reshape(-1)
    gsml = np.zeros((2, 512), np.float32)
    gsml[0] = A("gate_b2").reshape(-1)
    gsml[1, 0:8] = A("gate_b3").reshape(-1)
    gsml[1, 8:16] = A("aff_b").reshape(-1)
    pp = np.arange(128)[:, None]
    ii = np.arange(128)[None, :]
    d = {
        "x1": np.ascontiguousarray(x1), "xm16": np.ascontiguousarray(xm16),
        "imp_own": np.ascontiguousarray(imp_own), "n1_own": np.ascontiguousarray(n1_own),
        "imp_all": np.ascontiguousarray(imp_all), "noise": np.ascontiguousarray(noise_perm),
        "eu": A("expert_usage").reshape(1, 8),
        "gw1": A("gate_w1"), "gw2": A("gate_w2"), "gw3": A("gate_w3"),
        "aw": A("aff_w"), "gvec": gvec, "gsml": gsml,
        "suo": (pp < ii).astype(np.float32),
        "identf": np.eye(128, dtype=np.float32),
    }
    for f in (0, 1, 2):
        for li in range(len(FAMW[f])):
            d[f"e{f}w{li + 1}"] = A(f"e{f}_w{li + 1}")
            d[f"e{f}b{li + 1}"] = A(f"e{f}_b{li + 1}")
    return d


def _noise_perm(noise_tok):
    TOK = noise_tok.shape[0]
    TILES = TOK // 128
    return np.ascontiguousarray(
        noise_tok.reshape(TILES, 128, 8).transpose(1, 0, 2).reshape(128, TILES * 8))


def kernel(**inputs):
    import jax
    T = 4096
    x = np.asarray(inputs["L0_attn_x"], dtype=np.float32)
    BT = x.shape[0]
    with jax.default_device(jax.devices("cpu")[0]):
        noise = np.asarray(jax.random.normal(jax.random.key(1), (BT, T, 8)), np.float32)

    nc1 = build_l1(T)
    maps1 = []
    for c in range(NCORES):
        xc = x[B_CORE * c:B_CORE * (c + 1)].reshape(-1, C)
        maps1.append(l1_inputs(xc, inputs))
    r1 = run_bass_kernel_spmd(nc1, maps1, core_ids=list(range(NCORES))).results

    imp_all = np.concatenate([r1[c]["impo"] for c in range(NCORES)], axis=1)
    n_all = imp_all.size
    nc2 = build_l2(T, n_all)
    maps2 = []
    for c in range(NCORES):
        nz = noise[B_CORE * c:B_CORE * (c + 1)].reshape(-1, 8)
        maps2.append(l2_inputs(inputs, r1[c]["x1o"], r1[c]["xmo"], r1[c]["impo"],
                               r1[c]["n1o"], imp_all, _noise_perm(nz)))
    r2 = run_bass_kernel_spmd(nc2, maps2, core_ids=list(range(NCORES))).results
    y = np.concatenate([r2[c]["y"].reshape(B_CORE, T, C) for c in range(NCORES)], axis=0)
    return y

